# version 09
# baseline (speedup 1.0000x reference)
import numpy as np
import ml_dtypes
from contextlib import ExitStack, contextmanager

import concourse.bass as bass
import concourse.mybir as mybir
from concourse.bass_utils import run_bass_kernel_spmd
from concourse.alu_op_type import AluOpType as ALU

F32 = mybir.dt.float32
BF16 = mybir.dt.bfloat16
FP8 = mybir.dt.float8e4
AF = mybir.ActivationFunctionType
AX = mybir.AxisListType

EMB = 1024
FF = 4096
SEQ = 2048
NB = 2
NE = 8
NF = 32
T = 512
NT = 8
TPB = 4
NCORES = 8
CH = 1024
MEAN_W = float(1.0 / (FF * EMB))
LN_INV = float(1.0 / EMB)
SQ_SCALE = float(1.0 / 32.0)
SEQ_INV = float(1.0 / SEQ)
NPOOL = 3


class Fake:
    def __getattr__(self, n):
        return self

    def __call__(self, *a, **k):
        return self

    def __getitem__(self, i):
        return self

    def __enter__(self):
        return self

    def __exit__(self, *a):
        return False


class K:

    def __init__(self, emit, nc=None, M=None):
        self.emit = emit
        self.nc = nc
        self.cnt = {}
        self.M = M if M is not None else {}
        self.sems = {}
        self.es = ExitStack()

    def sem(self, name):
        self.cnt[name] = 0
        if self.emit:
            self.sems[name] = self.es.enter_context(self.nc.semaphore(name))

    def inc(self, instr, s, key=None, n=1):
        self.cnt[s] += n
        if key is not None:
            if not self.emit:
                self.M[(s,) + tuple(key)] = self.cnt[s]
        if self.emit and instr is not None:
            instr.then_inc(self.sems[s], n)

    def wait(self, eng, s, key):
        if self.emit:
            eng.wait_ge(self.sems[s], self.M[(s,) + tuple(key)])

    def dma(self, eng, semname, key, out, in_, chain=True):
        cur = self.cnt[semname]
        if self.emit and chain and cur:
            eng.wait_ge(self.sems[semname], cur)
        instr = eng.dma_start(out=out, in_=in_) if self.emit else None
        self.inc(instr, semname, key, 16)

    def sb(self, name, shape, dtype=F32):
        if self.emit:
            return self.es.enter_context(self.nc.sbuf_tensor(name, shape, dtype))
        return Fake()

    def ps(self, name, shape, dtype=F32):
        if self.emit:
            return self.es.enter_context(self.nc.psum_tensor(name, shape, dtype))
        return Fake()

    @contextmanager
    def block(self, name):
        if self.emit:
            with self.nc.Block(name) as b:
                yield b
        else:
            yield Fake()

    def body(self, blk, engname, fn):
        if self.emit:
            getattr(blk, engname)(fn)
        else:
            fn(Fake())


def xn_sem(t, e):
    return "v" if (t < 2 or e % 2 == 0) else "p"


def sg_sem(e2):
    return "v" if e2 % 2 == 0 else "p"


def construct(k: "K", io):
    CONSTS = ["identd", "onesrd", "onescbd", "onescfd", "cpoold", "b1d",
              "b2d", "nwd", "mixd", "cbd", "btriud", "bvi2d"]
    for s in ("t", "v", "s", "p", "cc"):
        k.sem(s)
    dma_sems = ["dcs", "dq0", "dq1", "dq2", "dxl0", "dxl1", "dxr0",
                "dxr1", "do0", "do1", "dst0", "dst1", "dsw", "dow0", "dor",
                "dcc", "drow"]
    for s in dma_sems:
        k.sem(s)

    w1t8 = k.sb("w1t8", [128, NE * FF], FP8)
    w2t8 = k.sb("w2t8", [128, NF * EMB], FP8)
    hT = k.sb("hT", [128, NF * T], BF16)
    xnt = [k.sb(f"xnt{i}", [128, NE * T], BF16) for i in range(2)]
    a_all = k.sb("a_all", [128, NE * T], BF16)
    gtb = k.sb("gtb", [128, NE * T], BF16)
    sqb = [k.sb(f"sqb{i}", [128, T], BF16) for i in range(2)]
    outb = [k.sb(f"outb{i}", [128, T]) for i in range(2)]
    xlnb = [k.sb(f"xlnb{i}", [128, T]) for i in range(2)]
    xresb = [k.sb(f"xresb{i}", [128, T]) for i in range(2)]
    mrB = [k.sb(f"mrB{i}", [128, T]) for i in range(2)]
    rsB = [k.sb(f"rsB{i}", [128, T]) for i in range(2)]
    rstdB = k.sb("rstdB", [128, T])
    mrbuf = [k.sb("mrbuf0", [1, T])]
    rsbuf = [k.sb("rsbuf0", [1, T])]
    rms_row = k.sb("rms_row", [1, T])
    rstd_row = k.sb("rstd_row", [1, T])
    pool = [k.sb(f"pool{i}", [128, CH]) for i in range(NPOOL)]
    mposb = [k.sb(f"mposb{i}", [128, CH], BF16) for i in range(2)]
    mnegb = [k.sb("mnegb0", [128, CH], BF16)]
    mcols = k.sb("mcols", [128, 16])
    rscols = k.sb("rscols", [128, 16])
    mrcols = k.sb("mrcols", [128, 16])
    sqcols = k.sb("sqcols", [128, 16])
    varcols = k.sb("varcols", [128, 16])
    mrT = k.sb("mrT", [16, 128])
    rsT = k.sb("rsT", [16, 128])
    hmv = k.sb("hmv", [128, 16])
    thrpart = k.sb("thrpart", [128, 8])
    thrcomb = k.sb("thrcomb", [128, 2])
    thrsums = k.sb("thrsums", [1, 2])
    thrsumg = k.sb("thrsumg", [1, 2])
    thrrow = k.sb("thrrow", [1, 2])
    thrcol = k.sb("thrcol", [128, 2])
    nthrcol = k.sb("nthrcol", [128, 2])
    sident = k.sb("sident", [128, 128])
    sonesr = k.sb("sonesr", [1, 128])
    sonescb = k.sb("sonescb", [128, 1], BF16)
    sonescf = k.sb("sonescf", [128, 1])
    scpool = k.sb("scpool", [128, 4])
    sb1 = k.sb("sb1", [128, 32])
    sb2 = k.sb("sb2", [128, 8])
    snw = k.sb("snw", [128, 8])
    smix = k.sb("smix", [128, 2])
    scb = k.sb("scb", [128, 128])
    sbtriu = k.sb("sbtriu", [64, 64])
    sbvi2 = k.sb("sbvi2", [2, 64])
    s64row = k.sb("s64row", [128, EMB])
    woos = k.sb("woos", [64, EMB], BF16)
    omu = k.sb("omu", [64, NB * 64])
    omg = k.sb("omg", [64, NB * 64])
    m2s = k.sb("m2s", [64, NB * 64])
    m3s = k.sb("m3s", [64, NB * 64])
    gtmp = k.sb("gtmp", [64, 64])
    GbT = k.sb("GbT", [64, NB * 64], BF16)
    wlie = k.sb("wlie", [64, NB * EMB], BF16)
    vqi = k.sb("vqi", [2, 64])
    vqq = k.sb("vqq", [2, 64])
    vnorm = k.sb("vnorm", [2, 1])
    nvq = k.sb("nvq", [2, 64])
    nvqT = k.sb("nvqT", [64, 2])
    csq = k.sb("csq", [128, 128])
    cnorm = k.sb("cnorm", [128, 2])
    sncb = k.sb("sncb", [128, 128])
    ncbT = k.sb("ncbT", [64, 256])
    scoresR = k.sb("scoresR", [2, 256])
    mxr = k.sb("mxr", [2, 1])
    mxT = k.sb("mxT", [1, 2])
    scT = k.sb("scT", [128, 4])
    soh = k.sb("soh", [128, 4])
    vqcT = k.sb("vqcT", [64, 2])
    bvoh = k.sb("bvoh", [2, T])
    booh = k.sb("booh", [2, T])
    constPP = k.sb("constPP", [128, 16])

    p_mm = [k.ps(f"p_mm{i}", [128, T]) for i in range(2)]
    p_lie = [k.ps(f"p_lie{i}", [128, T]) for i in range(2)]
    p_ms = k.ps("p_ms", [128, T])
    p_rstd = k.ps("p_rstd", [128, T])
    p_bc = [k.ps(f"p_bc{i}", [128, T]) for i in range(2)]

    def src_of(stream, idx):
        if stream == "w1s":
            return io["w1pred"][:, idx * CH:(idx + 1) * CH]
        if stream == "w2s":
            return io["w2pred"][idx * 128:(idx + 1) * 128, :]
        if stream == "xt":
            b, i = idx
            return io["xTokd"][b * SEQ + i * 128: b * SEQ + (i + 1) * 128, :]
        if stream == "w1b":
            h, e = idx
            return io["w1td"][e * 128:(e + 1) * 128, h * CH:(h + 1) * CH]
        if stream == "w2b":
            return io["w2td"][idx * 128:(idx + 1) * 128, :]
        if stream == "wtr":
            h, e = idx
            return io["wtriud"][e * 128:(e + 1) * 128, h * CH:(h + 1) * CH]
        raise KeyError(stream)

    def release_of(stream, idx):
        if stream in ("w1s", "w2s"):
            w = 0 if stream == "w1s" else 1
            return ("v", ("wred", w, idx))
        if stream == "xt":
            return ("t", ("hm",) + idx)
        if stream == "w1b":
            h, e = idx
            return ("p", ("w1t8", h, e))
        if stream == "w2b":
            return ("p", ("w2t8", idx))
        if stream == "wtr":
            return ("t", ("om",) + idx)
        raise KeyError(stream)

    uses = []
    for j in range(4):
        uses.append(("w1s", j))
    for j in range(4):
        uses.append(("w2s", j))
    for i in range(16):
        uses.append(("xt", (0, i)))
    for h in range(4):
        for e in range(NE):
            uses.append(("w1b", (h, e)))
    for F in range(NF):
        uses.append(("w2b", F))
    for i in range(16):
        uses.append(("xt", (1, i)))
    for h in range(4):
        for e in range(NE):
            uses.append(("wtr", (h, e)))
    POOLBUF = {u: n % NPOOL for n, u in enumerate(uses)}
    USE_POS = {u: n for n, u in enumerate(uses)}

    def x_src(t, e):
        b, i = t // TPB, t % TPB
        return io["xTd"][b * EMB + e * 128: b * EMB + (e + 1) * 128,
                         i * T:(i + 1) * T]

    def out_dst(t, e):
        b, i = t // TPB, t % TPB
        return io["outTd"][b * EMB + e * 128: b * EMB + (e + 1) * 128,
                           i * T:(i + 1) * T]

    def stats_src(t, row):
        b, i = t // TPB, t % TPB
        return io["stats_d"][b * 2 + row: b * 2 + row + 1, i * T:(i + 1) * T]

    def xln_prev(t, e):
        if e >= 2:
            return (t, e - 2)
        if t > 0:
            return (t - 1, e + 6)
        return None

    with k.block("main") as blk:
        def sync_body(sync):
            cdst = {"identd": sident, "onesrd": sonesr, "onescbd": sonescb,
                    "cpoold": scpool, "b1d": sb1, "b2d": sb2, "nwd": snw,
                    "mixd": smix, "cbd": scb, "btriud": sbtriu,
                    "bvi2d": sbvi2, "onescfd": sonescf}
            for n in CONSTS:
                k.dma(sync, "dcs", ("c", n), out=cdst[n][:], in_=io[n],
                      chain=False)
            k.inc(None, "dcs", ("call",), 0)
            k.dma(sync, "drow", ("woo",), out=s64row[0:64, :], in_=io["wooTd"])

            def fifo(u):
                stream, idx = u
                pos = USE_POS[u]
                if pos >= NPOOL:
                    ps_, pk = release_of(*uses[pos - NPOOL])
                    k.wait(sync, ps_, pk)
                k.dma(sync, f"dq{pos % NPOOL}", u, out=pool[POOLBUF[u]][:],
                      in_=src_of(stream, idx))

            for j in range(4):
                fifo(("w1s", j))
            for j in range(4):
                fifo(("w2s", j))
            k.wait(sync, "v", ("thrsums",))
            k.dma(sync, "dcc", ("ccw",), out=io["ccin"], in_=thrsums[:])
            for i in range(16):
                fifo(("xt", (0, i)))
            k.wait(sync, "cc", ("ccdone",))
            k.dma(sync, "dcc", ("ccr",), out=thrsumg[:], in_=io["ccout"])
            k.wait(sync, "s", ("stev", 0))
            k.dma(sync, "dsw", ("swb", 0, 0),
                  out=io["stats_d"][0:1, :].rearrange(
                      "one (c p) -> (one c) p", c=16),
                  in_=mrT[:])
            k.dma(sync, "dsw", ("swb", 0, 1),
                  out=io["stats_d"][1:2, :].rearrange(
                      "one (c p) -> (one c) p", c=16),
                  in_=rsT[:])
            k.wait(sync, "dsw", ("swb", 0, 1))
            k.dma(sync, "dst0", ("stm", 0), out=mrbuf[0][:], in_=stats_src(0, 0))
            k.dma(sync, "dst1", ("str", 0), out=rsbuf[0][:], in_=stats_src(0, 1))
            t0slots = [xlnb[0], xlnb[1], xresb[0], xresb[1], mrB[1], rsB[1],
                       rstdB, outb[1]]
            for e in range(NE):
                k.dma(sync, f"dxl{e % 2}", ("xln", 0, e),
                      out=t0slots[e][:], in_=x_src(0, e))
            for h in range(4):
                for e in range(NE):
                    fifo(("w1b", (h, e)))
            k.wait(sync, "t", ("bc", 0))
            for t_ in (1,):
                k.dma(sync, "dst0", ("stm", t_), out=mrbuf[0][:],
                      in_=stats_src(t_, 0))
                k.dma(sync, "dst1", ("str", t_), out=rsbuf[0][:],
                      in_=stats_src(t_, 1))
            k.wait(sync, "s", ("woos",))
            k.dma(sync, "drow", ("wvi",), out=s64row[:, 0:512], in_=io["wviTd"])
            for F in range(NF):
                fifo(("w2b", F))
                if F == 4:
                    for e in range(2):
                        pt, pe = xln_prev(1, e)
                        k.wait(sync, xn_sem(pt, pe), ("xn", pt, pe))
                        k.dma(sync, f"dxl{e % 2}", ("xln", 1, e),
                              out=xlnb[e % 2][:], in_=x_src(1, e))
                if F == 8:
                    k.wait(sync, "t", ("bc", 1))
                    for t_ in (2,):
                        k.dma(sync, "dst0", ("stm", t_), out=mrbuf[0][:],
                              in_=stats_src(t_, 0))
                        k.dma(sync, "dst1", ("str", t_), out=rsbuf[0][:],
                              in_=stats_src(t_, 1))
                if F == 12:
                    for e2 in (0, 1):
                        k.wait(sync, "v", ("xn", 0, e2 + 2))
                        k.dma(sync, f"dxr{e2 % 2}", ("xres", 0, e2),
                              out=xresb[e2 % 2][:], in_=x_src(0, e2))
            for i in range(16):
                fifo(("xt", (1, i)))
            for e in range(2, NE):
                pt, pe = xln_prev(1, e)
                k.wait(sync, xn_sem(pt, pe), ("xn", pt, pe))
                k.dma(sync, f"dxl{e % 2}", ("xln", 1, e),
                      out=xlnb[e % 2][:], in_=x_src(1, e))
            for e in range(2):
                pt, pe = xln_prev(2, e)
                k.wait(sync, xn_sem(pt, pe), ("xn", pt, pe))
                k.dma(sync, f"dxl{e % 2}", ("xln", 2, e),
                      out=xlnb[e % 2][:], in_=x_src(2, e))
            k.wait(sync, "s", ("stev", 1))
            k.dma(sync, "dsw", ("swb", 1, 0),
                  out=io["stats_d"][2:3, :].rearrange("one (c p) -> (one c) p", c=16),
                  in_=mrT[:])
            k.dma(sync, "dsw", ("swb", 1, 1),
                  out=io["stats_d"][3:4, :].rearrange("one (c p) -> (one c) p", c=16),
                  in_=rsT[:])
            k.dma(sync, "drow", ("bvo0a",), out=bvoh[:], in_=io["bvo2d"][:, 0:T])
            k.dma(sync, "drow", ("bvo", 0), out=booh[:], in_=io["boo2d"][:, 0:T])
            for h in range(4):
                for e in range(NE):
                    fifo(("wtr", (h, e)))
                for q in range(2):
                    k.wait(sync, "v", ("omev", h, q))
                    k.dma(sync, "dow0", ("omw", h, q),
                          out=io["omdram"][0:2, h * CH + q * T: h * CH + (q + 1) * T],
                          in_=outb[0][0:2, :])
            k.wait(sync, "t", ("spvq",))
            k.dma(sync, "drow", ("wvo",), out=s64row[0:64, :], in_=io["wvoTd"])
            k.wait(sync, "dow0", ("omw", 3, 1))
            for b in range(NB):
                rdsrc = io["omdram"][b:b + 1, :].rearrange(
                    "one (r c) -> (one r) c", r=64)
                k.dma(sync, "dor", ("omr", b), out=omu[:, b * 64:(b + 1) * 64],
                      in_=rdsrc)
            k.wait(sync, "v", ("spc2", 0))
            k.dma(sync, "drow", ("bvo1a",), out=bvoh[:], in_=io["bvo2d"][:, T:EMB])
            k.dma(sync, "drow", ("bvo", 1), out=booh[:], in_=io["boo2d"][:, T:EMB])
            for t in range(NT):
                if 2 <= t + 1 < NT:
                    erange = range(2, NE) if t + 1 == 2 else range(NE)
                    for e in erange:
                        pt, pe = xln_prev(t + 1, e)
                        k.wait(sync, xn_sem(pt, pe), ("xn", pt, pe))
                        k.dma(sync, f"dxl{e % 2}", ("xln", t + 1, e),
                              out=xlnb[e % 2][:], in_=x_src(t + 1, e))
                if t + 3 < NT:
                    k.wait(sync, "t", ("bc", t + 2))
                    if t + 3 == 4:
                        k.wait(sync, "dsw", ("swb", 1, 1))
                    k.dma(sync, "dst0", ("stm", t + 3), out=mrbuf[0][:],
                          in_=stats_src(t + 3, 0))
                    k.dma(sync, "dst1", ("str", t + 3), out=rsbuf[0][:],
                          in_=stats_src(t + 3, 1))
                for e2 in range(NE):
                    if not (t == 0 and e2 < 2):
                        pe = e2 - 2
                        pt = t
                        if pe < 0:
                            pe += 8
                            pt -= 1
                        k.wait(sync, sg_sem(pe), ("sg", pt, pe))
                        k.dma(sync, f"dxr{e2 % 2}", ("xres", t, e2),
                              out=xresb[e2 % 2][:], in_=x_src(t, e2))
                    if e2 >= 2:
                        k.wait(sync, sg_sem(e2 - 2), ("sg", t, e2 - 2))
                        k.dma(sync, f"do{e2 % 2}", ("od", t, e2 - 2),
                              out=out_dst(t, e2 - 2), in_=outb[e2 % 2][:])
                for e2 in (6, 7):
                    k.wait(sync, sg_sem(e2), ("sg", t, e2))
                    k.dma(sync, f"do{e2 % 2}", ("od", t, e2),
                          out=out_dst(t, e2), in_=outb[e2 % 2][:])
            k.wait(sync, "do0", ("od", NT - 1, NE - 2))
            k.wait(sync, "do1", ("od", NT - 1, NE - 1))
        k.body(blk, "sync", sync_body)

        def gp_body(gp):
            k.wait(gp, "dcs", ("call",))
            k.wait(gp, "dcc", ("ccw",))
            cci = gp.collective_compute(
                "AllReduce", ALU.add, replica_groups=[list(range(NCORES))],
                ins=[io["ccin"]], outs=[io["ccout"]]) if k.emit else None
            k.inc(cci, "cc", ("ccdone",))

            def wtern(w, clist):
                for c in clist:
                    if w == 0:
                        h, e = c // NE, c % NE
                        stream, idx = "w1b", (h, e)
                        dst = w1t8[:, e * FF + h * CH: e * FF + (h + 1) * CH]
                        key = ("w1t8", h, e)
                    else:
                        stream, idx = "w2b", c
                        dst = w2t8[:, c * EMB:(c + 1) * EMB]
                        key = ("w2t8", c)
                    u = (stream, idx)
                    k.wait(gp, f"dq{USE_POS[u] % NPOOL}", u)
                    k.wait(gp, "v", ("thrc",))
                    gp.tensor_scalar(mnegb[0][:], pool[POOLBUF[u]][:],
                                     scalar1=nthrcol[:, w:w + 1], scalar2=None,
                                     op0=ALU.is_le)
                    k.wait(gp, "v", ("mpos", w, c))
                    k.inc(gp.tensor_tensor(dst, mposb[c % 2][:], mnegb[0][:],
                                           op=ALU.subtract), "p", key)

            wtern(0, list(range(32)))
            wtern(1, list(range(32)))
            for t in range(NT):
                for e2 in range(1, NE, 2):
                    k.wait(gp, "s", ("silu", t, e2))
                    k.wait(gp, "s", ("rstdev", t))
                    k.wait(gp, "v", ("a", t, e2))
                    k.wait(gp, f"dxr{e2 % 2}", ("xres", t, e2))
                    if (t, e2) > (0, 1):
                        pe = e2 - 2
                        pt = t
                        if pe < 0:
                            pe += 8
                            pt -= 1
                        k.wait(gp, f"do{pe % 2}", ("od", pt, pe))
                    gp.tensor_scalar(outb[e2 % 2][:], gtb[:, e2 * T:(e2 + 1) * T],
                                     scalar1=snw[:, e2:e2 + 1], scalar2=None,
                                     op0=ALU.mult)
                    gp.tensor_tensor(outb[e2 % 2][:], outb[e2 % 2][:],
                                     rstdB[:], op=ALU.mult)
                    gp.tensor_tensor(outb[e2 % 2][:], outb[e2 % 2][:],
                                     a_all[:, e2 * T:(e2 + 1) * T], op=ALU.mult)
                    k.inc(gp.tensor_tensor(outb[e2 % 2][:], outb[e2 % 2][:],
                                           xresb[e2 % 2][:], op=ALU.add),
                          "p", ("sg", t, e2))
                t2 = t + 2
                if t2 < NT:
                    for e in range(1, NE, 2):
                        k.wait(gp, f"dxl{e % 2}", ("xln", t2, e))
                        k.wait(gp, "s", ("bcev", t2))
                        k.wait(gp, "v", ("a", t, e))
                        if e == 1:
                            k.wait(gp, "do1", ("od", t, 7))
                        gp.tensor_tensor(outb[1][:], xlnb[e % 2][:],
                                         mrB[t2 % 2][:], op=ALU.subtract)
                        k.inc(gp.tensor_tensor(xnt[t2 % 2][:, e * T:(e + 1) * T],
                                               outb[1][:], rsB[t2 % 2][:],
                                               op=ALU.mult), "p", ("xn", t2, e))
        k.body(blk, "gpsimd", gp_body)

        def vector_body(vector):
            k.wait(vector, "dcs", ("call",))
            for w in range(2):
                for j in range(4):
                    stream = "w1s" if w == 0 else "w2s"
                    k.wait(vector, f"dq{USE_POS[(stream, j)] % NPOOL}", (stream, j))
                    k.inc(vector.reduce_sum(thrpart[:, w * 4 + j:w * 4 + j + 1],
                                            pool[POOLBUF[(stream, j)]][:],
                                            axis=AX.X, apply_absolute_value=True),
                          "v", ("wred", w, j))
            for w in range(2):
                k.inc(vector.reduce_sum(thrcomb[:, w:w + 1],
                                        thrpart[:, w * 4:w * 4 + 4], axis=AX.X),
                      "v", ("thrc0", w))
            k.wait(vector, "t", ("thrtr",))
            vector.tensor_copy(rms_row[0:1, 0:128], p_bc[0][0:1, 0:128])
            k.inc(vector.reduce_sum(thrsums[0:1, 0:1], rms_row[0:1, 0:128],
                                    axis=AX.X), "v", ("ts0",))
            k.wait(vector, "t", ("thrtr2",))
            vector.tensor_copy(rms_row[0:1, 0:128], p_bc[1][0:1, 0:128])
            k.inc(vector.reduce_sum(thrsums[0:1, 1:2], rms_row[0:1, 0:128],
                                    axis=AX.X), "v", ("thrsums",))

            def stat_chain(b, i):
                k.wait(vector, "s", ("acc", b, i))
                k.inc(vector.tensor_tensor(varcols[:, i:i + 1], sqcols[:, i:i + 1],
                                           varcols[:, i:i + 1], op=ALU.subtract),
                      "v", ("var", b, i))
                k.wait(vector, "s", ("sqv", b, i))
                k.inc(vector.reciprocal(rscols[:, i:i + 1], varcols[:, i:i + 1]),
                      "v", ("rscr", b, i))
                k.wait(vector, "v", ("rscr", b, i))
                k.inc(vector.tensor_tensor(mrcols[:, i:i + 1], mcols[:, i:i + 1],
                                           rscols[:, i:i + 1], op=ALU.mult),
                      "v", ("rsc", b, i))

            for i in range(16):
                stat_chain(0, i)
            vector.reduce_sum(thrpart[:, 0:1], mrcols[:], axis=AX.X)
            k.inc(vector.tensor_copy(thrcomb[:, 0:1], thrpart[:, 0:1]),
                  "v", ("mrsum", 0))
            k.wait(vector, "t", ("smr", 0))
            k.inc(vector.tensor_copy(mcols[0:1, 0:1], p_ms[0:1, 20:21]),
                  "v", ("smre", 0))
            k.wait(vector, "t", ("hm", 0, 15))
            vector.tensor_scalar(rms_row[:], p_mm[0][0:1, :],
                                 scalar1=mcols[0:1, 0:1], scalar2=None,
                                 op0=ALU.subtract)
            k.inc(vector.tensor_scalar(rstd_row[:], p_mm[1][0:1, :],
                                       scalar1=mcols[0:1, 0:1], scalar2=None,
                                       op0=ALU.subtract), "v", ("hmev", 0))
            k.wait(vector, "t", ("hmv", 0))
            k.inc(vector.tensor_scalar(
                hmv[:, 0:16].rearrange("p (e b) -> p b e", b=2)[:, 0, :],
                p_ms[:, 0:16].rearrange("p (e b) -> p b e", b=2)[:, 0, :],
                scalar1=SEQ_INV, scalar2=None, op0=ALU.mult), "v", ("hmvv", 0))
            k.wait(vector, "dcc", ("ccr",))
            k.inc(vector.tensor_scalar(thrrow[:], thrsumg[:], scalar1=MEAN_W,
                                       scalar2=1e-8, op0=ALU.mult, op1=ALU.max),
                  "v", ("thrrow",))
            k.wait(vector, "t", ("thrbc",))
            vector.tensor_scalar(thrcol[:], p_bc[0][:, 0:2], scalar1=0.5,
                                 scalar2=None, op0=ALU.mult)
            k.inc(vector.tensor_scalar(nthrcol[:], p_bc[0][:, 0:2], scalar1=-0.5,
                                       scalar2=None, op0=ALU.mult),
                  "v", ("thrc",))

            t0slots_v = [xlnb[0], xlnb[1], xresb[0], xresb[1], mrB[1], rsB[1],
                         rstdB, outb[1]]

            def xn_tile(t):
                for e in range(NE):
                    if t == 0:
                        k.wait(vector, f"dxl{e % 2}", ("xln", t, e))
                        src = t0slots_v[e]
                    else:
                        k.wait(vector, f"dxl{e % 2}", ("xln", t, e))
                        src = xlnb[e % 2]
                    k.wait(vector, "s", ("bcev", t))
                    vector.tensor_tensor(outb[0][:], src[:], mrB[t % 2][:],
                                         op=ALU.subtract)
                    k.inc(vector.tensor_tensor(xnt[t % 2][:, e * T:(e + 1) * T],
                                               outb[0][:], rsB[t % 2][:],
                                               op=ALU.mult), "v", ("xn", t, e))

            def w_mpos(w, clist):
                for c in clist:
                    if w == 0:
                        stream, idx = "w1b", (c // NE, c % NE)
                    else:
                        stream, idx = "w2b", c
                    u = (stream, idx)
                    k.wait(vector, f"dq{USE_POS[u] % NPOOL}", u)
                    if c == 0:
                        k.wait(vector, "v", ("thrc",))
                        if w == 0:
                            k.wait(vector, "s", ("acc", 0, 15))
                    if c >= 2:
                        ps_, pk = release_of(*(("w1b", ((c - 2) // NE, (c - 2) % NE))
                                               if w == 0 else ("w2b", c - 2)))
                        k.wait(vector, ps_, pk)
                    elif w == 1:
                        k.wait(vector, "p", ("w1t8", 3, 6 + c))
                    k.inc(vector.tensor_scalar(mposb[c % 2][:],
                                               pool[POOLBUF[u]][:],
                                               scalar1=thrcol[:, w:w + 1],
                                               scalar2=None, op0=ALU.is_ge),
                          "v", ("mpos", w, c))

            w_mpos(0, list(range(8)))
            xn_tile(0)
            w_mpos(0, list(range(8, 32)))
            w_mpos(1, list(range(32)))
            k.wait(vector, "t", ("sttr", 0))
            k.wait(vector, "v", ("hmev", 0))
            for i in range(16):
                stat_chain(1, i)
            xn_tile(1)
            vector.reduce_sum(thrpart[:, 0:1], mrcols[:], axis=AX.X)
            k.inc(vector.tensor_copy(thrcomb[:, 0:1], thrpart[:, 0:1]),
                  "v", ("mrsum", 1))
            k.wait(vector, "t", ("smr", 1))
            k.inc(vector.tensor_copy(mcols[0:1, 0:1], p_ms[0:1, 20:21]),
                  "v", ("smre", 1))
            k.wait(vector, "t", ("hm", 1, 15))
            k.wait(vector, "t", ("hmv", 0))
            vector.tensor_scalar(rms_row[:], p_lie[0][0:1, :],
                                 scalar1=mcols[0:1, 0:1], scalar2=None,
                                 op0=ALU.subtract)
            k.inc(vector.tensor_scalar(rstd_row[:], p_lie[1][0:1, :],
                                       scalar1=mcols[0:1, 0:1], scalar2=None,
                                       op0=ALU.subtract), "v", ("hmev", 1))
            k.wait(vector, "t", ("hmv", 1))
            k.inc(vector.tensor_scalar(
                hmv[:, 0:16].rearrange("p (e b) -> p b e", b=2)[:, 1, :],
                p_ms[:, 0:16].rearrange("p (e b) -> p b e", b=2)[:, 1, :],
                scalar1=SEQ_INV, scalar2=None, op0=ALU.mult), "v", ("hmvv", 1))

            k.wait(vector, "t", ("spvq",))
            k.inc(vector.tensor_tensor(vqi[:], p_ms[0:2, 0:64], sbvi2[:],
                                       op=ALU.add), "v", ("spvqi",))
            k.wait(vector, "s", ("spvqsq",))
            k.inc(vector.reduce_sum(vnorm[:], vqq[:], axis=AX.X), "v", ("spvnr",))
            k.wait(vector, "s", ("spvqn",))
            vector.tensor_scalar(vnorm[:], vnorm[:], scalar1=1e-12, scalar2=None,
                                 op0=ALU.max)
            k.inc(vector.reciprocal(vnorm[:], vnorm[:]), "v", ("spvrec",))
            k.wait(vector, "v", ("spvrec",))
            k.inc(vector.tensor_scalar(nvq[:], vqi[:], scalar1=vnorm[:],
                                       scalar2=None, op0=ALU.mult),
                  "v", ("spnvq",))
            k.wait(vector, "s", ("spcbsq",))
            for c in range(2):
                rs_cn = vector.reduce_sum(cnorm[:, c:c + 1],
                                          csq[:, c * 64:(c + 1) * 64], axis=AX.X)
            k.inc(rs_cn, "v", ("spcn",))
            k.wait(vector, "s", ("spcbn",))
            for c in range(2):
                vector.tensor_scalar(cnorm[:, c:c + 1], cnorm[:, c:c + 1],
                                     scalar1=1e-12, scalar2=None, op0=ALU.max)
            k.inc(vector.reciprocal(cnorm[:], cnorm[:]), "v", ("spcrec",))
            k.wait(vector, "v", ("spcrec",))
            for c in range(2):
                k.inc(vector.tensor_scalar(sncb[:, c * 64:(c + 1) * 64],
                                           scb[:, c * 64:(c + 1) * 64],
                                           scalar1=cnorm[:, c:c + 1],
                                           scalar2=None, op0=ALU.mult),
                      "v", ("spncb", c))
            for c in range(2):
                k.wait(vector, "t", ("spncbt", c))
                src = p_rstd if c == 0 else p_ms
                k.inc(vector.tensor_copy(ncbT[:, c * 128:(c + 1) * 128],
                                         src[0:64, 0:128]), "v", ("spncbtev", c))
            k.wait(vector, "t", ("spnvqt",))
            k.inc(vector.tensor_copy(nvqT[:], p_lie[0][0:64, 0:2]),
                  "v", ("spnvqte",))
            k.wait(vector, "t", ("spsc",))
            vector.tensor_copy(scoresR[:], p_rstd[0:2, 0:256])
            k.inc(vector.reduce_max(mxr[:], scoresR[:], axis=AX.X), "v", ("spmx",))
            k.wait(vector, "t", ("spmxt",))
            k.inc(vector.tensor_copy(mxT[:], p_ms[0:1, 0:2]), "v", ("spmxte",))
            k.wait(vector, "t", ("spsct", 1))
            for c in range(2):
                vector.tensor_copy(scT[:, c * 2:c * 2 + 2],
                                   p_lie[0][:, c * 2:c * 2 + 2])
            k.wait(vector, "t", ("spmb",))
            for c in range(2):
                k.inc(vector.tensor_tensor(soh[:, c * 2:c * 2 + 2],
                                           scT[:, c * 2:c * 2 + 2],
                                           p_rstd[:, 0:2], op=ALU.is_ge),
                      "v", ("spoh", c))
            k.wait(vector, "t", ("spvc",))
            k.inc(vector.tensor_copy(vqcT[:], p_ms[0:64, 0:2]), "v", ("spvce",))
            for h in range(4):
                k.wait(vector, "t", ("om", h, NE - 1))
                for q in range(2):
                    if (h, q) != (0, 0):
                        pq = (h, q - 1) if q == 1 else (h - 1, 1)
                        k.wait(vector, "dow0", ("omw",) + pq)
                    k.inc(vector.tensor_copy(outb[0][0:2, :], p_lie[q][0:2, :]),
                          "v", ("omev", h, q))
            for s_ in range(2):
                k.wait(vector, "drow", ("bvo", s_))
                k.wait(vector, "t", ("spvcr", s_))
                vector.tensor_tensor(bvoh[:], bvoh[:], p_lie[s_][0:2, :],
                                     op=ALU.add)
                vector.tensor_scalar(bvoh[:], bvoh[:], scalar1=smix[0:2, 1:2],
                                     scalar2=None, op0=ALU.mult)
                vector.tensor_scalar(booh[:], booh[:], scalar1=smix[0:2, 0:1],
                                     scalar2=None, op0=ALU.mult)
                k.inc(vector.tensor_tensor(booh[:], booh[:], bvoh[:],
                                           op=ALU.add), "v", ("spc2", s_))
                for e4 in range(4):
                    e = s_ * 4 + e4
                    k.wait(vector, "t", ("spct", e))
                    src = p_ms if e % 2 == 0 else p_rstd
                    k.inc(vector.tensor_copy(constPP[:, e * 2:e * 2 + 2],
                                             src[:, 0:2]), "v", ("spcpp", e))
            for b in range(NB):
                k.wait(vector, "dor", ("omr", b))
                k.inc(vector.tensor_tensor(omu[:, b * 64:(b + 1) * 64],
                                           omu[:, b * 64:(b + 1) * 64],
                                           sbtriu[:], op=ALU.add),
                      "v", ("spomu", b))
                k.wait(vector, "t", ("sptr", b))
                k.inc(vector.tensor_tensor(omg[:, b * 64:(b + 1) * 64],
                                           omu[:, b * 64:(b + 1) * 64],
                                           p_bc[0][0:64, 0:64], op=ALU.subtract),
                      "v", ("spomb", b))
                k.wait(vector, "t", ("spm2", b))
                k.inc(vector.tensor_copy(m2s[:, b * 64:(b + 1) * 64],
                                         p_bc[1][0:64, 0:64]), "v", ("spm2e", b))
                k.wait(vector, "v", ("spm2e", b))
                k.wait(vector, "t", ("spm3", b))
                k.inc(vector.tensor_copy(m3s[:, b * 64:(b + 1) * 64],
                                         p_bc[0][0:64, 0:64]), "v", ("spm3e", b))
                k.wait(vector, "v", ("spm3e", b))
                if b == 1:
                    k.wait(vector, "t", ("gbt", 0))
                bs = slice(b * 64, (b + 1) * 64)
                vector.tensor_scalar(gtmp[:], m2s[:, bs], scalar1=-0.5,
                                     scalar2=None, op0=ALU.mult)
                vector.tensor_tensor(gtmp[:], gtmp[:], omg[:, bs], op=ALU.add)
                vector.tensor_scalar(m3s[:, bs], m3s[:, bs], scalar1=1.0 / 6.0,
                                     scalar2=None, op0=ALU.mult)
                vector.tensor_tensor(gtmp[:], gtmp[:], m3s[:, bs], op=ALU.add)
                k.inc(vector.tensor_tensor(gtmp[:], gtmp[:],
                                           sident[0:64, 0:64], op=ALU.add),
                      "v", ("spG", b))
                k.wait(vector, "t", ("gbt", b))
                k.inc(vector.tensor_copy(GbT[:, bs], p_bc[1][0:64, 0:64]),
                      "v", ("gbte", b))
                k.wait(vector, "t", ("wliem", b))
                vector.tensor_copy(wlie[:, b * EMB: b * EMB + T],
                                   p_bc[0][0:64, :])
                k.inc(vector.tensor_copy(wlie[:, b * EMB + T: (b + 1) * EMB],
                                         p_bc[1][0:64, :]), "v", ("wlie", b))

            for t in range(NT):
                b = t // TPB
                for e in range(NE):
                    k.wait(vector, "t", ("lie", t, e))
                    if t == 0 and e == 0:
                        k.wait(vector, "v", ("spcpp", 7))
                    if t >= 1:
                        k.wait(vector, sg_sem(e), ("sg", t - 1, e))
                        k.wait(vector, "s", ("sq", t - 1, e))
                    k.inc(vector.scalar_tensor_tensor(
                        out=a_all[:, e * T:(e + 1) * T], in0=p_lie[e % 2][:],
                        scalar=constPP[:, e * 2 + b:e * 2 + b + 1],
                        in1=xnt[t % 2][:, e * T:(e + 1) * T],
                        op0=ALU.add, op1=ALU.add), "v", ("a", t, e))
                k.wait(vector, "s", ("sqrt", t))
                if t >= 1:
                    k.wait(vector, "t", ("rstdbc", t - 1))
                k.inc(vector.reciprocal(rstd_row[:], rms_row[:]),
                      "v", ("recip", t))
                for e2 in range(0, NE, 2):
                    k.wait(vector, "s", ("silu", t, e2))
                    k.wait(vector, "s", ("rstdev", t))
                    k.wait(vector, f"dxr{e2 % 2}", ("xres", t, e2))
                    if (t, e2) > (0, 1):
                        pe = e2 - 2
                        pt = t
                        if pe < 0:
                            pe += 8
                            pt -= 1
                        k.wait(vector, f"do{pe % 2}", ("od", pt, pe))
                    vector.scalar_tensor_tensor(
                        out=outb[e2 % 2][:], in0=gtb[:, e2 * T:(e2 + 1) * T],
                        scalar=snw[:, e2:e2 + 1], in1=rstdB[:],
                        op0=ALU.mult, op1=ALU.mult)
                    vector.tensor_tensor(outb[e2 % 2][:], outb[e2 % 2][:],
                                         a_all[:, e2 * T:(e2 + 1) * T],
                                         op=ALU.mult)
                    k.inc(vector.tensor_tensor(outb[e2 % 2][:], outb[e2 % 2][:],
                                               xresb[e2 % 2][:], op=ALU.add),
                          "v", ("sg", t, e2))
                if t + 2 < NT:
                    t2 = t + 2
                    for e in range(0, NE, 2):
                        k.wait(vector, f"dxl{e % 2}", ("xln", t2, e))
                        k.wait(vector, "s", ("bcev", t2))
                        if e == 0:
                            k.wait(vector, "do0", ("od", t, 6))
                        vector.tensor_tensor(outb[0][:], xlnb[e % 2][:],
                                             mrB[t2 % 2][:], op=ALU.subtract)
                        k.inc(vector.tensor_tensor(
                            xnt[t2 % 2][:, e * T:(e + 1) * T], outb[0][:],
                            rsB[t2 % 2][:], op=ALU.mult), "v", ("xn", t2, e))
        k.body(blk, "vector", vector_body)

        def scalar_body(scalar):
            k.wait(scalar, "dcs", ("call",))

            def stats_acc(b, i):
                u = ("xt", (b, i))
                k.wait(scalar, f"dq{USE_POS[u] % NPOOL}", u)
                if b == 1 and i == 0:
                    k.wait(scalar, "v", ("hmev", 0))
                scr = mposb[0][:] if b == 0 else xnt[1][:, 0:CH]
                scalar.activation(scr, pool[POOLBUF[u]][:], AF.Copy,
                                  scale=LN_INV, accum_out=mcols[:, i:i + 1])
                scalar.activation(scr, pool[POOLBUF[u]][:],
                                  AF.Square, scale=SQ_SCALE,
                                  accum_out=sqcols[:, i:i + 1])
                k.inc(scalar.square(varcols[:, i:i + 1], mcols[:, i:i + 1]),
                      "s", ("acc", b, i))

            def stats_sqv(b, i):
                k.wait(scalar, "v", ("var", b, i))
                k.inc(scalar.activation(varcols[:, i:i + 1], varcols[:, i:i + 1],
                                        AF.Sqrt, bias=scpool[:, 1:2]),
                      "s", ("sqv", b, i))

            for i in range(16):
                stats_acc(0, i)
                if i >= 1:
                    stats_sqv(0, i - 1)
            stats_sqv(0, 15)
            k.wait(scalar, "t", ("sttr", 0))
            scalar.copy(mrT[:], p_bc[0][0:16, 0:128])
            k.inc(scalar.copy(rsT[:], p_bc[1][0:16, 0:128]), "s", ("stev", 0))
            k.wait(scalar, "drow", ("woo",))
            k.inc(scalar.activation(woos[:], s64row[0:64, :], AF.Copy,
                                    scale=smix[0:64, 0:1]), "s", ("woos",))
            k.wait(scalar, "t", ("bc", 0))
            scalar.copy(mrB[0][:], p_bc[0][:])
            k.inc(scalar.copy(rsB[0][:], p_bc[1][:]), "s", ("bcev", 0))
            for f in range(NF):
                k.wait(scalar, "t", ("mm1", 0, f))
                k.inc(scalar.activation(hT[:, f * T:(f + 1) * T], p_mm[f % 2][:],
                                        AF.Gelu, bias=sb1[:, f:f + 1]),
                      "s", ("gelu", 0, f))
            k.wait(scalar, "t", ("bc", 1))
            k.wait(scalar, "v", ("xn", 0, 4))
            k.wait(scalar, "v", ("xn", 0, 5))
            scalar.copy(mrB[1][:], p_bc[0][:])
            k.inc(scalar.copy(rsB[1][:], p_bc[1][:]), "s", ("bcev", 1))
            for e2 in range(NE):
                for i in (2 * e2, 2 * e2 + 1):
                    stats_acc(1, i)
                    if i >= 1:
                        stats_sqv(1, i - 1)
                k.wait(scalar, "t", ("mm2", 0, e2))
                k.inc(scalar.activation(gtb[:, e2 * T:(e2 + 1) * T],
                                        p_mm[e2 % 2][:], AF.Silu,
                                        bias=sb2[:, e2:e2 + 1]),
                      "s", ("silu", 0, e2))
            stats_sqv(1, 15)
            k.wait(scalar, "t", ("sttr", 1))
            scalar.copy(mrT[:], p_ms[0:16, 0:128])
            k.inc(scalar.copy(rsT[:], p_rstd[0:16, 0:128]), "s", ("stev", 1))
            k.wait(scalar, "v", ("spvqi",))
            k.inc(scalar.square(vqq[:], vqi[:]), "s", ("spvqsq",))
            k.inc(scalar.square(csq[:], scb[:]), "s", ("spcbsq",))
            k.wait(scalar, "v", ("spvnr",))
            k.inc(scalar.activation(vnorm[:], vnorm[:], AF.Sqrt,
                                    bias=scpool[0:2, 2:3]), "s", ("spvqn",))
            k.wait(scalar, "v", ("spcn",))
            k.inc(scalar.activation(cnorm[:], cnorm[:], AF.Sqrt,
                                    bias=scpool[:, 2:3]), "s", ("spcbn",))

            def sq_t(t):
                for e in range(NE):
                    k.wait(scalar, "v", ("a", t, e))
                    if e >= 2:
                        k.wait(scalar, "t", ("ms", t, e - 2))
                    elif t > 0:
                        k.wait(scalar, "t", ("ms", t - 1, e + 6))
                    k.inc(scalar.square(sqb[e % 2][:],
                                        a_all[:, e * T:(e + 1) * T]),
                          "s", ("sq", t, e))

            def sqrt_t(t):
                k.wait(scalar, "t", ("ms", t, NE - 1))
                if t > 0:
                    k.wait(scalar, "v", ("recip", t - 1))
                k.inc(scalar.activation(rms_row[:], p_ms[0:1, :], AF.Sqrt,
                                        bias=scpool[0:1, 0:1], scale=LN_INV),
                      "s", ("sqrt", t))

            def rstdev_t(t):
                k.wait(scalar, "t", ("rstdbc", t))
                if t >= 1:
                    k.wait(scalar, "v", ("sg", t - 1, 6))
                    k.wait(scalar, "p", ("sg", t - 1, 7))
                k.inc(scalar.copy(rstdB[:], p_rstd[:]), "s", ("rstdev", t))

            def silu_t(t):
                for e2 in range(NE):
                    k.wait(scalar, "t", ("mm2", t, e2))
                    k.wait(scalar, sg_sem(e2), ("sg", t - 1, e2))
                    k.inc(scalar.activation(gtb[:, e2 * T:(e2 + 1) * T],
                                            p_mm[e2 % 2][:], AF.Silu,
                                            bias=sb2[:, e2:e2 + 1]),
                          "s", ("silu", t, e2))

            def gelu_t(t):
                for f in range(NF):
                    k.wait(scalar, "t", ("mm1", t, f))
                    if f == 0:
                        k.wait(scalar, "t", ("mm2", t - 1, 7))
                    k.inc(scalar.activation(hT[:, f * T:(f + 1) * T],
                                            p_mm[f % 2][:], AF.Gelu,
                                            bias=sb1[:, f:f + 1]),
                          "s", ("gelu", t, f))

            def bcev_t(t):
                k.wait(scalar, "t", ("bc", t))
                if t >= 2:
                    k.wait(scalar, xn_sem(t - 2, 6), ("xn", t - 2, 6))
                    k.wait(scalar, xn_sem(t - 2, 7), ("xn", t - 2, 7))
                scalar.copy(mrB[t % 2][:], p_bc[0][:])
                k.inc(scalar.copy(rsB[t % 2][:], p_bc[1][:]), "s", ("bcev", t))

            gelu_t(1)
            bcev_t(2)
            sq_t(0)
            sqrt_t(0)
            rstdev_t(0)
            silu_t(1)
            sq_t(1)
            sqrt_t(1)
            rstdev_t(1)
            for t in range(2, NT):
                gelu_t(t)
                if t + 1 < NT:
                    bcev_t(t + 1)
                sq_t(t)
                sqrt_t(t)
                rstdev_t(t)
                silu_t(t)
        k.body(blk, "scalar", scalar_body)

        def tensor_body(tensor):
            k.wait(tensor, "dcs", ("call",))
            k.wait(tensor, "v", ("thrc0", 0))
            k.inc(tensor.transpose(p_bc[0][0:1, 0:128], thrcomb[:, 0:1],
                                   sident[:]), "t", ("thrtr",))
            k.wait(tensor, "v", ("thrc0", 1))
            k.inc(tensor.transpose(p_bc[1][0:1, 0:128], thrcomb[:, 1:2],
                                   sident[:]), "t", ("thrtr2",))

            def hm_i(b, i):
                hb = [p_mm[0], p_mm[1]] if b == 0 else [p_lie[0], p_lie[1]]
                u = ("xt", (b, i))
                k.wait(tensor, f"dq{USE_POS[u] % NPOOL}", u)
                k.wait(tensor, "v", ("rsc", b, i))
                for q in range(2):
                    mm = tensor.matmul(hb[q][0:1, :],
                                       lhsT=rscols[:, i:i + 1],
                                       rhs=pool[POOLBUF[u]][:, q * T:(q + 1) * T],
                                       start=(i == 0), stop=(i == 15))
                k.inc(mm, "t", ("hm", b, i))

            def smr_b(b):
                k.wait(tensor, "v", ("mrsum", b))
                tensor.matmul(p_ms[0:1, 20:21], lhsT=sonescf[:],
                              rhs=thrpart[:, 0:1], start=True, stop=True)
                mm = tensor.matmul(p_ms[0:1, 22:23], lhsT=sonescf[:],
                                   rhs=thrpart[:, 0:1], start=True, stop=True)
                k.inc(mm, "t", ("smr", b))

            def stats_tr(b):
                k.wait(tensor, "v", ("rsc", b, 15))
                tr = [p_bc[0], p_bc[1]] if b == 0 else [p_ms, p_rstd]
                if b == 0:
                    k.wait(tensor, "v", ("ts0",))
                    k.wait(tensor, "v", ("thrsums",))
                else:
                    k.wait(tensor, "v", ("hmvv", 0))
                    k.wait(tensor, "v", ("smre", 1))
                k.inc(tensor.transpose(tr[0][0:16, 0:128], mrcols[:], sident[:]),
                      "t", ("sttr0", b))
                k.inc(tensor.transpose(tr[1][0:16, 0:128], rscols[:], sident[:]),
                      "t", ("sttr", b))

            def stats_b(b):
                for i in range(16):
                    hm_i(b, i)
                smr_b(b)
                stats_tr(b)

            def hmv_b(b):
                k.wait(tensor, "v", ("hmev", b))
                if b == 1:
                    k.wait(tensor, "s", ("stev", 1))
                for e in range(NE):
                    hsrc = rms_row if e < 4 else rstd_row
                    mm = tensor.matmul(p_ms[:, (2 * e + b):(2 * e + b + 1)],
                                       lhsT=hsrc[0:1, (e % 4) * 128:(e % 4 + 1) * 128],
                                       rhs=sonesr[0:1, 0:1], start=True,
                                       stop=True)
                k.inc(mm, "t", ("hmv", b))

            stats_b(0)
            hmv_b(0)
            k.wait(tensor, "v", ("thrrow",))
            k.wait(tensor, "s", ("stev", 0))
            k.inc(tensor.matmul(p_bc[0][:, 0:2], lhsT=sonesr[:], rhs=thrrow[:],
                                start=True, stop=True), "t", ("thrbc",))

            def bc_t(t):
                k.wait(tensor, "dst0", ("stm", t))
                k.wait(tensor, "dst1", ("str", t))
                if t == 0:
                    k.wait(tensor, "v", ("thrc",))
                if t == 2:
                    k.wait(tensor, "v", ("wlie", 1))
                if t > 0:
                    for e in range(NE):
                        k.wait(tensor, xn_sem(t - 1, e), ("xn", t - 1, e))
                tensor.matmul(p_bc[0][:], lhsT=sonesr[:], rhs=mrbuf[0][:],
                              start=True, stop=True)
                mm = tensor.matmul(p_bc[1][:], lhsT=sonesr[:], rhs=rsbuf[0][:],
                                   start=True, stop=True)
                k.inc(mm, "t", ("bc", t))

            def vq_path():
                k.wait(tensor, "v", ("hmvv", 1))
                k.wait(tensor, "drow", ("wvi",))
                for e in range(NE):
                    mm = tensor.matmul(p_ms[0:2, 0:64],
                                       lhsT=hmv[:, 2 * e:2 * e + 2],
                                       rhs=s64row[:, e * 64:(e + 1) * 64],
                                       start=(e == 0), stop=(e == 7))
                k.inc(mm, "t", ("spvq",))
                for c in range(2):
                    k.wait(tensor, "v", ("spncb", c))
                    dst = p_rstd if c == 0 else p_ms
                    if c == 1:
                        k.wait(tensor, "v", ("spncbtev", 0))
                    k.inc(tensor.transpose(dst[0:64, 0:128],
                                           sncb[:, c * 64:(c + 1) * 64],
                                           sident[:]), "t", ("spncbt", c))
                k.wait(tensor, "v", ("spnvq",))
                k.wait(tensor, "v", ("hmev", 1))
                k.inc(tensor.transpose(p_lie[0][0:64, 0:2], nvq[:],
                                       sident[0:2, 0:2]), "t", ("spnvqt",))
                k.wait(tensor, "v", ("spncbtev", 1))
                k.wait(tensor, "v", ("spnvqte",))
                k.inc(tensor.matmul(p_rstd[0:2, 0:256], lhsT=nvqT[:], rhs=ncbT[:],
                                    start=True, stop=True), "t", ("spsc",))
                k.wait(tensor, "v", ("spmx",))
                k.inc(tensor.transpose(p_ms[0:1, 0:2], mxr[:],
                                       sident[0:2, 0:2]), "t", ("spmxt",))
                k.wait(tensor, "v", ("spmxte",))
                k.inc(tensor.matmul(p_rstd[:, 0:2], lhsT=sonesr[:], rhs=mxT[:],
                                    start=True, stop=True), "t", ("spmb",))
                for c in range(2):
                    k.inc(tensor.transpose(p_lie[0][:, c * 2:c * 2 + 2],
                                           scoresR[:, c * 128:(c + 1) * 128],
                                           sident[0:2, 0:2]), "t", ("spsct", c))
                k.wait(tensor, "v", ("spoh", 1))
                for c in range(2):
                    mm = tensor.matmul(p_ms[0:64, 0:2],
                                       lhsT=scb[:, c * 64:(c + 1) * 64],
                                       rhs=soh[:, c * 2:c * 2 + 2],
                                       start=(c == 0), stop=(c == 1))
                k.inc(mm, "t", ("spvc",))

            def omega_h(h):
                for e in range(NE):
                    k.wait(tensor, f"dq{USE_POS[('wtr', (h, e))] % NPOOL}",
                           ("wtr", (h, e)))
                    if e == 0:
                        k.wait(tensor, "v", ("hmvv", 1))
                        k.wait(tensor, "v", ("hmev", 1))
                        if h == 0:
                            k.wait(tensor, "v", ("spoh", 0))
                            k.wait(tensor, "v", ("spnvqte",))
                        else:
                            k.wait(tensor, "v", ("omev", h - 1, 1))
                    buf = pool[POOLBUF[("wtr", (h, e))]]
                    for q in range(2):
                        mm = tensor.matmul(p_lie[q][0:2, :],
                                           lhsT=hmv[:, 2 * e:2 * e + 2],
                                           rhs=buf[:, q * T:(q + 1) * T],
                                           start=(e == 0), stop=(e == 7))
                    k.inc(mm, "t", ("om", h, e))

            def vq_rows():
                k.wait(tensor, "v", ("spvce",))
                k.wait(tensor, "drow", ("wvo",))
                k.wait(tensor, "v", ("omev", 3, 1))
                for s_ in range(2):
                    k.inc(tensor.matmul(p_lie[s_][0:2, :], lhsT=vqcT[:],
                                        rhs=s64row[0:64, s_ * T:(s_ + 1) * T],
                                        start=True, stop=True),
                          "t", ("spvcr", s_))
                for s_ in range(2):
                    k.wait(tensor, "v", ("spc2", s_))
                    for e4 in range(4):
                        e = s_ * 4 + e4
                        dst = p_ms if e % 2 == 0 else p_rstd
                        if e >= 2:
                            k.wait(tensor, "v", ("spcpp", e - 2))
                        k.inc(tensor.transpose(dst[:, 0:2],
                                               booh[:, e4 * 128:(e4 + 1) * 128],
                                               sident[0:2, 0:2]),
                              "t", ("spct", e))

            def g_pe(b):
                k.wait(tensor, "v", ("spomu", b))
                k.wait(tensor, "s", ("bcev", 1))
                k.inc(tensor.transpose(p_bc[0][0:64, 0:64],
                                       omu[:, b * 64:(b + 1) * 64],
                                       sident[0:64, 0:64]), "t", ("sptr", b))
                k.wait(tensor, "v", ("spomb", b))
                k.inc(tensor.matmul(p_bc[1][0:64, 0:64],
                                    lhsT=omg[:, b * 64:(b + 1) * 64],
                                    rhs=omg[:, b * 64:(b + 1) * 64],
                                    start=True, stop=True), "t", ("spm2", b))
                k.wait(tensor, "v", ("spm2e", b))
                k.inc(tensor.matmul(p_bc[0][0:64, 0:64],
                                    lhsT=omg[:, b * 64:(b + 1) * 64],
                                    rhs=m2s[:, b * 64:(b + 1) * 64],
                                    start=True, stop=True), "t", ("spm3", b))
                k.wait(tensor, "v", ("spG", b))
                k.inc(tensor.transpose(p_bc[1][0:64, 0:64], gtmp[:],
                                       sident[0:64, 0:64]), "t", ("gbt", b))
                k.wait(tensor, "v", ("gbte", b))
                k.wait(tensor, "s", ("woos",))
                for q in range(2):
                    mm = tensor.matmul(p_bc[q][0:64, :],
                                       lhsT=GbT[:, b * 64:(b + 1) * 64],
                                       rhs=woos[:, q * T:(q + 1) * T],
                                       start=True, stop=True)
                k.inc(mm, "t", ("wliem", b))

            def lie_t(t):
                b = t // TPB
                for e in range(NE):
                    if e == 0:
                        k.wait(tensor, "v", ("wlie", b))
                        k.wait(tensor, xn_sem(t, 0), ("xn", t, 0))
                        if t == 0:
                            k.wait(tensor, "v", ("spcpp", 7))
                        if t > 0:
                            k.wait(tensor, "v", ("a", t - 1, 6))
                    elif e == 1 and t > 0:
                        k.wait(tensor, "v", ("a", t - 1, 7))
                    elif e >= 2:
                        k.wait(tensor, "v", ("a", t, e - 2))
                    k.inc(tensor.matmul(
                        p_lie[e % 2][:],
                        lhsT=wlie[:, b * EMB + e * 128: b * EMB + (e + 1) * 128],
                        rhs=xnt[t % 2][0:64, 0:T], start=True, stop=True),
                          "t", ("lie", t, e))

            def ms_t(t):
                for e in range(NE):
                    k.wait(tensor, "s", ("sq", t, e))
                    if e == 0:
                        if t > 0:
                            k.wait(tensor, "s", ("sqrt", t - 1))
                        else:
                            k.wait(tensor, "v", ("spcpp", 6))
                    k.inc(tensor.matmul(p_ms[0:1, 0:T], lhsT=sonescb[:],
                                        rhs=sqb[e % 2][:],
                                        start=(e == 0), stop=(e == NE - 1)),
                          "t", ("ms", t, e))

            def rstd_t(t):
                k.wait(tensor, "v", ("recip", t))
                if t > 0:
                    k.wait(tensor, "s", ("rstdev", t - 1))
                else:
                    k.wait(tensor, "v", ("spcpp", 7))
                k.inc(tensor.matmul(p_rstd[:], lhsT=sonesr[:], rhs=rstd_row[:],
                                    start=True, stop=True), "t", ("rstdbc", t))

            AUX = {}

            def add_aux(key, *fns):
                AUX.setdefault(key, []).extend(fns)

            def run_aux(key):
                for fn in AUX.get(key, []):
                    fn()

            for g_ in range(6):
                def mk(g=g_):
                    def f():
                        hm_i(1, 2 * g)
                        hm_i(1, 2 * g + 1)
                    return f
                add_aux(("e2", 0, g_ + 2), mk())
            def hm_tail():
                for i in range(12, 16):
                    hm_i(1, i)
            add_aux(("f", 1, 0), hm_tail, lambda: smr_b(1),
                    lambda: stats_tr(1))
            add_aux(("f", 1, 1), lambda: hmv_b(1))
            add_aux(("f", 1, 2), vq_path)
            add_aux(("f", 1, 4), lambda: omega_h(0))
            add_aux(("f", 1, 8), lambda: omega_h(1))
            add_aux(("f", 1, 12), lambda: omega_h(2))
            add_aux(("f", 1, 16), lambda: omega_h(3))
            add_aux(("f", 1, 20), vq_rows)
            add_aux(("f", 1, 24), lambda: g_pe(0))
            add_aux(("f", 1, 28), lambda: g_pe(1))
            add_aux(("e2", 1, 0), lambda: lie_t(0))
            add_aux(("e2", 1, 2), lambda: ms_t(0), lambda: rstd_t(0))
            add_aux(("e2", 1, 4), lambda: lie_t(1))
            add_aux(("f", 2, 0), lambda: ms_t(1), lambda: rstd_t(1))

            bc_t(0)
            for t in range(NT):
                for f in range(NF):
                    run_aux(("f", t, f))
                    if t == 0 and f % 8 == 0:
                        k.wait(tensor, "p", ("w1t8", f // 8, NE - 1))
                    if f == 0:
                        for e in range(NE):
                            k.wait(tensor, xn_sem(t, e), ("xn", t, e))
                        if t == 0:
                            k.wait(tensor, "v", ("hmev", 0))
                        if t > 0:
                            k.wait(tensor, "s", ("silu", t - 1, 6))
                    elif f == 1:
                        if t == 0:
                            k.wait(tensor, "v", ("hmev", 0))
                        if t > 0:
                            k.wait(tensor, "s", ("silu", t - 1, 7))
                    else:
                        k.wait(tensor, "s", ("gelu", t, f - 2))
                    for e in range(NE):
                        mm = tensor.matmul(
                            p_mm[f % 2][:],
                            lhsT=w1t8[:, e * FF + f * 128: e * FF + (f + 1) * 128],
                            rhs=xnt[t % 2][:, e * T:(e + 1) * T],
                            start=(e == 0), stop=(e == NE - 1))
                    k.inc(mm, "t", ("mm1", t, f))
                run_aux(("f", t, NF))
                if t + 1 < NT:
                    bc_t(t + 1)
                if t >= 2:
                    lie_t(t)

                def mm2_group(e2):
                    run_aux(("e2", t, e2))
                    if e2 == 0:
                        k.wait(tensor, "s", ("gelu", t, NF - 1))
                        if t == 0:
                            k.wait(tensor, "p", ("w2t8", NF - 1))
                    elif e2 >= 2:
                        k.wait(tensor, "s", ("silu", t, e2 - 2))
                    for f in range(NF):
                        mm = tensor.matmul(
                            p_mm[e2 % 2][:],
                            lhsT=w2t8[:, f * EMB + e2 * 128: f * EMB + (e2 + 1) * 128],
                            rhs=hT[:, f * T:(f + 1) * T],
                            start=(f == 0), stop=(f == NF - 1))
                    k.inc(mm, "t", ("mm2", t, e2))

                if t >= 2:
                    mm2_group(0)
                    mm2_group(1)
                    ms_t(t)
                    rstd_t(t)
                    for e2 in range(2, NE):
                        mm2_group(e2)
                else:
                    for e2 in range(NE):
                        mm2_group(e2)
        k.body(blk, "tensor", tensor_body)


def build_program():
    nc = bass.Bass("TRN2", target_bir_lowering=False, debug=False)
    io = {}
    io["xTd"] = nc.dram_tensor("xTd", [NB * EMB, SEQ], F32, kind="ExternalInput").ap()
    io["xTokd"] = nc.dram_tensor("xTokd", [NB * SEQ, EMB], F32, kind="ExternalInput").ap()
    io["w1td"] = nc.dram_tensor("w1td", [EMB, FF], F32, kind="ExternalInput").ap()
    io["w2td"] = nc.dram_tensor("w2td", [FF, EMB], F32, kind="ExternalInput").ap()
    io["w1pred"] = nc.dram_tensor("w1pred", [128, FF], F32, kind="ExternalInput").ap()
    io["w2pred"] = nc.dram_tensor("w2pred", [512, EMB], F32, kind="ExternalInput").ap()
    io["wtriud"] = nc.dram_tensor("wtriud", [EMB, FF], F32, kind="ExternalInput").ap()
    io["btriud"] = nc.dram_tensor("btriud", [64, 64], F32, kind="ExternalInput").ap()
    io["wooTd"] = nc.dram_tensor("wooTd", [64, EMB], F32, kind="ExternalInput").ap()
    io["wviTd"] = nc.dram_tensor("wviTd", [128, 512], F32, kind="ExternalInput").ap()
    io["wvoTd"] = nc.dram_tensor("wvoTd", [64, EMB], F32, kind="ExternalInput").ap()
    io["cbd"] = nc.dram_tensor("cbd", [128, 128], F32, kind="ExternalInput").ap()
    io["b1d"] = nc.dram_tensor("b1d", [128, 32], F32, kind="ExternalInput").ap()
    io["b2d"] = nc.dram_tensor("b2d", [128, 8], F32, kind="ExternalInput").ap()
    io["nwd"] = nc.dram_tensor("nwd", [128, 8], F32, kind="ExternalInput").ap()
    io["bvi2d"] = nc.dram_tensor("bvi2d", [2, 64], F32, kind="ExternalInput").ap()
    io["bvo2d"] = nc.dram_tensor("bvo2d", [2, EMB], F32, kind="ExternalInput").ap()
    io["boo2d"] = nc.dram_tensor("boo2d", [2, EMB], F32, kind="ExternalInput").ap()
    io["mixd"] = nc.dram_tensor("mixd", [128, 2], F32, kind="ExternalInput").ap()
    io["identd"] = nc.dram_tensor("identd", [128, 128], F32, kind="ExternalInput").ap()
    io["onesrd"] = nc.dram_tensor("onesrd", [1, 128], F32, kind="ExternalInput").ap()
    io["onescbd"] = nc.dram_tensor("onescbd", [128, 1], BF16, kind="ExternalInput").ap()
    io["onescfd"] = nc.dram_tensor("onescfd", [128, 1], F32, kind="ExternalInput").ap()
    io["cpoold"] = nc.dram_tensor("cpoold", [128, 4], F32, kind="ExternalInput").ap()
    io["omdram"] = nc.dram_tensor("omdram", [NB, 4096], F32, kind="Internal").ap()
    io["stats_d"] = nc.dram_tensor("stats_d", [4, SEQ], F32, kind="Internal").ap()
    io["ccin"] = nc.dram_tensor("ccin", [1, 2], F32, kind="Internal").ap()
    io["ccout"] = nc.dram_tensor("ccout", [1, 2], F32, kind="Internal").ap()
    io["outTd"] = nc.dram_tensor("outTd", [NB * EMB, SEQ], F32, kind="ExternalOutput").ap()

    k1 = K(False)
    construct(k1, {n: Fake() for n in io})
    k2 = K(True, nc, M=k1.M)
    with k2.es:
        construct(k2, io)
    return nc


def kernel(L0_attn_x, L0_attn_omega_proj_w, L0_attn_omega_proj_b, L0_attn_omega_out_w,
           L0_attn_omega_out_b, L0_attn_vq_proj_in_w, L0_attn_vq_proj_in_b,
           L0_attn_vq_proj_out_w, L0_attn_vq_proj_out_b, L0_attn_codebook,
           L0_attn_omega_mix, L0_attn_vq_mix, L0_attn_norm_w, L0_ffn_W1, L0_ffn_b1,
           L0_ffn_W2, L0_ffn_b2):
    f32 = np.float32
    x = np.asarray(L0_attn_x, f32)
    xT = np.ascontiguousarray(x.transpose(0, 2, 1))
    w1t = np.ascontiguousarray(np.asarray(L0_ffn_W1, f32).T)
    w2t = np.ascontiguousarray(np.asarray(L0_ffn_W2, f32).T)

    opw = np.asarray(L0_attn_omega_proj_w, f32)
    opb = np.asarray(L0_attn_omega_proj_b, f32)
    R, C = np.triu_indices(64, k=1)
    wtriu = np.zeros((EMB, 64, 64), f32)
    wtriu[:, R, C] = opw.T
    wtriu = np.ascontiguousarray(wtriu.reshape(EMB, FF))
    btriu = np.zeros((64, 64), f32)
    btriu[R, C] = opb

    wooT = np.ascontiguousarray(np.asarray(L0_attn_omega_out_w, f32).T)
    wviT = np.asarray(L0_attn_vq_proj_in_w, f32).T
    wviT = np.ascontiguousarray(wviT.reshape(8, 128, 64).transpose(1, 0, 2).reshape(128, 512))
    wvoT = np.ascontiguousarray(np.asarray(L0_attn_vq_proj_out_w, f32).T)
    cb = np.asarray(L0_attn_codebook, f32)
    cbPP = np.ascontiguousarray(cb.reshape(2, 128, 64).transpose(1, 0, 2).reshape(128, 128))
    b1PP = np.ascontiguousarray(np.asarray(L0_ffn_b1, f32).reshape(32, 128).T)
    b2PP = np.ascontiguousarray(np.asarray(L0_ffn_b2, f32).reshape(8, 128).T)
    nwPP = np.ascontiguousarray(np.asarray(L0_attn_norm_w, f32).reshape(8, 128).T)
    bvi2 = np.ascontiguousarray(np.broadcast_to(np.asarray(L0_attn_vq_proj_in_b, f32), (2, 64)))
    bvo2 = np.ascontiguousarray(np.broadcast_to(np.asarray(L0_attn_vq_proj_out_b, f32), (2, EMB)))
    boo2 = np.ascontiguousarray(np.broadcast_to(np.asarray(L0_attn_omega_out_b, f32), (2, EMB)))
    mixd = np.empty((128, 2), f32)
    mixd[:, 0] = np.float32(L0_attn_omega_mix)
    mixd[:, 1] = np.float32(L0_attn_vq_mix)
    identd = np.eye(128, dtype=f32)
    onesrd = np.ones((1, 128), f32)
    onescbd = np.ones((128, 1), ml_dtypes.bfloat16)
    onescfd = np.ones((128, 1), np.float32)
    cpoold = np.broadcast_to(np.array([1e-6, 1e-5, 0.0, 0.0], f32), (128, 4)).copy()

    shared = {
        "w1td": w1t, "w2td": w2t, "wtriud": wtriu, "btriud": btriu, "wooTd": wooT,
        "wviTd": wviT, "wvoTd": wvoT, "cbd": cbPP, "b1d": b1PP, "b2d": b2PP,
        "nwd": nwPP, "bvi2d": bvi2, "bvo2d": bvo2, "boo2d": boo2, "mixd": mixd,
        "identd": identd, "onesrd": onesrd, "onescbd": onescbd,
        "onescfd": onescfd, "cpoold": cpoold,
    }
    in_maps = []
    for c in range(NCORES):
        m = dict(shared)
        m["xTd"] = np.ascontiguousarray(xT[2 * c:2 * c + 2].reshape(NB * EMB, SEQ))
        m["xTokd"] = np.ascontiguousarray(x[2 * c:2 * c + 2].reshape(NB * SEQ, EMB))
        m["w1pred"] = np.ascontiguousarray(w1t[c * 128:(c + 1) * 128, :])
        m["w2pred"] = np.ascontiguousarray(w2t[c * 512:(c + 1) * 512, :])
        in_maps.append(m)

    nc = build_program()
    res = run_bass_kernel_spmd(nc, in_maps, core_ids=list(range(NCORES)))
    outs = []
    for c in range(NCORES):
        o = np.asarray(res.results[c]["outTd"], f32).reshape(NB, EMB, SEQ)
        outs.append(o.transpose(0, 2, 1))
    return np.ascontiguousarray(np.concatenate(outs, axis=0))


# version 10
# speedup vs baseline: 1.2904x; 1.2904x over previous
import numpy as np
import ml_dtypes
from contextlib import ExitStack, contextmanager

import concourse.bass as bass
import concourse.mybir as mybir
from concourse.bass_utils import run_bass_kernel_spmd
from concourse.alu_op_type import AluOpType as ALU

F32 = mybir.dt.float32
BF16 = mybir.dt.bfloat16
FP8 = mybir.dt.float8e4
AF = mybir.ActivationFunctionType
AX = mybir.AxisListType

EMB = 1024
FF = 4096
SEQ = 2048
NB = 2
NE = 8
NF = 32
T = 512
NT = 8
TPB = 4
NCORES = 8
CH = 1024
MEAN_W = float(1.0 / (FF * EMB))
LN_INV = float(1.0 / EMB)
SQ_SCALE = float(1.0 / 32.0)
SEQ_INV = float(1.0 / SEQ)
NPOOL = 3


class Fake:
    def __getattr__(self, n):
        return self

    def __call__(self, *a, **k):
        return self

    def __getitem__(self, i):
        return self

    def __enter__(self):
        return self

    def __exit__(self, *a):
        return False


class K:

    def __init__(self, emit, nc=None, M=None):
        self.emit = emit
        self.nc = nc
        self.cnt = {}
        self.M = M if M is not None else {}
        self.sems = {}
        self.es = ExitStack()

    def sem(self, name):
        self.cnt[name] = 0
        if self.emit:
            self.sems[name] = self.es.enter_context(self.nc.semaphore(name))

    def inc(self, instr, s, key=None, n=1):
        self.cnt[s] += n
        if key is not None:
            if not self.emit:
                self.M[(s,) + tuple(key)] = self.cnt[s]
        if self.emit and instr is not None:
            instr.then_inc(self.sems[s], n)

    def wait(self, eng, s, key):
        if self.emit:
            eng.wait_ge(self.sems[s], self.M[(s,) + tuple(key)])

    def dma(self, eng, semname, key, out, in_, chain=True):
        cur = self.cnt[semname]
        if self.emit and chain and cur:
            eng.wait_ge(self.sems[semname], cur)
        instr = eng.dma_start(out=out, in_=in_) if self.emit else None
        self.inc(instr, semname, key, 16)

    def sb(self, name, shape, dtype=F32):
        if self.emit:
            return self.es.enter_context(self.nc.sbuf_tensor(name, shape, dtype))
        return Fake()

    def ps(self, name, shape, dtype=F32):
        if self.emit:
            return self.es.enter_context(self.nc.psum_tensor(name, shape, dtype))
        return Fake()

    @contextmanager
    def block(self, name):
        if self.emit:
            with self.nc.Block(name) as b:
                yield b
        else:
            yield Fake()

    def body(self, blk, engname, fn):
        if self.emit:
            getattr(blk, engname)(fn)
        else:
            fn(Fake())


def xn_sem(t, e):
    return "v" if (t < 2 or e % 2 == 0) else "p"


def sg_sem(e2):
    return "v" if e2 % 2 == 0 else "p"


def construct(k: "K", io):
    CONSTS = ["identd", "onesrd", "onescbd", "onescfd", "cpoold", "b1d",
              "b2d", "nwd", "mixd", "cbd", "btriud", "bvi2d"]
    for s in ("t", "v", "s", "p", "cc"):
        k.sem(s)
    dma_sems = ["dcs", "dq0", "dq1", "dq2", "dxl0", "dxl1", "dxr0",
                "dxr1", "do0", "do1", "dst0", "dst1", "dsw", "dow0", "dor",
                "dcc", "drow"]
    for s in dma_sems:
        k.sem(s)

    w1t8 = k.sb("w1t8", [128, NE * FF], FP8)
    w2t8 = k.sb("w2t8", [128, NF * EMB], FP8)
    hT = k.sb("hT", [128, NF * T], BF16)
    xnt = [k.sb(f"xnt{i}", [128, NE * T], BF16) for i in range(2)]
    a_all = k.sb("a_all", [128, NE * T], BF16)
    gtb = k.sb("gtb", [128, NE * T], BF16)
    sqb = [k.sb(f"sqb{i}", [128, T], BF16) for i in range(2)]
    outb = [k.sb(f"outb{i}", [128, T]) for i in range(2)]
    xlnb = [k.sb(f"xlnb{i}", [128, T]) for i in range(2)]
    xresb = [k.sb(f"xresb{i}", [128, T]) for i in range(2)]
    mrB = [k.sb(f"mrB{i}", [128, T]) for i in range(2)]
    rsB = [k.sb(f"rsB{i}", [128, T]) for i in range(2)]
    rstdB = k.sb("rstdB", [128, T])
    mrbuf = [k.sb("mrbuf0", [1, T])]
    rsbuf = [k.sb("rsbuf0", [1, T])]
    rms_row = k.sb("rms_row", [1, T])
    rstd_row = k.sb("rstd_row", [1, T])
    pool = [k.sb(f"pool{i}", [128, CH]) for i in range(NPOOL)]
    mposb = [k.sb(f"mposb{i}", [128, CH], BF16) for i in range(2)]
    mneg8 = [k.sb(f"mneg8{i}", [128, CH], FP8) for i in range(2)]
    mcols = k.sb("mcols", [128, 16])
    rscols = k.sb("rscols", [128, 16])
    mrcols = k.sb("mrcols", [128, 16])
    sqcols = k.sb("sqcols", [128, 16])
    varcols = k.sb("varcols", [128, 16])
    mrT = k.sb("mrT", [16, 128])
    rsT = k.sb("rsT", [16, 128])
    hmv = k.sb("hmv", [128, 16])
    thrpart = k.sb("thrpart", [128, 8])
    thrcomb = k.sb("thrcomb", [128, 2])
    thrsums = k.sb("thrsums", [1, 2])
    thrsumg = k.sb("thrsumg", [1, 2])
    thrrow = k.sb("thrrow", [1, 2])
    thrcol = k.sb("thrcol", [128, 2])
    nthrcol = k.sb("nthrcol", [128, 2])
    sident = k.sb("sident", [128, 128])
    sonesr = k.sb("sonesr", [1, 128])
    sonescb = k.sb("sonescb", [128, 1], BF16)
    sonescf = k.sb("sonescf", [128, 1])
    scpool = k.sb("scpool", [128, 4])
    sb1 = k.sb("sb1", [128, 32])
    sb2 = k.sb("sb2", [128, 8])
    snw = k.sb("snw", [128, 8])
    smix = k.sb("smix", [128, 2])
    scb = k.sb("scb", [128, 128])
    sbtriu = k.sb("sbtriu", [64, 64])
    sbvi2 = k.sb("sbvi2", [2, 64])
    s64row = k.sb("s64row", [128, EMB])
    woos = k.sb("woos", [64, EMB], BF16)
    omu = k.sb("omu", [64, NB * 64])
    omg = k.sb("omg", [64, NB * 64])
    m2s = k.sb("m2s", [64, NB * 64])
    m3s = k.sb("m3s", [64, NB * 64])
    gtmp = k.sb("gtmp", [64, 64])
    GbT = k.sb("GbT", [64, NB * 64], BF16)
    wlie = k.sb("wlie", [64, NB * EMB], BF16)
    vqi = k.sb("vqi", [2, 64])
    vqq = k.sb("vqq", [2, 64])
    vnorm = k.sb("vnorm", [2, 1])
    nvq = k.sb("nvq", [2, 64])
    nvqT = k.sb("nvqT", [64, 2])
    csq = k.sb("csq", [128, 128])
    cnorm = k.sb("cnorm", [128, 2])
    sncb = k.sb("sncb", [128, 128])
    ncbT = k.sb("ncbT", [64, 256])
    scoresR = k.sb("scoresR", [2, 256])
    mxr = k.sb("mxr", [2, 1])
    mxT = k.sb("mxT", [1, 2])
    scT = k.sb("scT", [128, 4])
    soh = k.sb("soh", [128, 4])
    vqcT = k.sb("vqcT", [64, 2])
    bvoh = k.sb("bvoh", [2, T])
    booh = k.sb("booh", [2, T])
    constPP = k.sb("constPP", [128, 16])

    p_mm = [k.ps(f"p_mm{i}", [128, T]) for i in range(2)]
    p_lie = [k.ps(f"p_lie{i}", [128, T]) for i in range(2)]
    p_ms = k.ps("p_ms", [128, T])
    p_rstd = k.ps("p_rstd", [128, T])
    p_bc = [k.ps(f"p_bc{i}", [128, T]) for i in range(2)]

    def src_of(stream, idx):
        if stream == "w1s":
            return io["w1pred"][:, idx * CH:(idx + 1) * CH]
        if stream == "w2s":
            return io["w2pred"][idx * 128:(idx + 1) * 128, :]
        if stream == "xt":
            b, i = idx
            return io["xTokd"][b * SEQ + i * 128: b * SEQ + (i + 1) * 128, :]
        if stream == "w1b":
            h, e = idx
            return io["w1td"][e * 128:(e + 1) * 128, h * CH:(h + 1) * CH]
        if stream == "w2b":
            return io["w2td"][idx * 128:(idx + 1) * 128, :]
        if stream == "wtr":
            h, e = idx
            return io["wtriud"][e * 128:(e + 1) * 128, h * CH:(h + 1) * CH]
        raise KeyError(stream)

    def release_of(stream, idx):
        if stream in ("w1s", "w2s"):
            w = 0 if stream == "w1s" else 1
            return ("v", ("wred", w, idx))
        if stream == "xt":
            return ("t", ("hm",) + idx)
        if stream == "w1b":
            h, e = idx
            return ("s", ("sgn2", 0, h * NE + e))
        if stream == "w2b":
            return ("s", ("sgn2", 1, idx))
        if stream == "wtr":
            return ("t", ("om",) + idx)
        raise KeyError(stream)

    uses = []
    for j in range(4):
        uses.append(("w1s", j))
    for j in range(4):
        uses.append(("w2s", j))
    for i in range(16):
        uses.append(("xt", (0, i)))
    for h in range(4):
        for e in range(NE):
            uses.append(("w1b", (h, e)))
    for F in range(NF):
        uses.append(("w2b", F))
    for i in range(16):
        uses.append(("xt", (1, i)))
    for h in range(4):
        for e in range(NE):
            uses.append(("wtr", (h, e)))
    POOLBUF = {u: n % NPOOL for n, u in enumerate(uses)}
    USE_POS = {u: n for n, u in enumerate(uses)}

    def x_src(t, e):
        b, i = t // TPB, t % TPB
        return io["xTd"][b * EMB + e * 128: b * EMB + (e + 1) * 128,
                         i * T:(i + 1) * T]

    def out_dst(t, e):
        b, i = t // TPB, t % TPB
        return io["outTd"][b * EMB + e * 128: b * EMB + (e + 1) * 128,
                           i * T:(i + 1) * T]

    def stats_src(t, row):
        b, i = t // TPB, t % TPB
        return io["stats_d"][b * 2 + row: b * 2 + row + 1, i * T:(i + 1) * T]

    def xln_prev(t, e):
        if e >= 2:
            return (t, e - 2)
        if t > 0:
            return (t - 1, e + 6)
        return None

    with k.block("main") as blk:
        def sync_body(sync):
            cdst = {"identd": sident, "onesrd": sonesr, "onescbd": sonescb,
                    "cpoold": scpool, "b1d": sb1, "b2d": sb2, "nwd": snw,
                    "mixd": smix, "cbd": scb, "btriud": sbtriu,
                    "bvi2d": sbvi2, "onescfd": sonescf}
            for n in CONSTS:
                k.dma(sync, "dcs", ("c", n), out=cdst[n][:], in_=io[n],
                      chain=False)
            k.inc(None, "dcs", ("call",), 0)
            k.dma(sync, "drow", ("woo",), out=s64row[0:64, :], in_=io["wooTd"])

            def fifo(u):
                stream, idx = u
                pos = USE_POS[u]
                if pos >= NPOOL:
                    ps_, pk = release_of(*uses[pos - NPOOL])
                    k.wait(sync, ps_, pk)
                k.dma(sync, f"dq{pos % NPOOL}", u, out=pool[POOLBUF[u]][:],
                      in_=src_of(stream, idx))

            for j in range(4):
                fifo(("w1s", j))
            for j in range(4):
                fifo(("w2s", j))
            k.wait(sync, "v", ("thrsums",))
            k.dma(sync, "dcc", ("ccw",), out=io["ccin"], in_=thrsums[:])
            for i in range(16):
                fifo(("xt", (0, i)))
            k.wait(sync, "cc", ("ccdone",))
            k.dma(sync, "dcc", ("ccr",), out=thrsumg[:], in_=io["ccout"])
            k.wait(sync, "s", ("stev", 0))
            k.dma(sync, "dsw", ("swb", 0, 0),
                  out=io["stats_d"][0:1, :].rearrange(
                      "one (c p) -> (one c) p", c=16),
                  in_=mrT[:])
            k.dma(sync, "dsw", ("swb", 0, 1),
                  out=io["stats_d"][1:2, :].rearrange(
                      "one (c p) -> (one c) p", c=16),
                  in_=rsT[:])
            k.wait(sync, "dsw", ("swb", 0, 1))
            k.dma(sync, "dst0", ("stm", 0), out=mrbuf[0][:], in_=stats_src(0, 0))
            k.dma(sync, "dst1", ("str", 0), out=rsbuf[0][:], in_=stats_src(0, 1))
            t0slots = [xlnb[0], xlnb[1], xresb[0], xresb[1], mrB[1], rsB[1],
                       rstdB, outb[1]]
            for e in range(NE):
                k.dma(sync, f"dxl{e % 2}", ("xln", 0, e),
                      out=t0slots[e][:], in_=x_src(0, e))
            for h in range(4):
                for e in range(NE):
                    fifo(("w1b", (h, e)))
            k.wait(sync, "t", ("bc", 0))
            for t_ in (1,):
                k.dma(sync, "dst0", ("stm", t_), out=mrbuf[0][:],
                      in_=stats_src(t_, 0))
                k.dma(sync, "dst1", ("str", t_), out=rsbuf[0][:],
                      in_=stats_src(t_, 1))
            k.wait(sync, "s", ("woos",))
            k.dma(sync, "drow", ("wvi",), out=s64row[:, 0:512], in_=io["wviTd"])
            for F in range(NF):
                fifo(("w2b", F))
                if F == 4:
                    for e in range(2):
                        pt, pe = xln_prev(1, e)
                        k.wait(sync, xn_sem(pt, pe), ("xn", pt, pe))
                        k.dma(sync, f"dxl{e % 2}", ("xln", 1, e),
                              out=xlnb[e % 2][:], in_=x_src(1, e))
                if F == 8:
                    k.wait(sync, "t", ("bc", 1))
                    for t_ in (2,):
                        k.dma(sync, "dst0", ("stm", t_), out=mrbuf[0][:],
                              in_=stats_src(t_, 0))
                        k.dma(sync, "dst1", ("str", t_), out=rsbuf[0][:],
                              in_=stats_src(t_, 1))
                if F == 12:
                    for e2 in (0, 1):
                        k.wait(sync, "v", ("xn", 0, e2 + 2))
                        k.dma(sync, f"dxr{e2 % 2}", ("xres", 0, e2),
                              out=xresb[e2 % 2][:], in_=x_src(0, e2))
            for i in range(16):
                fifo(("xt", (1, i)))
            for e in range(2, NE):
                pt, pe = xln_prev(1, e)
                k.wait(sync, xn_sem(pt, pe), ("xn", pt, pe))
                k.dma(sync, f"dxl{e % 2}", ("xln", 1, e),
                      out=xlnb[e % 2][:], in_=x_src(1, e))
            for e in range(2):
                pt, pe = xln_prev(2, e)
                k.wait(sync, xn_sem(pt, pe), ("xn", pt, pe))
                k.dma(sync, f"dxl{e % 2}", ("xln", 2, e),
                      out=xlnb[e % 2][:], in_=x_src(2, e))
            k.wait(sync, "s", ("stev", 1))
            k.dma(sync, "dsw", ("swb", 1, 0),
                  out=io["stats_d"][2:3, :].rearrange("one (c p) -> (one c) p", c=16),
                  in_=mrT[:])
            k.dma(sync, "dsw", ("swb", 1, 1),
                  out=io["stats_d"][3:4, :].rearrange("one (c p) -> (one c) p", c=16),
                  in_=rsT[:])
            k.dma(sync, "drow", ("bvo0a",), out=bvoh[:], in_=io["bvo2d"][:, 0:T])
            k.dma(sync, "drow", ("bvo", 0), out=booh[:], in_=io["boo2d"][:, 0:T])
            for h in range(4):
                for e in range(NE):
                    fifo(("wtr", (h, e)))
                for q in range(2):
                    k.wait(sync, "v", ("omev", h, q))
                    k.dma(sync, "dow0", ("omw", h, q),
                          out=io["omdram"][0:2, h * CH + q * T: h * CH + (q + 1) * T],
                          in_=outb[0][0:2, :])
            k.wait(sync, "t", ("spvq",))
            k.dma(sync, "drow", ("wvo",), out=s64row[0:64, :], in_=io["wvoTd"])
            k.wait(sync, "dow0", ("omw", 3, 1))
            for b in range(NB):
                rdsrc = io["omdram"][b:b + 1, :].rearrange(
                    "one (r c) -> (one r) c", r=64)
                k.dma(sync, "dor", ("omr", b), out=omu[:, b * 64:(b + 1) * 64],
                      in_=rdsrc)
            k.wait(sync, "v", ("spc2", 0))
            k.dma(sync, "drow", ("bvo1a",), out=bvoh[:], in_=io["bvo2d"][:, T:EMB])
            k.dma(sync, "drow", ("bvo", 1), out=booh[:], in_=io["boo2d"][:, T:EMB])
            for t in range(NT):
                if 2 <= t + 1 < NT:
                    erange = range(2, NE) if t + 1 == 2 else range(NE)
                    for e in erange:
                        pt, pe = xln_prev(t + 1, e)
                        k.wait(sync, xn_sem(pt, pe), ("xn", pt, pe))
                        k.dma(sync, f"dxl{e % 2}", ("xln", t + 1, e),
                              out=xlnb[e % 2][:], in_=x_src(t + 1, e))
                if t + 3 < NT:
                    k.wait(sync, "t", ("bc", t + 2))
                    if t + 3 == 4:
                        k.wait(sync, "dsw", ("swb", 1, 1))
                    k.dma(sync, "dst0", ("stm", t + 3), out=mrbuf[0][:],
                          in_=stats_src(t + 3, 0))
                    k.dma(sync, "dst1", ("str", t + 3), out=rsbuf[0][:],
                          in_=stats_src(t + 3, 1))
                for e2 in range(NE):
                    if not (t == 0 and e2 < 2):
                        pe = e2 - 2
                        pt = t
                        if pe < 0:
                            pe += 8
                            pt -= 1
                        k.wait(sync, sg_sem(pe), ("sg", pt, pe))
                        k.dma(sync, f"dxr{e2 % 2}", ("xres", t, e2),
                              out=xresb[e2 % 2][:], in_=x_src(t, e2))
                    if e2 >= 2:
                        k.wait(sync, sg_sem(e2 - 2), ("sg", t, e2 - 2))
                        k.dma(sync, f"do{e2 % 2}", ("od", t, e2 - 2),
                              out=out_dst(t, e2 - 2), in_=outb[e2 % 2][:])
                for e2 in (6, 7):
                    k.wait(sync, sg_sem(e2), ("sg", t, e2))
                    k.dma(sync, f"do{e2 % 2}", ("od", t, e2),
                          out=out_dst(t, e2), in_=outb[e2 % 2][:])
            k.wait(sync, "do0", ("od", NT - 1, NE - 2))
            k.wait(sync, "do1", ("od", NT - 1, NE - 1))
        k.body(blk, "sync", sync_body)

        def gp_body(gp):
            k.wait(gp, "dcs", ("call",))
            k.wait(gp, "dcc", ("ccw",))
            cci = gp.collective_compute(
                "AllReduce", ALU.add, replica_groups=[list(range(NCORES))],
                ins=[io["ccin"]], outs=[io["ccout"]]) if k.emit else None
            k.inc(cci, "cc", ("ccdone",))

            def tcomb_gp(w, clist):
                for c in clist:
                    if w == 0:
                        h, e = c // NE, c % NE
                        dst = w1t8[:, e * FF + h * CH: e * FF + (h + 1) * CH]
                        key = ("w1t8", h, e)
                    else:
                        dst = w2t8[:, c * EMB:(c + 1) * EMB]
                        key = ("w2t8", c)
                    k.wait(gp, "s", ("sgn2", w, c))
                    gp.tensor_tensor(mposb[c % 2][:], mposb[c % 2][:],
                                     mneg8[c % 2][:], op=ALU.add)
                    k.inc(gp.tensor_scalar(dst, mposb[c % 2][:], scalar1=0.5,
                                           scalar2=None, op0=ALU.mult),
                          "p", key)

            tcomb_gp(0, list(range(1, 32, 2)))
            tcomb_gp(1, list(range(1, 32, 2)))
            for t in range(NT):
                for e2 in range(1, NE, 2):
                    k.wait(gp, "s", ("silu", t, e2))
                    k.wait(gp, "s", ("rstdev", t))
                    k.wait(gp, "v", ("a", t, e2))
                    k.wait(gp, f"dxr{e2 % 2}", ("xres", t, e2))
                    if (t, e2) > (0, 1):
                        pe = e2 - 2
                        pt = t
                        if pe < 0:
                            pe += 8
                            pt -= 1
                        k.wait(gp, f"do{pe % 2}", ("od", pt, pe))
                    gp.tensor_tensor(outb[e2 % 2][:],
                                     gtb[:, e2 * T:(e2 + 1) * T],
                                     rstdB[:], op=ALU.mult)
                    gp.tensor_tensor(outb[e2 % 2][:], outb[e2 % 2][:],
                                     a_all[:, e2 * T:(e2 + 1) * T], op=ALU.mult)
                    k.inc(gp.tensor_tensor(outb[e2 % 2][:], outb[e2 % 2][:],
                                           xresb[e2 % 2][:], op=ALU.add),
                          "p", ("sg", t, e2))
                t2 = t + 2
                if t2 < NT:
                    for e in range(1, NE, 2):
                        k.wait(gp, f"dxl{e % 2}", ("xln", t2, e))
                        k.wait(gp, "s", ("bcev", t2))
                        k.wait(gp, "v", ("a", t, e))
                        if e == 1:
                            k.wait(gp, "do1", ("od", t, 7))
                        gp.tensor_tensor(outb[1][:], xlnb[e % 2][:],
                                         mrB[t2 % 2][:], op=ALU.subtract)
                        k.inc(gp.tensor_tensor(xnt[t2 % 2][:, e * T:(e + 1) * T],
                                               outb[1][:], rsB[t2 % 2][:],
                                               op=ALU.mult), "p", ("xn", t2, e))
        k.body(blk, "gpsimd", gp_body)

        def vector_body(vector):
            k.wait(vector, "dcs", ("call",))
            for w in range(2):
                for j in range(4):
                    stream = "w1s" if w == 0 else "w2s"
                    k.wait(vector, f"dq{USE_POS[(stream, j)] % NPOOL}", (stream, j))
                    k.inc(vector.reduce_sum(thrpart[:, w * 4 + j:w * 4 + j + 1],
                                            pool[POOLBUF[(stream, j)]][:],
                                            axis=AX.X, apply_absolute_value=True),
                          "v", ("wred", w, j))
            for w in range(2):
                k.inc(vector.reduce_sum(thrcomb[:, w:w + 1],
                                        thrpart[:, w * 4:w * 4 + 4], axis=AX.X),
                      "v", ("thrc0", w))
            k.wait(vector, "t", ("thrtr",))
            vector.tensor_copy(rms_row[0:1, 0:128], p_bc[0][0:1, 0:128])
            k.inc(vector.reduce_sum(thrsums[0:1, 0:1], rms_row[0:1, 0:128],
                                    axis=AX.X), "v", ("ts0",))
            k.wait(vector, "t", ("thrtr2",))
            vector.tensor_copy(rms_row[0:1, 0:128], p_bc[1][0:1, 0:128])
            k.inc(vector.reduce_sum(thrsums[0:1, 1:2], rms_row[0:1, 0:128],
                                    axis=AX.X), "v", ("thrsums",))

            def stat_chain(b, i):
                k.wait(vector, "s", ("acc", b, i))
                k.inc(vector.tensor_tensor(varcols[:, i:i + 1], sqcols[:, i:i + 1],
                                           varcols[:, i:i + 1], op=ALU.subtract),
                      "v", ("var", b, i))
                k.wait(vector, "s", ("sqv", b, i))
                k.inc(vector.reciprocal(rscols[:, i:i + 1], varcols[:, i:i + 1]),
                      "v", ("rscr", b, i))
                k.wait(vector, "v", ("rscr", b, i))
                k.inc(vector.tensor_tensor(mrcols[:, i:i + 1], mcols[:, i:i + 1],
                                           rscols[:, i:i + 1], op=ALU.mult),
                      "v", ("rsc", b, i))

            for i in range(16):
                stat_chain(0, i)
            vector.reduce_sum(thrpart[:, 0:1], mrcols[:], axis=AX.X)
            k.inc(vector.tensor_copy(thrcomb[:, 0:1], thrpart[:, 0:1]),
                  "v", ("mrsum", 0))
            k.wait(vector, "t", ("smr", 0))
            k.inc(vector.tensor_copy(mcols[0:1, 0:1], p_ms[0:1, 20:21]),
                  "v", ("smre", 0))
            k.wait(vector, "t", ("hm", 0, 15))
            vector.tensor_scalar(rms_row[:], p_mm[0][0:1, :],
                                 scalar1=mcols[0:1, 0:1], scalar2=None,
                                 op0=ALU.subtract)
            k.inc(vector.tensor_scalar(rstd_row[:], p_mm[1][0:1, :],
                                       scalar1=mcols[0:1, 0:1], scalar2=None,
                                       op0=ALU.subtract), "v", ("hmev", 0))
            k.wait(vector, "t", ("hmv", 0))
            k.inc(vector.tensor_scalar(
                hmv[:, 0:16].rearrange("p (e b) -> p b e", b=2)[:, 0, :],
                p_ms[:, 0:16].rearrange("p (e b) -> p b e", b=2)[:, 0, :],
                scalar1=SEQ_INV, scalar2=None, op0=ALU.mult), "v", ("hmvv", 0))
            k.wait(vector, "dcc", ("ccr",))
            k.inc(vector.tensor_scalar(thrrow[:], thrsumg[:], scalar1=MEAN_W,
                                       scalar2=1e-8, op0=ALU.mult, op1=ALU.max),
                  "v", ("thrrow",))
            k.wait(vector, "t", ("thrbc",))
            vector.tensor_scalar(thrcol[:], p_bc[0][:, 0:2], scalar1=0.5,
                                 scalar2=None, op0=ALU.mult)
            k.inc(vector.tensor_scalar(nthrcol[:], p_bc[0][:, 0:2], scalar1=-0.5,
                                       scalar2=None, op0=ALU.mult),
                  "v", ("thrc",))

            t0slots_v = [xlnb[0], xlnb[1], xresb[0], xresb[1], mrB[1], rsB[1],
                         rstdB, outb[1]]

            def xn_tile(t):
                for e in range(NE):
                    if t == 0:
                        k.wait(vector, f"dxl{e % 2}", ("xln", t, e))
                        src = t0slots_v[e]
                    else:
                        k.wait(vector, f"dxl{e % 2}", ("xln", t, e))
                        src = xlnb[e % 2]
                    k.wait(vector, "s", ("bcev", t))
                    vector.tensor_tensor(outb[0][:], src[:], mrB[t % 2][:],
                                         op=ALU.subtract)
                    k.inc(vector.tensor_tensor(xnt[t % 2][:, e * T:(e + 1) * T],
                                               outb[0][:], rsB[t % 2][:],
                                               op=ALU.mult), "v", ("xn", t, e))

            def tcomb_v(w, clist):
                for c in clist:
                    if w == 0:
                        h, e = c // NE, c % NE
                        dst = w1t8[:, e * FF + h * CH: e * FF + (h + 1) * CH]
                        key = ("w1t8", h, e)
                    else:
                        dst = w2t8[:, c * EMB:(c + 1) * EMB]
                        key = ("w2t8", c)
                    k.wait(vector, "s", ("sgn2", w, c))
                    vector.tensor_tensor(mposb[c % 2][:], mposb[c % 2][:],
                                         mneg8[c % 2][:], op=ALU.add)
                    k.inc(vector.tensor_scalar(dst, mposb[c % 2][:], scalar1=0.5,
                                               scalar2=None, op0=ALU.mult),
                          "v", key)

            tcomb_v(0, list(range(0, 8, 2)))
            xn_tile(0)
            tcomb_v(0, list(range(8, 32, 2)))
            tcomb_v(1, list(range(0, 32, 2)))
            k.wait(vector, "t", ("sttr", 0))
            k.wait(vector, "v", ("hmev", 0))
            for i in range(16):
                stat_chain(1, i)
            xn_tile(1)
            vector.reduce_sum(thrpart[:, 0:1], mrcols[:], axis=AX.X)
            k.inc(vector.tensor_copy(thrcomb[:, 0:1], thrpart[:, 0:1]),
                  "v", ("mrsum", 1))
            k.wait(vector, "t", ("smr", 1))
            k.inc(vector.tensor_copy(mcols[0:1, 0:1], p_ms[0:1, 20:21]),
                  "v", ("smre", 1))
            k.wait(vector, "t", ("hm", 1, 15))
            k.wait(vector, "t", ("hmv", 0))
            vector.tensor_scalar(rms_row[:], p_lie[0][0:1, :],
                                 scalar1=mcols[0:1, 0:1], scalar2=None,
                                 op0=ALU.subtract)
            k.inc(vector.tensor_scalar(rstd_row[:], p_lie[1][0:1, :],
                                       scalar1=mcols[0:1, 0:1], scalar2=None,
                                       op0=ALU.subtract), "v", ("hmev", 1))
            k.wait(vector, "t", ("hmv", 1))
            k.inc(vector.tensor_scalar(
                hmv[:, 0:16].rearrange("p (e b) -> p b e", b=2)[:, 1, :],
                p_ms[:, 0:16].rearrange("p (e b) -> p b e", b=2)[:, 1, :],
                scalar1=SEQ_INV, scalar2=None, op0=ALU.mult), "v", ("hmvv", 1))

            k.wait(vector, "t", ("spvq",))
            k.inc(vector.tensor_tensor(vqi[:], p_ms[0:2, 0:64], sbvi2[:],
                                       op=ALU.add), "v", ("spvqi",))
            k.wait(vector, "s", ("spvqsq",))
            k.inc(vector.reduce_sum(vnorm[:], vqq[:], axis=AX.X), "v", ("spvnr",))
            k.wait(vector, "s", ("spvqn",))
            vector.tensor_scalar(vnorm[:], vnorm[:], scalar1=1e-12, scalar2=None,
                                 op0=ALU.max)
            k.inc(vector.reciprocal(vnorm[:], vnorm[:]), "v", ("spvrec",))
            k.wait(vector, "v", ("spvrec",))
            k.inc(vector.tensor_scalar(nvq[:], vqi[:], scalar1=vnorm[:],
                                       scalar2=None, op0=ALU.mult),
                  "v", ("spnvq",))
            k.wait(vector, "s", ("spcbsq",))
            for c in range(2):
                rs_cn = vector.reduce_sum(cnorm[:, c:c + 1],
                                          csq[:, c * 64:(c + 1) * 64], axis=AX.X)
            k.inc(rs_cn, "v", ("spcn",))
            k.wait(vector, "s", ("spcbn",))
            for c in range(2):
                vector.tensor_scalar(cnorm[:, c:c + 1], cnorm[:, c:c + 1],
                                     scalar1=1e-12, scalar2=None, op0=ALU.max)
            k.inc(vector.reciprocal(cnorm[:], cnorm[:]), "v", ("spcrec",))
            k.wait(vector, "v", ("spcrec",))
            for c in range(2):
                k.inc(vector.tensor_scalar(sncb[:, c * 64:(c + 1) * 64],
                                           scb[:, c * 64:(c + 1) * 64],
                                           scalar1=cnorm[:, c:c + 1],
                                           scalar2=None, op0=ALU.mult),
                      "v", ("spncb", c))
            for c in range(2):
                k.wait(vector, "t", ("spncbt", c))
                src = p_rstd if c == 0 else p_ms
                k.inc(vector.tensor_copy(ncbT[:, c * 128:(c + 1) * 128],
                                         src[0:64, 0:128]), "v", ("spncbtev", c))
            k.wait(vector, "t", ("spnvqt",))
            k.inc(vector.tensor_copy(nvqT[:], p_lie[0][0:64, 0:2]),
                  "v", ("spnvqte",))
            k.wait(vector, "t", ("spsc",))
            vector.tensor_copy(scoresR[:], p_rstd[0:2, 0:256])
            k.inc(vector.reduce_max(mxr[:], scoresR[:], axis=AX.X), "v", ("spmx",))
            k.wait(vector, "t", ("spmxt",))
            k.inc(vector.tensor_copy(mxT[:], p_ms[0:1, 0:2]), "v", ("spmxte",))
            k.wait(vector, "t", ("spsct", 1))
            for c in range(2):
                vector.tensor_copy(scT[:, c * 2:c * 2 + 2],
                                   p_lie[0][:, c * 2:c * 2 + 2])
            k.wait(vector, "t", ("spmb",))
            for c in range(2):
                k.inc(vector.tensor_tensor(soh[:, c * 2:c * 2 + 2],
                                           scT[:, c * 2:c * 2 + 2],
                                           p_rstd[:, 0:2], op=ALU.is_ge),
                      "v", ("spoh", c))
            k.wait(vector, "t", ("spvc",))
            k.inc(vector.tensor_copy(vqcT[:], p_ms[0:64, 0:2]), "v", ("spvce",))
            for h in range(4):
                k.wait(vector, "t", ("om", h, NE - 1))
                for q in range(2):
                    if (h, q) != (0, 0):
                        pq = (h, q - 1) if q == 1 else (h - 1, 1)
                        k.wait(vector, "dow0", ("omw",) + pq)
                    k.inc(vector.tensor_copy(outb[0][0:2, :], p_lie[q][0:2, :]),
                          "v", ("omev", h, q))
            for s_ in range(2):
                k.wait(vector, "drow", ("bvo", s_))
                k.wait(vector, "t", ("spvcr", s_))
                vector.tensor_tensor(bvoh[:], bvoh[:], p_lie[s_][0:2, :],
                                     op=ALU.add)
                vector.tensor_scalar(bvoh[:], bvoh[:], scalar1=smix[0:2, 1:2],
                                     scalar2=None, op0=ALU.mult)
                vector.tensor_scalar(booh[:], booh[:], scalar1=smix[0:2, 0:1],
                                     scalar2=None, op0=ALU.mult)
                k.inc(vector.tensor_tensor(booh[:], booh[:], bvoh[:],
                                           op=ALU.add), "v", ("spc2", s_))
                for e4 in range(4):
                    e = s_ * 4 + e4
                    k.wait(vector, "t", ("spct", e))
                    src = p_ms if e % 2 == 0 else p_rstd
                    k.inc(vector.tensor_copy(constPP[:, e * 2:e * 2 + 2],
                                             src[:, 0:2]), "v", ("spcpp", e))
            for b in range(NB):
                k.wait(vector, "dor", ("omr", b))
                k.inc(vector.tensor_tensor(omu[:, b * 64:(b + 1) * 64],
                                           omu[:, b * 64:(b + 1) * 64],
                                           sbtriu[:], op=ALU.add),
                      "v", ("spomu", b))
                k.wait(vector, "t", ("sptr", b))
                k.inc(vector.tensor_tensor(omg[:, b * 64:(b + 1) * 64],
                                           omu[:, b * 64:(b + 1) * 64],
                                           p_bc[0][0:64, 0:64], op=ALU.subtract),
                      "v", ("spomb", b))
                k.wait(vector, "t", ("spm2", b))
                k.inc(vector.tensor_copy(m2s[:, b * 64:(b + 1) * 64],
                                         p_bc[1][0:64, 0:64]), "v", ("spm2e", b))
                k.wait(vector, "v", ("spm2e", b))
                k.wait(vector, "t", ("spm3", b))
                k.inc(vector.tensor_copy(m3s[:, b * 64:(b + 1) * 64],
                                         p_bc[0][0:64, 0:64]), "v", ("spm3e", b))
                k.wait(vector, "v", ("spm3e", b))
                if b == 1:
                    k.wait(vector, "t", ("gbt", 0))
                bs = slice(b * 64, (b + 1) * 64)
                vector.tensor_scalar(gtmp[:], m2s[:, bs], scalar1=-0.5,
                                     scalar2=None, op0=ALU.mult)
                vector.tensor_tensor(gtmp[:], gtmp[:], omg[:, bs], op=ALU.add)
                vector.tensor_scalar(m3s[:, bs], m3s[:, bs], scalar1=1.0 / 6.0,
                                     scalar2=None, op0=ALU.mult)
                vector.tensor_tensor(gtmp[:], gtmp[:], m3s[:, bs], op=ALU.add)
                k.inc(vector.tensor_tensor(gtmp[:], gtmp[:],
                                           sident[0:64, 0:64], op=ALU.add),
                      "v", ("spG", b))
                k.wait(vector, "t", ("gbt", b))
                k.inc(vector.tensor_copy(GbT[:, bs], p_bc[1][0:64, 0:64]),
                      "v", ("gbte", b))
                k.wait(vector, "t", ("wliem", b))
                vector.tensor_copy(wlie[:, b * EMB: b * EMB + T],
                                   p_bc[0][0:64, :])
                k.inc(vector.tensor_copy(wlie[:, b * EMB + T: (b + 1) * EMB],
                                         p_bc[1][0:64, :]), "v", ("wlie", b))

            for t in range(NT):
                b = t // TPB
                for e in range(NE):
                    k.wait(vector, "t", ("lie", t, e))
                    if t == 0 and e == 0:
                        k.wait(vector, "v", ("spcpp", 7))
                    if t >= 1:
                        k.wait(vector, sg_sem(e), ("sg", t - 1, e))
                        k.wait(vector, "s", ("sq", t - 1, e))
                    k.inc(vector.scalar_tensor_tensor(
                        out=a_all[:, e * T:(e + 1) * T], in0=p_lie[e % 2][:],
                        scalar=constPP[:, e * 2 + b:e * 2 + b + 1],
                        in1=xnt[t % 2][:, e * T:(e + 1) * T],
                        op0=ALU.add, op1=ALU.add), "v", ("a", t, e))
                k.wait(vector, "s", ("sqrt", t))
                if t >= 1:
                    k.wait(vector, "t", ("rstdbc", t - 1))
                k.inc(vector.reciprocal(rstd_row[:], rms_row[:]),
                      "v", ("recip", t))
                for e2 in range(0, NE, 2):
                    k.wait(vector, "s", ("silu", t, e2))
                    k.wait(vector, "s", ("rstdev", t))
                    k.wait(vector, f"dxr{e2 % 2}", ("xres", t, e2))
                    if (t, e2) > (0, 1):
                        pe = e2 - 2
                        pt = t
                        if pe < 0:
                            pe += 8
                            pt -= 1
                        k.wait(vector, f"do{pe % 2}", ("od", pt, pe))
                    vector.tensor_tensor(outb[e2 % 2][:],
                                         gtb[:, e2 * T:(e2 + 1) * T],
                                         rstdB[:], op=ALU.mult)
                    vector.tensor_tensor(outb[e2 % 2][:], outb[e2 % 2][:],
                                         a_all[:, e2 * T:(e2 + 1) * T],
                                         op=ALU.mult)
                    k.inc(vector.tensor_tensor(outb[e2 % 2][:], outb[e2 % 2][:],
                                               xresb[e2 % 2][:], op=ALU.add),
                          "v", ("sg", t, e2))
                if t + 2 < NT:
                    t2 = t + 2
                    for e in range(0, NE, 2):
                        k.wait(vector, f"dxl{e % 2}", ("xln", t2, e))
                        k.wait(vector, "s", ("bcev", t2))
                        if e == 0:
                            k.wait(vector, "do0", ("od", t, 6))
                        vector.tensor_tensor(outb[0][:], xlnb[e % 2][:],
                                             mrB[t2 % 2][:], op=ALU.subtract)
                        k.inc(vector.tensor_tensor(
                            xnt[t2 % 2][:, e * T:(e + 1) * T], outb[0][:],
                            rsB[t2 % 2][:], op=ALU.mult), "v", ("xn", t2, e))
        k.body(blk, "vector", vector_body)

        def scalar_body(scalar):
            k.wait(scalar, "dcs", ("call",))

            def stats_acc(b, i):
                u = ("xt", (b, i))
                k.wait(scalar, f"dq{USE_POS[u] % NPOOL}", u)
                if b == 1 and i == 0:
                    k.wait(scalar, "v", ("hmev", 0))
                scr = mposb[0][:] if b == 0 else xnt[1][:, 0:CH]
                scalar.activation(scr, pool[POOLBUF[u]][:], AF.Copy,
                                  scale=LN_INV, accum_out=mcols[:, i:i + 1])
                scalar.activation(scr, pool[POOLBUF[u]][:],
                                  AF.Square, scale=SQ_SCALE,
                                  accum_out=sqcols[:, i:i + 1])
                k.inc(scalar.square(varcols[:, i:i + 1], mcols[:, i:i + 1]),
                      "s", ("acc", b, i))

            def stats_sqv(b, i):
                k.wait(scalar, "v", ("var", b, i))
                k.inc(scalar.activation(varcols[:, i:i + 1], varcols[:, i:i + 1],
                                        AF.Sqrt, bias=scpool[:, 1:2]),
                      "s", ("sqv", b, i))

            for i in range(16):
                stats_acc(0, i)
                if i >= 1:
                    stats_sqv(0, i - 1)
            stats_sqv(0, 15)
            k.wait(scalar, "t", ("sttr", 0))
            scalar.copy(mrT[:], p_bc[0][0:16, 0:128])
            k.inc(scalar.copy(rsT[:], p_bc[1][0:16, 0:128]), "s", ("stev", 0))
            k.wait(scalar, "drow", ("woo",))
            k.inc(scalar.activation(woos[:], s64row[0:64, :], AF.Copy,
                                    scale=smix[0:64, 0:1]), "s", ("woos",))
            k.wait(scalar, "t", ("bc", 0))
            scalar.copy(mrB[0][:], p_bc[0][:])
            k.inc(scalar.copy(rsB[0][:], p_bc[1][:]), "s", ("bcev", 0))
            def wsigns(w, clist):
                for c in clist:
                    if w == 0:
                        stream, idx = "w1b", (c // NE, c % NE)
                    else:
                        stream, idx = "w2b", c
                    u = (stream, idx)
                    k.wait(scalar, f"dq{USE_POS[u] % NPOOL}", u)
                    if c == 0:
                        k.wait(scalar, "v", ("thrc",))
                    if c >= 2:
                        sem = "v" if (c - 2) % 2 == 0 else "p"
                        pk = ("w1t8", (c - 2) // NE, (c - 2) % NE) if w == 0 \
                            else ("w2t8", c - 2)
                        k.wait(scalar, sem, pk)
                    scalar.activation(mposb[c % 2][:], pool[POOLBUF[u]][:],
                                      AF.Sign, bias=nthrcol[:, w:w + 1])
                    k.inc(scalar.activation(mneg8[c % 2][:], pool[POOLBUF[u]][:],
                                            AF.Sign, bias=thrcol[:, w:w + 1]),
                          "s", ("sgn2", w, c))

            wsigns(0, list(range(16)))
            for f in range(NF):
                k.wait(scalar, "t", ("mm1", 0, f))
                k.inc(scalar.activation(hT[:, f * T:(f + 1) * T], p_mm[f % 2][:],
                                        AF.Gelu, bias=sb1[:, f:f + 1]),
                      "s", ("gelu", 0, f))
                if f < 16:
                    wsigns(0, [16 + f])
            k.wait(scalar, "t", ("bc", 1))
            k.wait(scalar, "v", ("xn", 0, 4))
            k.wait(scalar, "v", ("xn", 0, 5))
            scalar.copy(mrB[1][:], p_bc[0][:])
            k.inc(scalar.copy(rsB[1][:], p_bc[1][:]), "s", ("bcev", 1))
            wsigns(1, list(range(32)))
            for e2 in range(NE):
                for i in (2 * e2, 2 * e2 + 1):
                    stats_acc(1, i)
                    if i >= 1:
                        stats_sqv(1, i - 1)
                k.wait(scalar, "t", ("mm2", 0, e2))
                scalar.activation(gtb[:, e2 * T:(e2 + 1) * T],
                                  p_mm[e2 % 2][:], AF.Silu,
                                  bias=sb2[:, e2:e2 + 1])
                k.inc(scalar.activation(gtb[:, e2 * T:(e2 + 1) * T],
                                        gtb[:, e2 * T:(e2 + 1) * T],
                                        AF.Copy, scale=snw[:, e2:e2 + 1]),
                      "s", ("silu", 0, e2))
            stats_sqv(1, 15)
            k.wait(scalar, "t", ("sttr", 1))
            scalar.copy(mrT[:], p_ms[0:16, 0:128])
            k.inc(scalar.copy(rsT[:], p_rstd[0:16, 0:128]), "s", ("stev", 1))
            k.wait(scalar, "v", ("spvqi",))
            k.inc(scalar.square(vqq[:], vqi[:]), "s", ("spvqsq",))
            k.inc(scalar.square(csq[:], scb[:]), "s", ("spcbsq",))
            k.wait(scalar, "v", ("spvnr",))
            k.inc(scalar.activation(vnorm[:], vnorm[:], AF.Sqrt,
                                    bias=scpool[0:2, 2:3]), "s", ("spvqn",))
            k.wait(scalar, "v", ("spcn",))
            k.inc(scalar.activation(cnorm[:], cnorm[:], AF.Sqrt,
                                    bias=scpool[:, 2:3]), "s", ("spcbn",))

            def sq_t(t):
                for e in range(NE):
                    k.wait(scalar, "v", ("a", t, e))
                    if e >= 2:
                        k.wait(scalar, "t", ("ms", t, e - 2))
                    elif t > 0:
                        k.wait(scalar, "t", ("ms", t - 1, e + 6))
                    k.inc(scalar.square(sqb[e % 2][:],
                                        a_all[:, e * T:(e + 1) * T]),
                          "s", ("sq", t, e))

            def sqrt_t(t):
                k.wait(scalar, "t", ("ms", t, NE - 1))
                if t > 0:
                    k.wait(scalar, "v", ("recip", t - 1))
                k.inc(scalar.activation(rms_row[:], p_ms[0:1, :], AF.Sqrt,
                                        bias=scpool[0:1, 0:1], scale=LN_INV),
                      "s", ("sqrt", t))

            def rstdev_t(t):
                k.wait(scalar, "t", ("rstdbc", t))
                if t >= 1:
                    k.wait(scalar, "v", ("sg", t - 1, 6))
                    k.wait(scalar, "p", ("sg", t - 1, 7))
                k.inc(scalar.copy(rstdB[:], p_rstd[:]), "s", ("rstdev", t))

            def silu_t(t):
                for e2 in range(NE):
                    k.wait(scalar, "t", ("mm2", t, e2))
                    k.wait(scalar, sg_sem(e2), ("sg", t - 1, e2))
                    scalar.activation(gtb[:, e2 * T:(e2 + 1) * T],
                                      p_mm[e2 % 2][:], AF.Silu,
                                      bias=sb2[:, e2:e2 + 1])
                    k.inc(scalar.activation(gtb[:, e2 * T:(e2 + 1) * T],
                                            gtb[:, e2 * T:(e2 + 1) * T],
                                            AF.Copy, scale=snw[:, e2:e2 + 1]),
                          "s", ("silu", t, e2))

            def gelu_t(t):
                for f in range(NF):
                    k.wait(scalar, "t", ("mm1", t, f))
                    if f == 0:
                        k.wait(scalar, "t", ("mm2", t - 1, 7))
                    k.inc(scalar.activation(hT[:, f * T:(f + 1) * T],
                                            p_mm[f % 2][:], AF.Gelu,
                                            bias=sb1[:, f:f + 1]),
                          "s", ("gelu", t, f))

            def bcev_t(t):
                k.wait(scalar, "t", ("bc", t))
                if t >= 2:
                    k.wait(scalar, xn_sem(t - 2, 6), ("xn", t - 2, 6))
                    k.wait(scalar, xn_sem(t - 2, 7), ("xn", t - 2, 7))
                scalar.copy(mrB[t % 2][:], p_bc[0][:])
                k.inc(scalar.copy(rsB[t % 2][:], p_bc[1][:]), "s", ("bcev", t))

            gelu_t(1)
            bcev_t(2)
            sq_t(0)
            sqrt_t(0)
            rstdev_t(0)
            silu_t(1)
            sq_t(1)
            sqrt_t(1)
            rstdev_t(1)
            for t in range(2, NT):
                gelu_t(t)
                if t + 1 < NT:
                    bcev_t(t + 1)
                sq_t(t)
                sqrt_t(t)
                rstdev_t(t)
                silu_t(t)
        k.body(blk, "scalar", scalar_body)

        def tensor_body(tensor):
            k.wait(tensor, "dcs", ("call",))
            k.wait(tensor, "v", ("thrc0", 0))
            k.inc(tensor.transpose(p_bc[0][0:1, 0:128], thrcomb[:, 0:1],
                                   sident[:]), "t", ("thrtr",))
            k.wait(tensor, "v", ("thrc0", 1))
            k.inc(tensor.transpose(p_bc[1][0:1, 0:128], thrcomb[:, 1:2],
                                   sident[:]), "t", ("thrtr2",))

            def hm_i(b, i):
                hb = [p_mm[0], p_mm[1]] if b == 0 else [p_lie[0], p_lie[1]]
                u = ("xt", (b, i))
                k.wait(tensor, f"dq{USE_POS[u] % NPOOL}", u)
                k.wait(tensor, "v", ("rsc", b, i))
                for q in range(2):
                    mm = tensor.matmul(hb[q][0:1, :],
                                       lhsT=rscols[:, i:i + 1],
                                       rhs=pool[POOLBUF[u]][:, q * T:(q + 1) * T],
                                       start=(i == 0), stop=(i == 15))
                k.inc(mm, "t", ("hm", b, i))

            def smr_b(b):
                k.wait(tensor, "v", ("mrsum", b))
                tensor.matmul(p_ms[0:1, 20:21], lhsT=sonescf[:],
                              rhs=thrpart[:, 0:1], start=True, stop=True)
                mm = tensor.matmul(p_ms[0:1, 22:23], lhsT=sonescf[:],
                                   rhs=thrpart[:, 0:1], start=True, stop=True)
                k.inc(mm, "t", ("smr", b))

            def stats_tr(b):
                k.wait(tensor, "v", ("rsc", b, 15))
                tr = [p_bc[0], p_bc[1]] if b == 0 else [p_ms, p_rstd]
                if b == 0:
                    k.wait(tensor, "v", ("ts0",))
                    k.wait(tensor, "v", ("thrsums",))
                else:
                    k.wait(tensor, "v", ("hmvv", 0))
                    k.wait(tensor, "v", ("smre", 1))
                k.inc(tensor.transpose(tr[0][0:16, 0:128], mrcols[:], sident[:]),
                      "t", ("sttr0", b))
                k.inc(tensor.transpose(tr[1][0:16, 0:128], rscols[:], sident[:]),
                      "t", ("sttr", b))

            def stats_b(b):
                for i in range(16):
                    hm_i(b, i)
                smr_b(b)
                stats_tr(b)

            def hmv_b(b):
                k.wait(tensor, "v", ("hmev", b))
                if b == 1:
                    k.wait(tensor, "s", ("stev", 1))
                for e in range(NE):
                    hsrc = rms_row if e < 4 else rstd_row
                    mm = tensor.matmul(p_ms[:, (2 * e + b):(2 * e + b + 1)],
                                       lhsT=hsrc[0:1, (e % 4) * 128:(e % 4 + 1) * 128],
                                       rhs=sonesr[0:1, 0:1], start=True,
                                       stop=True)
                k.inc(mm, "t", ("hmv", b))

            stats_b(0)
            hmv_b(0)
            k.wait(tensor, "v", ("thrrow",))
            k.wait(tensor, "s", ("stev", 0))
            k.inc(tensor.matmul(p_bc[0][:, 0:2], lhsT=sonesr[:], rhs=thrrow[:],
                                start=True, stop=True), "t", ("thrbc",))

            def bc_t(t):
                k.wait(tensor, "dst0", ("stm", t))
                k.wait(tensor, "dst1", ("str", t))
                if t == 0:
                    k.wait(tensor, "v", ("thrc",))
                if t == 2:
                    k.wait(tensor, "v", ("wlie", 1))
                if t > 0:
                    for e in range(NE):
                        k.wait(tensor, xn_sem(t - 1, e), ("xn", t - 1, e))
                tensor.matmul(p_bc[0][:], lhsT=sonesr[:], rhs=mrbuf[0][:],
                              start=True, stop=True)
                mm = tensor.matmul(p_bc[1][:], lhsT=sonesr[:], rhs=rsbuf[0][:],
                                   start=True, stop=True)
                k.inc(mm, "t", ("bc", t))

            def vq_path():
                k.wait(tensor, "v", ("hmvv", 1))
                k.wait(tensor, "drow", ("wvi",))
                for e in range(NE):
                    mm = tensor.matmul(p_ms[0:2, 0:64],
                                       lhsT=hmv[:, 2 * e:2 * e + 2],
                                       rhs=s64row[:, e * 64:(e + 1) * 64],
                                       start=(e == 0), stop=(e == 7))
                k.inc(mm, "t", ("spvq",))
                for c in range(2):
                    k.wait(tensor, "v", ("spncb", c))
                    dst = p_rstd if c == 0 else p_ms
                    if c == 1:
                        k.wait(tensor, "v", ("spncbtev", 0))
                    k.inc(tensor.transpose(dst[0:64, 0:128],
                                           sncb[:, c * 64:(c + 1) * 64],
                                           sident[:]), "t", ("spncbt", c))
                k.wait(tensor, "v", ("spnvq",))
                k.wait(tensor, "v", ("hmev", 1))
                k.inc(tensor.transpose(p_lie[0][0:64, 0:2], nvq[:],
                                       sident[0:2, 0:2]), "t", ("spnvqt",))
                k.wait(tensor, "v", ("spncbtev", 1))
                k.wait(tensor, "v", ("spnvqte",))
                k.inc(tensor.matmul(p_rstd[0:2, 0:256], lhsT=nvqT[:], rhs=ncbT[:],
                                    start=True, stop=True), "t", ("spsc",))
                k.wait(tensor, "v", ("spmx",))
                k.inc(tensor.transpose(p_ms[0:1, 0:2], mxr[:],
                                       sident[0:2, 0:2]), "t", ("spmxt",))
                k.wait(tensor, "v", ("spmxte",))
                k.inc(tensor.matmul(p_rstd[:, 0:2], lhsT=sonesr[:], rhs=mxT[:],
                                    start=True, stop=True), "t", ("spmb",))
                for c in range(2):
                    k.inc(tensor.transpose(p_lie[0][:, c * 2:c * 2 + 2],
                                           scoresR[:, c * 128:(c + 1) * 128],
                                           sident[0:2, 0:2]), "t", ("spsct", c))
                k.wait(tensor, "v", ("spoh", 1))
                for c in range(2):
                    mm = tensor.matmul(p_ms[0:64, 0:2],
                                       lhsT=scb[:, c * 64:(c + 1) * 64],
                                       rhs=soh[:, c * 2:c * 2 + 2],
                                       start=(c == 0), stop=(c == 1))
                k.inc(mm, "t", ("spvc",))

            def omega_h(h):
                for e in range(NE):
                    k.wait(tensor, f"dq{USE_POS[('wtr', (h, e))] % NPOOL}",
                           ("wtr", (h, e)))
                    if e == 0:
                        k.wait(tensor, "v", ("hmvv", 1))
                        k.wait(tensor, "v", ("hmev", 1))
                        if h == 0:
                            k.wait(tensor, "v", ("spoh", 0))
                            k.wait(tensor, "v", ("spnvqte",))
                        else:
                            k.wait(tensor, "v", ("omev", h - 1, 1))
                    buf = pool[POOLBUF[("wtr", (h, e))]]
                    for q in range(2):
                        mm = tensor.matmul(p_lie[q][0:2, :],
                                           lhsT=hmv[:, 2 * e:2 * e + 2],
                                           rhs=buf[:, q * T:(q + 1) * T],
                                           start=(e == 0), stop=(e == 7))
                    k.inc(mm, "t", ("om", h, e))

            def vq_rows():
                k.wait(tensor, "v", ("spvce",))
                k.wait(tensor, "drow", ("wvo",))
                k.wait(tensor, "v", ("omev", 3, 1))
                for s_ in range(2):
                    k.inc(tensor.matmul(p_lie[s_][0:2, :], lhsT=vqcT[:],
                                        rhs=s64row[0:64, s_ * T:(s_ + 1) * T],
                                        start=True, stop=True),
                          "t", ("spvcr", s_))
                for s_ in range(2):
                    k.wait(tensor, "v", ("spc2", s_))
                    for e4 in range(4):
                        e = s_ * 4 + e4
                        dst = p_ms if e % 2 == 0 else p_rstd
                        if e >= 2:
                            k.wait(tensor, "v", ("spcpp", e - 2))
                        k.inc(tensor.transpose(dst[:, 0:2],
                                               booh[:, e4 * 128:(e4 + 1) * 128],
                                               sident[0:2, 0:2]),
                              "t", ("spct", e))

            def g_pe(b):
                k.wait(tensor, "v", ("spomu", b))
                k.wait(tensor, "s", ("bcev", 1))
                k.inc(tensor.transpose(p_bc[0][0:64, 0:64],
                                       omu[:, b * 64:(b + 1) * 64],
                                       sident[0:64, 0:64]), "t", ("sptr", b))
                k.wait(tensor, "v", ("spomb", b))
                k.inc(tensor.matmul(p_bc[1][0:64, 0:64],
                                    lhsT=omg[:, b * 64:(b + 1) * 64],
                                    rhs=omg[:, b * 64:(b + 1) * 64],
                                    start=True, stop=True), "t", ("spm2", b))
                k.wait(tensor, "v", ("spm2e", b))
                k.inc(tensor.matmul(p_bc[0][0:64, 0:64],
                                    lhsT=omg[:, b * 64:(b + 1) * 64],
                                    rhs=m2s[:, b * 64:(b + 1) * 64],
                                    start=True, stop=True), "t", ("spm3", b))
                k.wait(tensor, "v", ("spG", b))
                k.inc(tensor.transpose(p_bc[1][0:64, 0:64], gtmp[:],
                                       sident[0:64, 0:64]), "t", ("gbt", b))
                k.wait(tensor, "v", ("gbte", b))
                k.wait(tensor, "s", ("woos",))
                for q in range(2):
                    mm = tensor.matmul(p_bc[q][0:64, :],
                                       lhsT=GbT[:, b * 64:(b + 1) * 64],
                                       rhs=woos[:, q * T:(q + 1) * T],
                                       start=True, stop=True)
                k.inc(mm, "t", ("wliem", b))

            def lie_t(t):
                b = t // TPB
                for e in range(NE):
                    if e == 0:
                        k.wait(tensor, "v", ("wlie", b))
                        k.wait(tensor, xn_sem(t, 0), ("xn", t, 0))
                        if t == 0:
                            k.wait(tensor, "v", ("spcpp", 7))
                        if t > 0:
                            k.wait(tensor, "v", ("a", t - 1, 6))
                    elif e == 1 and t > 0:
                        k.wait(tensor, "v", ("a", t - 1, 7))
                    elif e >= 2:
                        k.wait(tensor, "v", ("a", t, e - 2))
                    k.inc(tensor.matmul(
                        p_lie[e % 2][:],
                        lhsT=wlie[:, b * EMB + e * 128: b * EMB + (e + 1) * 128],
                        rhs=xnt[t % 2][0:64, 0:T], start=True, stop=True),
                          "t", ("lie", t, e))

            def ms_t(t):
                for e in range(NE):
                    k.wait(tensor, "s", ("sq", t, e))
                    if e == 0:
                        if t > 0:
                            k.wait(tensor, "s", ("sqrt", t - 1))
                        else:
                            k.wait(tensor, "v", ("spcpp", 6))
                    k.inc(tensor.matmul(p_ms[0:1, 0:T], lhsT=sonescb[:],
                                        rhs=sqb[e % 2][:],
                                        start=(e == 0), stop=(e == NE - 1)),
                          "t", ("ms", t, e))

            def rstd_t(t):
                k.wait(tensor, "v", ("recip", t))
                if t > 0:
                    k.wait(tensor, "s", ("rstdev", t - 1))
                else:
                    k.wait(tensor, "v", ("spcpp", 7))
                k.inc(tensor.matmul(p_rstd[:], lhsT=sonesr[:], rhs=rstd_row[:],
                                    start=True, stop=True), "t", ("rstdbc", t))

            AUX = {}

            def add_aux(key, *fns):
                AUX.setdefault(key, []).extend(fns)

            def run_aux(key):
                for fn in AUX.get(key, []):
                    fn()

            for g_ in range(6):
                def mk(g=g_):
                    def f():
                        hm_i(1, 2 * g)
                        hm_i(1, 2 * g + 1)
                    return f
                add_aux(("e2", 0, g_ + 2), mk())
            def hm_tail():
                for i in range(12, 16):
                    hm_i(1, i)
            add_aux(("f", 1, 0), hm_tail, lambda: smr_b(1),
                    lambda: stats_tr(1))
            add_aux(("f", 1, 1), lambda: hmv_b(1))
            add_aux(("f", 1, 2), vq_path)
            add_aux(("f", 1, 4), lambda: omega_h(0))
            add_aux(("f", 1, 8), lambda: omega_h(1))
            add_aux(("f", 1, 12), lambda: omega_h(2))
            add_aux(("f", 1, 16), lambda: omega_h(3))
            add_aux(("f", 1, 20), vq_rows)
            add_aux(("f", 1, 24), lambda: g_pe(0))
            add_aux(("f", 1, 28), lambda: g_pe(1))
            add_aux(("e2", 1, 0), lambda: lie_t(0))
            add_aux(("e2", 1, 2), lambda: ms_t(0), lambda: rstd_t(0))
            add_aux(("e2", 1, 4), lambda: lie_t(1))
            add_aux(("f", 2, 0), lambda: ms_t(1), lambda: rstd_t(1))

            bc_t(0)
            for t in range(NT):
                for f in range(NF):
                    run_aux(("f", t, f))
                    if t == 0 and f % 8 == 0:
                        k.wait(tensor, "v", ("w1t8", f // 8, NE - 2))
                        k.wait(tensor, "p", ("w1t8", f // 8, NE - 1))
                    if f == 0:
                        for e in range(NE):
                            k.wait(tensor, xn_sem(t, e), ("xn", t, e))
                        if t == 0:
                            k.wait(tensor, "v", ("hmev", 0))
                        if t > 0:
                            k.wait(tensor, "s", ("silu", t - 1, 6))
                    elif f == 1:
                        if t == 0:
                            k.wait(tensor, "v", ("hmev", 0))
                        if t > 0:
                            k.wait(tensor, "s", ("silu", t - 1, 7))
                    else:
                        k.wait(tensor, "s", ("gelu", t, f - 2))
                    for e in range(NE):
                        mm = tensor.matmul(
                            p_mm[f % 2][:],
                            lhsT=w1t8[:, e * FF + f * 128: e * FF + (f + 1) * 128],
                            rhs=xnt[t % 2][:, e * T:(e + 1) * T],
                            start=(e == 0), stop=(e == NE - 1))
                    k.inc(mm, "t", ("mm1", t, f))
                run_aux(("f", t, NF))
                if t + 1 < NT:
                    bc_t(t + 1)
                if t >= 2:
                    lie_t(t)

                def mm2_group(e2):
                    run_aux(("e2", t, e2))
                    if e2 == 0:
                        k.wait(tensor, "s", ("gelu", t, NF - 1))
                        if t == 0:
                            k.wait(tensor, "v", ("w2t8", NF - 2))
                            k.wait(tensor, "p", ("w2t8", NF - 1))
                    elif e2 >= 2:
                        k.wait(tensor, "s", ("silu", t, e2 - 2))
                    for f in range(NF):
                        mm = tensor.matmul(
                            p_mm[e2 % 2][:],
                            lhsT=w2t8[:, f * EMB + e2 * 128: f * EMB + (e2 + 1) * 128],
                            rhs=hT[:, f * T:(f + 1) * T],
                            start=(f == 0), stop=(f == NF - 1))
                    k.inc(mm, "t", ("mm2", t, e2))

                if t >= 2:
                    mm2_group(0)
                    mm2_group(1)
                    ms_t(t)
                    rstd_t(t)
                    for e2 in range(2, NE):
                        mm2_group(e2)
                else:
                    for e2 in range(NE):
                        mm2_group(e2)
        k.body(blk, "tensor", tensor_body)


def build_program():
    nc = bass.Bass("TRN2", target_bir_lowering=False, debug=False)
    io = {}
    io["xTd"] = nc.dram_tensor("xTd", [NB * EMB, SEQ], F32, kind="ExternalInput").ap()
    io["xTokd"] = nc.dram_tensor("xTokd", [NB * SEQ, EMB], F32, kind="ExternalInput").ap()
    io["w1td"] = nc.dram_tensor("w1td", [EMB, FF], F32, kind="ExternalInput").ap()
    io["w2td"] = nc.dram_tensor("w2td", [FF, EMB], F32, kind="ExternalInput").ap()
    io["w1pred"] = nc.dram_tensor("w1pred", [128, FF], F32, kind="ExternalInput").ap()
    io["w2pred"] = nc.dram_tensor("w2pred", [512, EMB], F32, kind="ExternalInput").ap()
    io["wtriud"] = nc.dram_tensor("wtriud", [EMB, FF], F32, kind="ExternalInput").ap()
    io["btriud"] = nc.dram_tensor("btriud", [64, 64], F32, kind="ExternalInput").ap()
    io["wooTd"] = nc.dram_tensor("wooTd", [64, EMB], F32, kind="ExternalInput").ap()
    io["wviTd"] = nc.dram_tensor("wviTd", [128, 512], F32, kind="ExternalInput").ap()
    io["wvoTd"] = nc.dram_tensor("wvoTd", [64, EMB], F32, kind="ExternalInput").ap()
    io["cbd"] = nc.dram_tensor("cbd", [128, 128], F32, kind="ExternalInput").ap()
    io["b1d"] = nc.dram_tensor("b1d", [128, 32], F32, kind="ExternalInput").ap()
    io["b2d"] = nc.dram_tensor("b2d", [128, 8], F32, kind="ExternalInput").ap()
    io["nwd"] = nc.dram_tensor("nwd", [128, 8], F32, kind="ExternalInput").ap()
    io["bvi2d"] = nc.dram_tensor("bvi2d", [2, 64], F32, kind="ExternalInput").ap()
    io["bvo2d"] = nc.dram_tensor("bvo2d", [2, EMB], F32, kind="ExternalInput").ap()
    io["boo2d"] = nc.dram_tensor("boo2d", [2, EMB], F32, kind="ExternalInput").ap()
    io["mixd"] = nc.dram_tensor("mixd", [128, 2], F32, kind="ExternalInput").ap()
    io["identd"] = nc.dram_tensor("identd", [128, 128], F32, kind="ExternalInput").ap()
    io["onesrd"] = nc.dram_tensor("onesrd", [1, 128], F32, kind="ExternalInput").ap()
    io["onescbd"] = nc.dram_tensor("onescbd", [128, 1], BF16, kind="ExternalInput").ap()
    io["onescfd"] = nc.dram_tensor("onescfd", [128, 1], F32, kind="ExternalInput").ap()
    io["cpoold"] = nc.dram_tensor("cpoold", [128, 4], F32, kind="ExternalInput").ap()
    io["omdram"] = nc.dram_tensor("omdram", [NB, 4096], F32, kind="Internal").ap()
    io["stats_d"] = nc.dram_tensor("stats_d", [4, SEQ], F32, kind="Internal").ap()
    io["ccin"] = nc.dram_tensor("ccin", [1, 2], F32, kind="Internal").ap()
    io["ccout"] = nc.dram_tensor("ccout", [1, 2], F32, kind="Internal").ap()
    io["outTd"] = nc.dram_tensor("outTd", [NB * EMB, SEQ], F32, kind="ExternalOutput").ap()

    k1 = K(False)
    construct(k1, {n: Fake() for n in io})
    k2 = K(True, nc, M=k1.M)
    with k2.es:
        construct(k2, io)
    return nc


def kernel(L0_attn_x, L0_attn_omega_proj_w, L0_attn_omega_proj_b, L0_attn_omega_out_w,
           L0_attn_omega_out_b, L0_attn_vq_proj_in_w, L0_attn_vq_proj_in_b,
           L0_attn_vq_proj_out_w, L0_attn_vq_proj_out_b, L0_attn_codebook,
           L0_attn_omega_mix, L0_attn_vq_mix, L0_attn_norm_w, L0_ffn_W1, L0_ffn_b1,
           L0_ffn_W2, L0_ffn_b2):
    f32 = np.float32
    x = np.asarray(L0_attn_x, f32)
    xT = np.ascontiguousarray(x.transpose(0, 2, 1))
    w1t = np.ascontiguousarray(np.asarray(L0_ffn_W1, f32).T)
    w2t = np.ascontiguousarray(np.asarray(L0_ffn_W2, f32).T)

    opw = np.asarray(L0_attn_omega_proj_w, f32)
    opb = np.asarray(L0_attn_omega_proj_b, f32)
    R, C = np.triu_indices(64, k=1)
    wtriu = np.zeros((EMB, 64, 64), f32)
    wtriu[:, R, C] = opw.T
    wtriu = np.ascontiguousarray(wtriu.reshape(EMB, FF))
    btriu = np.zeros((64, 64), f32)
    btriu[R, C] = opb

    wooT = np.ascontiguousarray(np.asarray(L0_attn_omega_out_w, f32).T)
    wviT = np.asarray(L0_attn_vq_proj_in_w, f32).T
    wviT = np.ascontiguousarray(wviT.reshape(8, 128, 64).transpose(1, 0, 2).reshape(128, 512))
    wvoT = np.ascontiguousarray(np.asarray(L0_attn_vq_proj_out_w, f32).T)
    cb = np.asarray(L0_attn_codebook, f32)
    cbPP = np.ascontiguousarray(cb.reshape(2, 128, 64).transpose(1, 0, 2).reshape(128, 128))
    b1PP = np.ascontiguousarray(np.asarray(L0_ffn_b1, f32).reshape(32, 128).T)
    b2PP = np.ascontiguousarray(np.asarray(L0_ffn_b2, f32).reshape(8, 128).T)
    nwPP = np.ascontiguousarray(np.asarray(L0_attn_norm_w, f32).reshape(8, 128).T)
    bvi2 = np.ascontiguousarray(np.broadcast_to(np.asarray(L0_attn_vq_proj_in_b, f32), (2, 64)))
    bvo2 = np.ascontiguousarray(np.broadcast_to(np.asarray(L0_attn_vq_proj_out_b, f32), (2, EMB)))
    boo2 = np.ascontiguousarray(np.broadcast_to(np.asarray(L0_attn_omega_out_b, f32), (2, EMB)))
    mixd = np.empty((128, 2), f32)
    mixd[:, 0] = np.float32(L0_attn_omega_mix)
    mixd[:, 1] = np.float32(L0_attn_vq_mix)
    identd = np.eye(128, dtype=f32)
    onesrd = np.ones((1, 128), f32)
    onescbd = np.ones((128, 1), ml_dtypes.bfloat16)
    onescfd = np.ones((128, 1), np.float32)
    cpoold = np.broadcast_to(np.array([1e-6, 1e-5, 0.0, 0.0], f32), (128, 4)).copy()

    shared = {
        "w1td": w1t, "w2td": w2t, "wtriud": wtriu, "btriud": btriu, "wooTd": wooT,
        "wviTd": wviT, "wvoTd": wvoT, "cbd": cbPP, "b1d": b1PP, "b2d": b2PP,
        "nwd": nwPP, "bvi2d": bvi2, "bvo2d": bvo2, "boo2d": boo2, "mixd": mixd,
        "identd": identd, "onesrd": onesrd, "onescbd": onescbd,
        "onescfd": onescfd, "cpoold": cpoold,
    }
    in_maps = []
    for c in range(NCORES):
        m = dict(shared)
        m["xTd"] = np.ascontiguousarray(xT[2 * c:2 * c + 2].reshape(NB * EMB, SEQ))
        m["xTokd"] = np.ascontiguousarray(x[2 * c:2 * c + 2].reshape(NB * SEQ, EMB))
        m["w1pred"] = np.ascontiguousarray(w1t[c * 128:(c + 1) * 128, :])
        m["w2pred"] = np.ascontiguousarray(w2t[c * 512:(c + 1) * 512, :])
        in_maps.append(m)

    nc = build_program()
    res = run_bass_kernel_spmd(nc, in_maps, core_ids=list(range(NCORES)))
    outs = []
    for c in range(NCORES):
        o = np.asarray(res.results[c]["outTd"], f32).reshape(NB, EMB, SEQ)
        outs.append(o.transpose(0, 2, 1))
    return np.ascontiguousarray(np.concatenate(outs, axis=0))


# version 11
# speedup vs baseline: 1.4925x; 1.1566x over previous
import numpy as np
import ml_dtypes
from contextlib import ExitStack, contextmanager

import concourse.bass as bass
import concourse.mybir as mybir
from concourse.bass_utils import run_bass_kernel_spmd
from concourse.alu_op_type import AluOpType as ALU

F32 = mybir.dt.float32
BF16 = mybir.dt.bfloat16
FP8 = mybir.dt.float8e4
AF = mybir.ActivationFunctionType
AX = mybir.AxisListType

EMB = 1024
FF = 4096
SEQ = 2048
NB = 2
NE = 8
NF = 32
T = 512
NT = 8
TPB = 4
NCORES = 8
CH = 1024
MEAN_W = float(1.0 / (FF * EMB))
LN_INV = float(1.0 / EMB)
SQ_SCALE = float(1.0 / 32.0)
SEQ_INV = float(1.0 / SEQ)
NPOOL = 3


class Fake:
    def __getattr__(self, n):
        return self

    def __call__(self, *a, **k):
        return self

    def __getitem__(self, i):
        return self

    def __enter__(self):
        return self

    def __exit__(self, *a):
        return False


class K:

    def __init__(self, emit, nc=None, M=None):
        self.emit = emit
        self.nc = nc
        self.cnt = {}
        self.M = M if M is not None else {}
        self.sems = {}
        self.es = ExitStack()

    def sem(self, name):
        self.cnt[name] = 0
        if self.emit:
            self.sems[name] = self.es.enter_context(self.nc.semaphore(name))

    def inc(self, instr, s, key=None, n=1):
        self.cnt[s] += n
        if key is not None:
            if not self.emit:
                self.M[(s,) + tuple(key)] = self.cnt[s]
        if self.emit and instr is not None:
            instr.then_inc(self.sems[s], n)

    def wait(self, eng, s, key):
        if self.emit:
            eng.wait_ge(self.sems[s], self.M[(s,) + tuple(key)])

    def dma(self, eng, semname, key, out, in_, chain=True):
        cur = self.cnt[semname]
        if self.emit and chain and cur:
            eng.wait_ge(self.sems[semname], cur)
        instr = eng.dma_start(out=out, in_=in_) if self.emit else None
        self.inc(instr, semname, key, 16)

    def sb(self, name, shape, dtype=F32):
        if self.emit:
            return self.es.enter_context(self.nc.sbuf_tensor(name, shape, dtype))
        return Fake()

    def ps(self, name, shape, dtype=F32):
        if self.emit:
            return self.es.enter_context(self.nc.psum_tensor(name, shape, dtype))
        return Fake()

    @contextmanager
    def block(self, name):
        if self.emit:
            with self.nc.Block(name) as b:
                yield b
        else:
            yield Fake()

    def body(self, blk, engname, fn):
        if self.emit:
            getattr(blk, engname)(fn)
        else:
            fn(Fake())


def xn_sem(t, e):
    return "v" if (t < 2 or e % 2 == 0) else "p"


def sg_sem(e2):
    return "v" if e2 % 2 == 0 else "p"


def construct(k: "K", io):
    CONSTS = ["identd", "onesrd", "onescbd", "onescfd", "cpoold", "b1d",
              "b2d", "nwd", "mixd", "cbd", "btriud", "bvi2d"]
    for s in ("t", "v", "s", "p", "cc"):
        k.sem(s)
    dma_sems = ["dcs", "dq0", "dq1", "dq2", "dxl0", "dxl1", "dxr0",
                "dxr1", "do0", "do1", "dst0", "dst1", "dsw", "dow0", "dor",
                "dcc", "drow"]
    for s in dma_sems:
        k.sem(s)

    w1t8 = k.sb("w1t8", [128, NE * FF], FP8)
    w2t8 = k.sb("w2t8", [128, NF * EMB], FP8)
    hT = k.sb("hT", [128, NF * T], BF16)
    xnt = [k.sb(f"xnt{i}", [128, NE * T], BF16) for i in range(2)]
    a_all = k.sb("a_all", [128, NE * T], BF16)
    gtb = k.sb("gtb", [128, NE * T], BF16)
    sqb = [k.sb("sqb0", [128, T], BF16)]
    outb = [k.sb(f"outb{i}", [128, T]) for i in range(2)]
    xlnb = [k.sb(f"xlnb{i}", [128, T]) for i in range(2)]
    xresb = [k.sb(f"xresb{i}", [128, T]) for i in range(2)]
    mrB = [k.sb(f"mrB{i}", [128, T]) for i in range(2)]
    rsB = [k.sb(f"rsB{i}", [128, T]) for i in range(2)]
    rstdB = k.sb("rstdB", [128, T])
    mrbuf = [k.sb("mrbuf0", [1, T])]
    rsbuf = [k.sb("rsbuf0", [1, T])]
    rms_row = k.sb("rms_row", [1, T])
    rstd_row = k.sb("rstd_row", [1, T])
    pool = [k.sb(f"pool{i}", [128, CH]) for i in range(NPOOL)]
    mposb = [k.sb(f"mposb{i}", [128, CH], BF16) for i in range(2)]
    mnegb = [k.sb(f"mnegb{i}", [128, CH], BF16) for i in range(2)]
    mcols = k.sb("mcols", [128, 16])
    rscols = k.sb("rscols", [128, 16])
    mrcols = k.sb("mrcols", [128, 16])
    sqcols = k.sb("sqcols", [128, 16])
    varcols = k.sb("varcols", [128, 16])
    mrT = k.sb("mrT", [16, 128])
    rsT = k.sb("rsT", [16, 128])
    hmv = k.sb("hmv", [128, 16])
    thrpart = k.sb("thrpart", [128, 8])
    thrcomb = k.sb("thrcomb", [128, 2])
    thrsums = k.sb("thrsums", [1, 2])
    thrsumg = k.sb("thrsumg", [1, 2])
    thrrow = k.sb("thrrow", [1, 2])
    thrcol = k.sb("thrcol", [128, 2])
    nthrcol = k.sb("nthrcol", [128, 2])
    sident = k.sb("sident", [128, 128])
    sonesr = k.sb("sonesr", [1, 128])
    sonescb = k.sb("sonescb", [128, 1], BF16)
    sonescf = k.sb("sonescf", [128, 1])
    scpool = k.sb("scpool", [128, 4])
    sb1 = k.sb("sb1", [128, 32])
    sb2 = k.sb("sb2", [128, 8])
    snw = k.sb("snw", [128, 8])
    smix = k.sb("smix", [128, 2])
    scb = k.sb("scb", [128, 128])
    sbtriu = k.sb("sbtriu", [64, 64])
    sbvi2 = k.sb("sbvi2", [2, 64])
    s64row = k.sb("s64row", [128, EMB])
    woos = k.sb("woos", [64, EMB], BF16)
    omu = k.sb("omu", [64, NB * 64])
    omg = k.sb("omg", [64, NB * 64])
    m2s = k.sb("m2s", [64, NB * 64])
    m3s = k.sb("m3s", [64, NB * 64])
    gtmp = k.sb("gtmp", [64, 64])
    GbT = k.sb("GbT", [64, NB * 64], BF16)
    wlie = k.sb("wlie", [64, NB * EMB], BF16)
    vqi = k.sb("vqi", [2, 64])
    vqq = k.sb("vqq", [2, 64])
    vnorm = k.sb("vnorm", [2, 1])
    nvq = k.sb("nvq", [2, 64])
    nvqT = k.sb("nvqT", [64, 2])
    csq = k.sb("csq", [128, 128])
    cnorm = k.sb("cnorm", [128, 2])
    sncb = k.sb("sncb", [128, 128])
    ncbT = k.sb("ncbT", [64, 256])
    scoresR = k.sb("scoresR", [2, 256])
    mxr = k.sb("mxr", [2, 1])
    mxT = k.sb("mxT", [1, 2])
    scT = k.sb("scT", [128, 4])
    soh = k.sb("soh", [128, 4])
    vqcT = k.sb("vqcT", [64, 2])
    bvoh = k.sb("bvoh", [2, T])
    booh = k.sb("booh", [2, T])
    constPP = k.sb("constPP", [128, 16])

    p_mm = [k.ps(f"p_mm{i}", [128, T]) for i in range(2)]
    p_lie = [k.ps(f"p_lie{i}", [128, T]) for i in range(2)]
    p_ms = k.ps("p_ms", [128, T])
    p_rstd = k.ps("p_rstd", [128, T])
    p_bc = [k.ps(f"p_bc{i}", [128, T]) for i in range(2)]

    def src_of(stream, idx):
        if stream == "w1s":
            return io["w1pred"][:, idx * CH:(idx + 1) * CH]
        if stream == "w2s":
            return io["w2pred"][idx * 128:(idx + 1) * 128, :]
        if stream == "xt":
            b, i = idx
            return io["xTokd"][b * SEQ + i * 128: b * SEQ + (i + 1) * 128, :]
        if stream == "w1b":
            h, e = idx
            return io["w1td"][e * 128:(e + 1) * 128, h * CH:(h + 1) * CH]
        if stream == "w2b":
            return io["w2td"][idx * 128:(idx + 1) * 128, :]
        if stream == "wtr":
            h, e = idx
            return io["wtriud"][e * 128:(e + 1) * 128, h * CH:(h + 1) * CH]
        raise KeyError(stream)

    def release_of(stream, idx):
        if stream in ("w1s", "w2s"):
            w = 0 if stream == "w1s" else 1
            return ("v", ("wred", w, idx))
        if stream == "xt":
            return ("t", ("hm",) + idx)
        if stream == "w1b":
            h, e = idx
            return ("s", ("sgn2", 0, h * NE + e))
        if stream == "w2b":
            return ("s", ("sgn2", 1, idx))
        if stream == "wtr":
            return ("t", ("om",) + idx)
        raise KeyError(stream)

    uses = []
    for j in range(4):
        uses.append(("w1s", j))
    for j in range(4):
        uses.append(("w2s", j))
    for i in range(16):
        uses.append(("xt", (0, i)))
    for h in range(4):
        for e in range(NE):
            uses.append(("w1b", (h, e)))
    for F in range(NF):
        uses.append(("w2b", F))
    for i in range(16):
        uses.append(("xt", (1, i)))
    for h in range(4):
        for e in range(NE):
            uses.append(("wtr", (h, e)))
    POOLBUF = {u: n % NPOOL for n, u in enumerate(uses)}
    USE_POS = {u: n for n, u in enumerate(uses)}

    def x_src(t, e):
        b, i = t // TPB, t % TPB
        return io["xTd"][b * EMB + e * 128: b * EMB + (e + 1) * 128,
                         i * T:(i + 1) * T]

    def out_dst(t, e):
        b, i = t // TPB, t % TPB
        return io["outTd"][b * EMB + e * 128: b * EMB + (e + 1) * 128,
                           i * T:(i + 1) * T]

    def stats_src(t, row):
        b, i = t // TPB, t % TPB
        return io["stats_d"][b * 2 + row: b * 2 + row + 1, i * T:(i + 1) * T]

    def xln_prev(t, e):
        if e >= 2:
            return (t, e - 2)
        if t > 0:
            return (t - 1, e + 6)
        return None

    with k.block("main") as blk:
        def sync_body(sync):
            cdst = {"identd": sident, "onesrd": sonesr, "onescbd": sonescb,
                    "cpoold": scpool, "b1d": sb1, "b2d": sb2, "nwd": snw,
                    "mixd": smix, "cbd": scb, "btriud": sbtriu,
                    "bvi2d": sbvi2, "onescfd": sonescf}
            for n in CONSTS:
                k.dma(sync, "dcs", ("c", n), out=cdst[n][:], in_=io[n],
                      chain=False)
            k.inc(None, "dcs", ("call",), 0)
            k.dma(sync, "drow", ("woo",), out=s64row[0:64, :], in_=io["wooTd"])

            def fifo(u):
                stream, idx = u
                pos = USE_POS[u]
                if pos >= NPOOL:
                    ps_, pk = release_of(*uses[pos - NPOOL])
                    k.wait(sync, ps_, pk)
                k.dma(sync, f"dq{pos % NPOOL}", u, out=pool[POOLBUF[u]][:],
                      in_=src_of(stream, idx))

            for j in range(4):
                fifo(("w1s", j))
            for j in range(4):
                fifo(("w2s", j))
            k.wait(sync, "v", ("thrsums",))
            k.dma(sync, "dcc", ("ccw",), out=io["ccin"], in_=thrsums[:])
            for i in range(16):
                fifo(("xt", (0, i)))
            k.wait(sync, "cc", ("ccdone",))
            k.dma(sync, "dcc", ("ccr",), out=thrsumg[:], in_=io["ccout"])
            k.wait(sync, "s", ("stev", 0))
            k.dma(sync, "dsw", ("swb", 0, 0),
                  out=io["stats_d"][0:1, :].rearrange(
                      "one (c p) -> (one c) p", c=16),
                  in_=mrT[:])
            k.dma(sync, "dsw", ("swb", 0, 1),
                  out=io["stats_d"][1:2, :].rearrange(
                      "one (c p) -> (one c) p", c=16),
                  in_=rsT[:])
            k.wait(sync, "dsw", ("swb", 0, 1))
            k.dma(sync, "dst0", ("stm", 0), out=mrbuf[0][:], in_=stats_src(0, 0))
            k.dma(sync, "dst1", ("str", 0), out=rsbuf[0][:], in_=stats_src(0, 1))
            t0slots = [xlnb[0], xlnb[1], xresb[0], xresb[1], mrB[1], rsB[1],
                       rstdB, outb[1]]
            for e in range(NE):
                k.dma(sync, f"dxl{e % 2}", ("xln", 0, e),
                      out=t0slots[e][:], in_=x_src(0, e))
            for h in range(4):
                for e in range(NE):
                    fifo(("w1b", (h, e)))
            k.wait(sync, "t", ("bc", 0))
            for t_ in (1,):
                k.dma(sync, "dst0", ("stm", t_), out=mrbuf[0][:],
                      in_=stats_src(t_, 0))
                k.dma(sync, "dst1", ("str", t_), out=rsbuf[0][:],
                      in_=stats_src(t_, 1))
            k.wait(sync, "s", ("woos",))
            k.dma(sync, "drow", ("wvi",), out=s64row[:, 0:512], in_=io["wviTd"])
            for F in range(NF):
                fifo(("w2b", F))
                if F == 4:
                    for e in range(2):
                        pt, pe = xln_prev(1, e)
                        k.wait(sync, xn_sem(pt, pe), ("xn", pt, pe))
                        k.dma(sync, f"dxl{e % 2}", ("xln", 1, e),
                              out=xlnb[e % 2][:], in_=x_src(1, e))
                if F == 8:
                    k.wait(sync, "t", ("bc", 1))
                    for t_ in (2,):
                        k.dma(sync, "dst0", ("stm", t_), out=mrbuf[0][:],
                              in_=stats_src(t_, 0))
                        k.dma(sync, "dst1", ("str", t_), out=rsbuf[0][:],
                              in_=stats_src(t_, 1))
                if F == 12:
                    for e2 in (0, 1):
                        k.wait(sync, "v", ("xn", 0, e2 + 2))
                        k.dma(sync, f"dxr{e2 % 2}", ("xres", 0, e2),
                              out=xresb[e2 % 2][:], in_=x_src(0, e2))
            for i in range(16):
                fifo(("xt", (1, i)))
            for e in range(2, NE):
                pt, pe = xln_prev(1, e)
                k.wait(sync, xn_sem(pt, pe), ("xn", pt, pe))
                k.dma(sync, f"dxl{e % 2}", ("xln", 1, e),
                      out=xlnb[e % 2][:], in_=x_src(1, e))
            for e in range(2):
                pt, pe = xln_prev(2, e)
                k.wait(sync, xn_sem(pt, pe), ("xn", pt, pe))
                k.dma(sync, f"dxl{e % 2}", ("xln", 2, e),
                      out=xlnb[e % 2][:], in_=x_src(2, e))
            k.wait(sync, "s", ("stev", 1))
            k.dma(sync, "dsw", ("swb", 1, 0),
                  out=io["stats_d"][2:3, :].rearrange("one (c p) -> (one c) p", c=16),
                  in_=mrT[:])
            k.dma(sync, "dsw", ("swb", 1, 1),
                  out=io["stats_d"][3:4, :].rearrange("one (c p) -> (one c) p", c=16),
                  in_=rsT[:])
            k.dma(sync, "drow", ("bvo0a",), out=bvoh[:], in_=io["bvo2d"][:, 0:T])
            k.dma(sync, "drow", ("bvo", 0), out=booh[:], in_=io["boo2d"][:, 0:T])
            for h in range(4):
                for e in range(NE):
                    fifo(("wtr", (h, e)))
                for q in range(2):
                    k.wait(sync, "v", ("omev", h, q))
                    k.dma(sync, "dow0", ("omw", h, q),
                          out=io["omdram"][0:2, h * CH + q * T: h * CH + (q + 1) * T],
                          in_=outb[0][0:2, :])
            k.wait(sync, "t", ("spvq",))
            k.dma(sync, "drow", ("wvo",), out=s64row[0:64, :], in_=io["wvoTd"])
            k.wait(sync, "dow0", ("omw", 3, 1))
            for b in range(NB):
                rdsrc = io["omdram"][b:b + 1, :].rearrange(
                    "one (r c) -> (one r) c", r=64)
                k.dma(sync, "dor", ("omr", b), out=omu[:, b * 64:(b + 1) * 64],
                      in_=rdsrc)
            k.wait(sync, "v", ("spc2", 0))
            k.dma(sync, "drow", ("bvo1a",), out=bvoh[:], in_=io["bvo2d"][:, T:EMB])
            k.dma(sync, "drow", ("bvo", 1), out=booh[:], in_=io["boo2d"][:, T:EMB])
            for t in range(NT):
                if 2 <= t + 1 < NT:
                    erange = range(2, NE) if t + 1 == 2 else range(NE)
                    for e in erange:
                        pt, pe = xln_prev(t + 1, e)
                        k.wait(sync, xn_sem(pt, pe), ("xn", pt, pe))
                        k.dma(sync, f"dxl{e % 2}", ("xln", t + 1, e),
                              out=xlnb[e % 2][:], in_=x_src(t + 1, e))
                if t + 3 < NT:
                    k.wait(sync, "t", ("bc", t + 2))
                    if t + 3 == 4:
                        k.wait(sync, "dsw", ("swb", 1, 1))
                    k.dma(sync, "dst0", ("stm", t + 3), out=mrbuf[0][:],
                          in_=stats_src(t + 3, 0))
                    k.dma(sync, "dst1", ("str", t + 3), out=rsbuf[0][:],
                          in_=stats_src(t + 3, 1))
                for e2 in range(NE):
                    if not (t == 0 and e2 < 2):
                        pe = e2 - 2
                        pt = t
                        if pe < 0:
                            pe += 8
                            pt -= 1
                        k.wait(sync, sg_sem(pe), ("sg", pt, pe))
                        k.dma(sync, f"dxr{e2 % 2}", ("xres", t, e2),
                              out=xresb[e2 % 2][:], in_=x_src(t, e2))
                    if e2 >= 2:
                        k.wait(sync, sg_sem(e2 - 2), ("sg", t, e2 - 2))
                        k.dma(sync, f"do{e2 % 2}", ("od", t, e2 - 2),
                              out=out_dst(t, e2 - 2), in_=outb[e2 % 2][:])
                for e2 in (6, 7):
                    k.wait(sync, sg_sem(e2), ("sg", t, e2))
                    k.dma(sync, f"do{e2 % 2}", ("od", t, e2),
                          out=out_dst(t, e2), in_=outb[e2 % 2][:])
            k.wait(sync, "do0", ("od", NT - 1, NE - 2))
            k.wait(sync, "do1", ("od", NT - 1, NE - 1))
        k.body(blk, "sync", sync_body)

        def gp_body(gp):
            k.wait(gp, "dcs", ("call",))
            k.wait(gp, "dcc", ("ccw",))
            cci = gp.collective_compute(
                "AllReduce", ALU.add, replica_groups=[list(range(NCORES))],
                ins=[io["ccin"]], outs=[io["ccout"]]) if k.emit else None
            k.inc(cci, "cc", ("ccdone",))

            def tadd_gp(w, clist):
                for c in clist:
                    k.wait(gp, "s", ("sgn2", w, c))
                    k.inc(gp.tensor_tensor(mposb[c % 2][:], mposb[c % 2][:],
                                           mnegb[c % 2][:], op=ALU.add),
                          "p", ("tadd", w, c))

            tadd_gp(0, list(range(0, 32, 2)))
            tadd_gp(1, list(range(0, 32, 2)))
            for t in range(NT):
                for e2 in range(1, NE, 2):
                    k.wait(gp, "s", ("silu", t, e2))
                    k.wait(gp, "s", ("rstdev", t))
                    k.wait(gp, "v", ("a", t, e2))
                    k.wait(gp, f"dxr{e2 % 2}", ("xres", t, e2))
                    if (t, e2) > (0, 1):
                        pe = e2 - 2
                        pt = t
                        if pe < 0:
                            pe += 8
                            pt -= 1
                        k.wait(gp, f"do{pe % 2}", ("od", pt, pe))
                    gp.tensor_tensor(outb[e2 % 2][:],
                                     gtb[:, e2 * T:(e2 + 1) * T],
                                     rstdB[:], op=ALU.mult)
                    gp.tensor_tensor(outb[e2 % 2][:], outb[e2 % 2][:],
                                     a_all[:, e2 * T:(e2 + 1) * T], op=ALU.mult)
                    k.inc(gp.tensor_tensor(outb[e2 % 2][:], outb[e2 % 2][:],
                                           xresb[e2 % 2][:], op=ALU.add),
                          "p", ("sg", t, e2))
                t2 = t + 2
                if t2 < NT:
                    for e in range(1, NE, 2):
                        k.wait(gp, f"dxl{e % 2}", ("xln", t2, e))
                        k.wait(gp, "s", ("bcev", t2))
                        k.wait(gp, "v", ("a", t, e))
                        if e == 1:
                            k.wait(gp, "do1", ("od", t, 7))
                        gp.tensor_tensor(outb[1][:], xlnb[e % 2][:],
                                         mrB[t2 % 2][:], op=ALU.subtract)
                        k.inc(gp.tensor_tensor(xnt[t2 % 2][:, e * T:(e + 1) * T],
                                               outb[1][:], rsB[t2 % 2][:],
                                               op=ALU.mult), "p", ("xn", t2, e))
        k.body(blk, "gpsimd", gp_body)

        def vector_body(vector):
            k.wait(vector, "dcs", ("call",))
            for w in range(2):
                for j in range(4):
                    stream = "w1s" if w == 0 else "w2s"
                    k.wait(vector, f"dq{USE_POS[(stream, j)] % NPOOL}", (stream, j))
                    k.inc(vector.reduce_sum(thrpart[:, w * 4 + j:w * 4 + j + 1],
                                            pool[POOLBUF[(stream, j)]][:],
                                            axis=AX.X, apply_absolute_value=True),
                          "v", ("wred", w, j))
            for w in range(2):
                k.inc(vector.reduce_sum(thrcomb[:, w:w + 1],
                                        thrpart[:, w * 4:w * 4 + 4], axis=AX.X),
                      "v", ("thrc0", w))
            k.wait(vector, "t", ("thrtr",))
            vector.tensor_copy(rms_row[0:1, 0:128], p_bc[0][0:1, 0:128])
            k.inc(vector.reduce_sum(thrsums[0:1, 0:1], rms_row[0:1, 0:128],
                                    axis=AX.X), "v", ("ts0",))
            k.wait(vector, "t", ("thrtr2",))
            vector.tensor_copy(rms_row[0:1, 0:128], p_bc[1][0:1, 0:128])
            k.inc(vector.reduce_sum(thrsums[0:1, 1:2], rms_row[0:1, 0:128],
                                    axis=AX.X), "v", ("thrsums",))

            def stat_chain(b, i):
                k.wait(vector, "s", ("acc", b, i))
                k.inc(vector.tensor_tensor(varcols[:, i:i + 1], sqcols[:, i:i + 1],
                                           varcols[:, i:i + 1], op=ALU.subtract),
                      "v", ("var", b, i))
                k.wait(vector, "s", ("sqv", b, i))
                k.inc(vector.reciprocal(rscols[:, i:i + 1], varcols[:, i:i + 1]),
                      "v", ("rscr", b, i))
                k.wait(vector, "v", ("rscr", b, i))
                k.inc(vector.tensor_tensor(mrcols[:, i:i + 1], mcols[:, i:i + 1],
                                           rscols[:, i:i + 1], op=ALU.mult),
                      "v", ("rsc", b, i))

            for i in range(16):
                stat_chain(0, i)
            vector.reduce_sum(thrpart[:, 0:1], mrcols[:], axis=AX.X)
            k.inc(vector.tensor_copy(thrcomb[:, 0:1], thrpart[:, 0:1]),
                  "v", ("mrsum", 0))
            k.wait(vector, "t", ("smr", 0))
            k.inc(vector.tensor_copy(mcols[0:1, 0:1], p_ms[0:1, 20:21]),
                  "v", ("smre", 0))
            k.wait(vector, "t", ("hm", 0, 15))
            vector.tensor_scalar(rms_row[:], p_mm[0][0:1, :],
                                 scalar1=mcols[0:1, 0:1], scalar2=None,
                                 op0=ALU.subtract)
            k.inc(vector.tensor_scalar(rstd_row[:], p_mm[1][0:1, :],
                                       scalar1=mcols[0:1, 0:1], scalar2=None,
                                       op0=ALU.subtract), "v", ("hmev", 0))
            k.wait(vector, "t", ("hmv", 0))
            k.inc(vector.tensor_scalar(
                hmv[:, 0:16].rearrange("p (e b) -> p b e", b=2)[:, 0, :],
                p_ms[:, 0:16].rearrange("p (e b) -> p b e", b=2)[:, 0, :],
                scalar1=SEQ_INV, scalar2=None, op0=ALU.mult), "v", ("hmvv", 0))
            k.wait(vector, "dcc", ("ccr",))
            k.inc(vector.tensor_scalar(thrrow[:], thrsumg[:], scalar1=MEAN_W,
                                       scalar2=1e-8, op0=ALU.mult, op1=ALU.max),
                  "v", ("thrrow",))
            k.wait(vector, "t", ("thrbc",))
            vector.tensor_scalar(thrcol[:], p_bc[0][:, 0:2], scalar1=0.5,
                                 scalar2=None, op0=ALU.mult)
            k.inc(vector.tensor_scalar(nthrcol[:], p_bc[0][:, 0:2], scalar1=-0.5,
                                       scalar2=None, op0=ALU.mult),
                  "v", ("thrc",))

            t0slots_v = [xlnb[0], xlnb[1], xresb[0], xresb[1], mrB[1], rsB[1],
                         rstdB, outb[1]]

            def xn_tile(t):
                for e in range(NE):
                    if t == 0:
                        k.wait(vector, f"dxl{e % 2}", ("xln", t, e))
                        src = t0slots_v[e]
                    else:
                        k.wait(vector, f"dxl{e % 2}", ("xln", t, e))
                        src = xlnb[e % 2]
                    k.wait(vector, "s", ("bcev", t))
                    vector.tensor_tensor(outb[0][:], src[:], mrB[t % 2][:],
                                         op=ALU.subtract)
                    k.inc(vector.tensor_tensor(xnt[t % 2][:, e * T:(e + 1) * T],
                                               outb[0][:], rsB[t % 2][:],
                                               op=ALU.mult), "v", ("xn", t, e))

            def tfin_v(w, clist):
                for c in clist:
                    if w == 0:
                        h, e = c // NE, c % NE
                        dst = w1t8[:, e * FF + h * CH: e * FF + (h + 1) * CH]
                        key = ("w1t8", h, e)
                    else:
                        dst = w2t8[:, c * EMB:(c + 1) * EMB]
                        key = ("w2t8", c)
                    if c % 2 == 0:
                        k.wait(vector, "p", ("tadd", w, c))
                    else:
                        k.wait(vector, "s", ("sgn2", w, c))
                        vector.tensor_tensor(mposb[c % 2][:], mposb[c % 2][:],
                                             mnegb[c % 2][:], op=ALU.add)
                    k.inc(vector.tensor_scalar(dst, mposb[c % 2][:], scalar1=0.5,
                                               scalar2=None, op0=ALU.mult),
                          "v", key)

            tfin_v(0, list(range(8)))
            xn_tile(0)
            tfin_v(0, list(range(8, 32)))
            tfin_v(1, list(range(32)))
            k.wait(vector, "t", ("sttr", 0))
            k.wait(vector, "v", ("hmev", 0))
            for i in range(16):
                stat_chain(1, i)
            xn_tile(1)
            vector.reduce_sum(thrpart[:, 0:1], mrcols[:], axis=AX.X)
            k.inc(vector.tensor_copy(thrcomb[:, 0:1], thrpart[:, 0:1]),
                  "v", ("mrsum", 1))
            k.wait(vector, "t", ("smr", 1))
            k.inc(vector.tensor_copy(mcols[0:1, 0:1], p_ms[0:1, 20:21]),
                  "v", ("smre", 1))
            k.wait(vector, "t", ("hm", 1, 15))
            k.wait(vector, "t", ("hmv", 0))
            vector.tensor_scalar(rms_row[:], p_lie[0][0:1, :],
                                 scalar1=mcols[0:1, 0:1], scalar2=None,
                                 op0=ALU.subtract)
            k.inc(vector.tensor_scalar(rstd_row[:], p_lie[1][0:1, :],
                                       scalar1=mcols[0:1, 0:1], scalar2=None,
                                       op0=ALU.subtract), "v", ("hmev", 1))
            k.wait(vector, "t", ("hmv", 1))
            k.inc(vector.tensor_scalar(
                hmv[:, 0:16].rearrange("p (e b) -> p b e", b=2)[:, 1, :],
                p_ms[:, 0:16].rearrange("p (e b) -> p b e", b=2)[:, 1, :],
                scalar1=SEQ_INV, scalar2=None, op0=ALU.mult), "v", ("hmvv", 1))

            k.wait(vector, "t", ("spvq",))
            k.inc(vector.tensor_tensor(vqi[:], p_ms[0:2, 0:64], sbvi2[:],
                                       op=ALU.add), "v", ("spvqi",))
            k.wait(vector, "s", ("spvqsq",))
            k.inc(vector.reduce_sum(vnorm[:], vqq[:], axis=AX.X), "v", ("spvnr",))
            k.wait(vector, "s", ("spvqn",))
            vector.tensor_scalar(vnorm[:], vnorm[:], scalar1=1e-12, scalar2=None,
                                 op0=ALU.max)
            k.inc(vector.reciprocal(vnorm[:], vnorm[:]), "v", ("spvrec",))
            k.wait(vector, "v", ("spvrec",))
            k.inc(vector.tensor_scalar(nvq[:], vqi[:], scalar1=vnorm[:],
                                       scalar2=None, op0=ALU.mult),
                  "v", ("spnvq",))
            k.wait(vector, "s", ("spcbsq",))
            for c in range(2):
                rs_cn = vector.reduce_sum(cnorm[:, c:c + 1],
                                          csq[:, c * 64:(c + 1) * 64], axis=AX.X)
            k.inc(rs_cn, "v", ("spcn",))
            k.wait(vector, "s", ("spcbn",))
            for c in range(2):
                vector.tensor_scalar(cnorm[:, c:c + 1], cnorm[:, c:c + 1],
                                     scalar1=1e-12, scalar2=None, op0=ALU.max)
            k.inc(vector.reciprocal(cnorm[:], cnorm[:]), "v", ("spcrec",))
            k.wait(vector, "v", ("spcrec",))
            for c in range(2):
                k.inc(vector.tensor_scalar(sncb[:, c * 64:(c + 1) * 64],
                                           scb[:, c * 64:(c + 1) * 64],
                                           scalar1=cnorm[:, c:c + 1],
                                           scalar2=None, op0=ALU.mult),
                      "v", ("spncb", c))
            for c in range(2):
                k.wait(vector, "t", ("spncbt", c))
                src = p_rstd if c == 0 else p_ms
                k.inc(vector.tensor_copy(ncbT[:, c * 128:(c + 1) * 128],
                                         src[0:64, 0:128]), "v", ("spncbtev", c))
            k.wait(vector, "t", ("spnvqt",))
            k.inc(vector.tensor_copy(nvqT[:], p_lie[0][0:64, 0:2]),
                  "v", ("spnvqte",))
            k.wait(vector, "t", ("spsc",))
            vector.tensor_copy(scoresR[:], p_rstd[0:2, 0:256])
            k.inc(vector.reduce_max(mxr[:], scoresR[:], axis=AX.X), "v", ("spmx",))
            k.wait(vector, "t", ("spmxt",))
            k.inc(vector.tensor_copy(mxT[:], p_ms[0:1, 0:2]), "v", ("spmxte",))
            k.wait(vector, "t", ("spsct", 1))
            for c in range(2):
                vector.tensor_copy(scT[:, c * 2:c * 2 + 2],
                                   p_lie[0][:, c * 2:c * 2 + 2])
            k.wait(vector, "t", ("spmb",))
            for c in range(2):
                k.inc(vector.tensor_tensor(soh[:, c * 2:c * 2 + 2],
                                           scT[:, c * 2:c * 2 + 2],
                                           p_rstd[:, 0:2], op=ALU.is_ge),
                      "v", ("spoh", c))
            k.wait(vector, "t", ("spvc",))
            k.inc(vector.tensor_copy(vqcT[:], p_ms[0:64, 0:2]), "v", ("spvce",))
            for h in range(4):
                k.wait(vector, "t", ("om", h, NE - 1))
                for q in range(2):
                    if (h, q) != (0, 0):
                        pq = (h, q - 1) if q == 1 else (h - 1, 1)
                        k.wait(vector, "dow0", ("omw",) + pq)
                    k.inc(vector.tensor_copy(outb[0][0:2, :], p_lie[q][0:2, :]),
                          "v", ("omev", h, q))
            for s_ in range(2):
                k.wait(vector, "drow", ("bvo", s_))
                k.wait(vector, "t", ("spvcr", s_))
                vector.tensor_tensor(bvoh[:], bvoh[:], p_lie[s_][0:2, :],
                                     op=ALU.add)
                vector.tensor_scalar(bvoh[:], bvoh[:], scalar1=smix[0:2, 1:2],
                                     scalar2=None, op0=ALU.mult)
                vector.tensor_scalar(booh[:], booh[:], scalar1=smix[0:2, 0:1],
                                     scalar2=None, op0=ALU.mult)
                k.inc(vector.tensor_tensor(booh[:], booh[:], bvoh[:],
                                           op=ALU.add), "v", ("spc2", s_))
                for e4 in range(4):
                    e = s_ * 4 + e4
                    k.wait(vector, "t", ("spct", e))
                    src = p_ms if e % 2 == 0 else p_rstd
                    k.inc(vector.tensor_copy(constPP[:, e * 2:e * 2 + 2],
                                             src[:, 0:2]), "v", ("spcpp", e))
            for b in range(NB):
                k.wait(vector, "dor", ("omr", b))
                k.inc(vector.tensor_tensor(omu[:, b * 64:(b + 1) * 64],
                                           omu[:, b * 64:(b + 1) * 64],
                                           sbtriu[:], op=ALU.add),
                      "v", ("spomu", b))
                k.wait(vector, "t", ("sptr", b))
                k.inc(vector.tensor_tensor(omg[:, b * 64:(b + 1) * 64],
                                           omu[:, b * 64:(b + 1) * 64],
                                           p_bc[0][0:64, 0:64], op=ALU.subtract),
                      "v", ("spomb", b))
                k.wait(vector, "t", ("spm2", b))
                k.inc(vector.tensor_copy(m2s[:, b * 64:(b + 1) * 64],
                                         p_bc[1][0:64, 0:64]), "v", ("spm2e", b))
                k.wait(vector, "v", ("spm2e", b))
                k.wait(vector, "t", ("spm3", b))
                k.inc(vector.tensor_copy(m3s[:, b * 64:(b + 1) * 64],
                                         p_bc[0][0:64, 0:64]), "v", ("spm3e", b))
                k.wait(vector, "v", ("spm3e", b))
                if b == 1:
                    k.wait(vector, "t", ("gbt", 0))
                bs = slice(b * 64, (b + 1) * 64)
                vector.tensor_scalar(gtmp[:], m2s[:, bs], scalar1=-0.5,
                                     scalar2=None, op0=ALU.mult)
                vector.tensor_tensor(gtmp[:], gtmp[:], omg[:, bs], op=ALU.add)
                vector.tensor_scalar(m3s[:, bs], m3s[:, bs], scalar1=1.0 / 6.0,
                                     scalar2=None, op0=ALU.mult)
                vector.tensor_tensor(gtmp[:], gtmp[:], m3s[:, bs], op=ALU.add)
                k.inc(vector.tensor_tensor(gtmp[:], gtmp[:],
                                           sident[0:64, 0:64], op=ALU.add),
                      "v", ("spG", b))
                k.wait(vector, "t", ("gbt", b))
                k.inc(vector.tensor_copy(GbT[:, bs], p_bc[1][0:64, 0:64]),
                      "v", ("gbte", b))
                k.wait(vector, "t", ("wliem", b))
                vector.tensor_copy(wlie[:, b * EMB: b * EMB + T],
                                   p_bc[0][0:64, :])
                k.inc(vector.tensor_copy(wlie[:, b * EMB + T: (b + 1) * EMB],
                                         p_bc[1][0:64, :]), "v", ("wlie", b))

            for t in range(NT):
                b = t // TPB
                for e in range(NE):
                    k.wait(vector, "t", ("lie", t, e))
                    if t == 0 and e == 0:
                        k.wait(vector, "v", ("spcpp", 7))
                    if t >= 1:
                        k.wait(vector, sg_sem(e), ("sg", t - 1, e))
                        k.wait(vector, "s", ("sq", t - 1, e))
                    k.inc(vector.scalar_tensor_tensor(
                        out=a_all[:, e * T:(e + 1) * T], in0=p_lie[e % 2][:],
                        scalar=constPP[:, e * 2 + b:e * 2 + b + 1],
                        in1=xnt[t % 2][:, e * T:(e + 1) * T],
                        op0=ALU.add, op1=ALU.add), "v", ("a", t, e))
                k.wait(vector, "s", ("sqrt", t))
                if t >= 1:
                    k.wait(vector, "t", ("rstdbc", t - 1))
                k.inc(vector.reciprocal(rstd_row[:], rms_row[:]),
                      "v", ("recip", t))
                for e2 in range(0, NE, 2):
                    k.wait(vector, "s", ("silu", t, e2))
                    k.wait(vector, "s", ("rstdev", t))
                    k.wait(vector, f"dxr{e2 % 2}", ("xres", t, e2))
                    if (t, e2) > (0, 1):
                        pe = e2 - 2
                        pt = t
                        if pe < 0:
                            pe += 8
                            pt -= 1
                        k.wait(vector, f"do{pe % 2}", ("od", pt, pe))
                    vector.tensor_tensor(outb[e2 % 2][:],
                                         gtb[:, e2 * T:(e2 + 1) * T],
                                         rstdB[:], op=ALU.mult)
                    vector.tensor_tensor(outb[e2 % 2][:], outb[e2 % 2][:],
                                         a_all[:, e2 * T:(e2 + 1) * T],
                                         op=ALU.mult)
                    k.inc(vector.tensor_tensor(outb[e2 % 2][:], outb[e2 % 2][:],
                                               xresb[e2 % 2][:], op=ALU.add),
                          "v", ("sg", t, e2))
                if t + 2 < NT:
                    t2 = t + 2
                    for e in range(0, NE, 2):
                        k.wait(vector, f"dxl{e % 2}", ("xln", t2, e))
                        k.wait(vector, "s", ("bcev", t2))
                        if e == 0:
                            k.wait(vector, "do0", ("od", t, 6))
                        vector.tensor_tensor(outb[0][:], xlnb[e % 2][:],
                                             mrB[t2 % 2][:], op=ALU.subtract)
                        k.inc(vector.tensor_tensor(
                            xnt[t2 % 2][:, e * T:(e + 1) * T], outb[0][:],
                            rsB[t2 % 2][:], op=ALU.mult), "v", ("xn", t2, e))
        k.body(blk, "vector", vector_body)

        def scalar_body(scalar):
            k.wait(scalar, "dcs", ("call",))

            def stats_acc(b, i):
                u = ("xt", (b, i))
                k.wait(scalar, f"dq{USE_POS[u] % NPOOL}", u)
                if b == 1 and i == 0:
                    k.wait(scalar, "v", ("hmev", 0))
                scr = mposb[0][:] if b == 0 else xnt[1][:, 0:CH]
                scalar.activation(scr, pool[POOLBUF[u]][:], AF.Copy,
                                  scale=LN_INV, accum_out=mcols[:, i:i + 1])
                scalar.activation(scr, pool[POOLBUF[u]][:],
                                  AF.Square, scale=SQ_SCALE,
                                  accum_out=sqcols[:, i:i + 1])
                k.inc(scalar.square(varcols[:, i:i + 1], mcols[:, i:i + 1]),
                      "s", ("acc", b, i))

            def stats_sqv(b, i):
                k.wait(scalar, "v", ("var", b, i))
                k.inc(scalar.activation(varcols[:, i:i + 1], varcols[:, i:i + 1],
                                        AF.Sqrt, bias=scpool[:, 1:2]),
                      "s", ("sqv", b, i))

            for i in range(16):
                stats_acc(0, i)
                if i >= 1:
                    stats_sqv(0, i - 1)
            stats_sqv(0, 15)
            k.wait(scalar, "t", ("sttr", 0))
            scalar.copy(mrT[:], p_bc[0][0:16, 0:128])
            k.inc(scalar.copy(rsT[:], p_bc[1][0:16, 0:128]), "s", ("stev", 0))
            k.wait(scalar, "drow", ("woo",))
            k.inc(scalar.activation(woos[:], s64row[0:64, :], AF.Copy,
                                    scale=smix[0:64, 0:1]), "s", ("woos",))
            k.wait(scalar, "t", ("bc", 0))
            scalar.copy(mrB[0][:], p_bc[0][:])
            k.inc(scalar.copy(rsB[0][:], p_bc[1][:]), "s", ("bcev", 0))
            def wsigns(w, clist):
                for c in clist:
                    if w == 0:
                        stream, idx = "w1b", (c // NE, c % NE)
                    else:
                        stream, idx = "w2b", c
                    u = (stream, idx)
                    k.wait(scalar, f"dq{USE_POS[u] % NPOOL}", u)
                    if c == 0:
                        k.wait(scalar, "v", ("thrc",))
                    if c >= 2:
                        pk = ("w1t8", (c - 2) // NE, (c - 2) % NE) if w == 0 \
                            else ("w2t8", c - 2)
                        k.wait(scalar, "v", pk)
                    scalar.activation(mposb[c % 2][:], pool[POOLBUF[u]][:],
                                      AF.Sign, bias=nthrcol[:, w:w + 1])
                    k.inc(scalar.activation(mnegb[c % 2][:], pool[POOLBUF[u]][:],
                                            AF.Sign, bias=thrcol[:, w:w + 1]),
                          "s", ("sgn2", w, c))

            wsigns(0, list(range(16)))
            for f in range(NF):
                k.wait(scalar, "t", ("mm1", 0, f))
                k.inc(scalar.activation(hT[:, f * T:(f + 1) * T], p_mm[f % 2][:],
                                        AF.Gelu, bias=sb1[:, f:f + 1]),
                      "s", ("gelu", 0, f))
                if f < 16:
                    wsigns(0, [16 + f])
            k.wait(scalar, "t", ("bc", 1))
            k.wait(scalar, "v", ("xn", 0, 4))
            k.wait(scalar, "v", ("xn", 0, 5))
            scalar.copy(mrB[1][:], p_bc[0][:])
            k.inc(scalar.copy(rsB[1][:], p_bc[1][:]), "s", ("bcev", 1))
            wsigns(1, list(range(32)))
            for e2 in range(NE):
                for i in (2 * e2, 2 * e2 + 1):
                    stats_acc(1, i)
                    if i >= 1:
                        stats_sqv(1, i - 1)
                k.wait(scalar, "t", ("mm2", 0, e2))
                scalar.activation(gtb[:, e2 * T:(e2 + 1) * T],
                                  p_mm[e2 % 2][:], AF.Silu,
                                  bias=sb2[:, e2:e2 + 1])
                k.inc(scalar.activation(gtb[:, e2 * T:(e2 + 1) * T],
                                        gtb[:, e2 * T:(e2 + 1) * T],
                                        AF.Copy, scale=snw[:, e2:e2 + 1]),
                      "s", ("silu", 0, e2))
            stats_sqv(1, 15)
            k.wait(scalar, "t", ("sttr", 1))
            scalar.copy(mrT[:], p_ms[0:16, 0:128])
            k.inc(scalar.copy(rsT[:], p_rstd[0:16, 0:128]), "s", ("stev", 1))
            k.wait(scalar, "v", ("spvqi",))
            k.inc(scalar.square(vqq[:], vqi[:]), "s", ("spvqsq",))
            k.inc(scalar.square(csq[:], scb[:]), "s", ("spcbsq",))
            k.wait(scalar, "v", ("spvnr",))
            k.inc(scalar.activation(vnorm[:], vnorm[:], AF.Sqrt,
                                    bias=scpool[0:2, 2:3]), "s", ("spvqn",))
            k.wait(scalar, "v", ("spcn",))
            k.inc(scalar.activation(cnorm[:], cnorm[:], AF.Sqrt,
                                    bias=scpool[:, 2:3]), "s", ("spcbn",))

            def sq_t(t):
                for e in range(NE):
                    k.wait(scalar, "v", ("a", t, e))
                    if e >= 1:
                        k.wait(scalar, "t", ("ms", t, e - 1))
                    elif t > 0:
                        k.wait(scalar, "t", ("ms", t - 1, 7))
                    k.inc(scalar.square(sqb[0][:],
                                        a_all[:, e * T:(e + 1) * T]),
                          "s", ("sq", t, e))

            def sqrt_t(t):
                k.wait(scalar, "t", ("ms", t, NE - 1))
                if t > 0:
                    k.wait(scalar, "v", ("recip", t - 1))
                k.inc(scalar.activation(rms_row[:], p_ms[0:1, :], AF.Sqrt,
                                        bias=scpool[0:1, 0:1], scale=LN_INV),
                      "s", ("sqrt", t))

            def rstdev_t(t):
                k.wait(scalar, "t", ("rstdbc", t))
                if t >= 1:
                    k.wait(scalar, "v", ("sg", t - 1, 6))
                    k.wait(scalar, "p", ("sg", t - 1, 7))
                k.inc(scalar.copy(rstdB[:], p_rstd[:]), "s", ("rstdev", t))

            def silu_t(t):
                for e2 in range(NE):
                    k.wait(scalar, "t", ("mm2", t, e2))
                    k.wait(scalar, sg_sem(e2), ("sg", t - 1, e2))
                    scalar.activation(gtb[:, e2 * T:(e2 + 1) * T],
                                      p_mm[e2 % 2][:], AF.Silu,
                                      bias=sb2[:, e2:e2 + 1])
                    k.inc(scalar.activation(gtb[:, e2 * T:(e2 + 1) * T],
                                            gtb[:, e2 * T:(e2 + 1) * T],
                                            AF.Copy, scale=snw[:, e2:e2 + 1]),
                          "s", ("silu", t, e2))

            def gelu_t(t):
                for f in range(NF):
                    k.wait(scalar, "t", ("mm1", t, f))
                    if f == 0:
                        k.wait(scalar, "t", ("mm2", t - 1, 7))
                    k.inc(scalar.activation(hT[:, f * T:(f + 1) * T],
                                            p_mm[f % 2][:], AF.Gelu,
                                            bias=sb1[:, f:f + 1]),
                          "s", ("gelu", t, f))

            def bcev_t(t):
                k.wait(scalar, "t", ("bc", t))
                if t >= 2:
                    k.wait(scalar, xn_sem(t - 2, 6), ("xn", t - 2, 6))
                    k.wait(scalar, xn_sem(t - 2, 7), ("xn", t - 2, 7))
                scalar.copy(mrB[t % 2][:], p_bc[0][:])
                k.inc(scalar.copy(rsB[t % 2][:], p_bc[1][:]), "s", ("bcev", t))

            gelu_t(1)
            bcev_t(2)
            sq_t(0)
            sqrt_t(0)
            rstdev_t(0)
            silu_t(1)
            sq_t(1)
            sqrt_t(1)
            rstdev_t(1)
            for t in range(2, NT):
                gelu_t(t)
                if t + 1 < NT:
                    bcev_t(t + 1)
                sq_t(t)
                sqrt_t(t)
                rstdev_t(t)
                silu_t(t)
        k.body(blk, "scalar", scalar_body)

        def tensor_body(tensor):
            k.wait(tensor, "dcs", ("call",))
            k.wait(tensor, "v", ("thrc0", 0))
            k.inc(tensor.transpose(p_bc[0][0:1, 0:128], thrcomb[:, 0:1],
                                   sident[:]), "t", ("thrtr",))
            k.wait(tensor, "v", ("thrc0", 1))
            k.inc(tensor.transpose(p_bc[1][0:1, 0:128], thrcomb[:, 1:2],
                                   sident[:]), "t", ("thrtr2",))

            def hm_i(b, i):
                hb = [p_mm[0], p_mm[1]] if b == 0 else [p_lie[0], p_lie[1]]
                u = ("xt", (b, i))
                k.wait(tensor, f"dq{USE_POS[u] % NPOOL}", u)
                k.wait(tensor, "v", ("rsc", b, i))
                for q in range(2):
                    mm = tensor.matmul(hb[q][0:1, :],
                                       lhsT=rscols[:, i:i + 1],
                                       rhs=pool[POOLBUF[u]][:, q * T:(q + 1) * T],
                                       start=(i == 0), stop=(i == 15))
                k.inc(mm, "t", ("hm", b, i))

            def smr_b(b):
                k.wait(tensor, "v", ("mrsum", b))
                tensor.matmul(p_ms[0:1, 20:21], lhsT=sonescf[:],
                              rhs=thrpart[:, 0:1], start=True, stop=True)
                mm = tensor.matmul(p_ms[0:1, 22:23], lhsT=sonescf[:],
                                   rhs=thrpart[:, 0:1], start=True, stop=True)
                k.inc(mm, "t", ("smr", b))

            def stats_tr(b):
                k.wait(tensor, "v", ("rsc", b, 15))
                tr = [p_bc[0], p_bc[1]] if b == 0 else [p_ms, p_rstd]
                if b == 0:
                    k.wait(tensor, "v", ("ts0",))
                    k.wait(tensor, "v", ("thrsums",))
                else:
                    k.wait(tensor, "v", ("hmvv", 0))
                    k.wait(tensor, "v", ("smre", 1))
                k.inc(tensor.transpose(tr[0][0:16, 0:128], mrcols[:], sident[:]),
                      "t", ("sttr0", b))
                k.inc(tensor.transpose(tr[1][0:16, 0:128], rscols[:], sident[:]),
                      "t", ("sttr", b))

            def stats_b(b):
                for i in range(16):
                    hm_i(b, i)
                smr_b(b)
                stats_tr(b)

            def hmv_b(b):
                k.wait(tensor, "v", ("hmev", b))
                if b == 1:
                    k.wait(tensor, "s", ("stev", 1))
                for e in range(NE):
                    hsrc = rms_row if e < 4 else rstd_row
                    mm = tensor.matmul(p_ms[:, (2 * e + b):(2 * e + b + 1)],
                                       lhsT=hsrc[0:1, (e % 4) * 128:(e % 4 + 1) * 128],
                                       rhs=sonesr[0:1, 0:1], start=True,
                                       stop=True)
                k.inc(mm, "t", ("hmv", b))

            stats_b(0)
            hmv_b(0)
            k.wait(tensor, "v", ("thrrow",))
            k.wait(tensor, "s", ("stev", 0))
            k.inc(tensor.matmul(p_bc[0][:, 0:2], lhsT=sonesr[:], rhs=thrrow[:],
                                start=True, stop=True), "t", ("thrbc",))

            def bc_t(t):
                k.wait(tensor, "dst0", ("stm", t))
                k.wait(tensor, "dst1", ("str", t))
                if t == 0:
                    k.wait(tensor, "v", ("thrc",))
                if t == 2:
                    k.wait(tensor, "v", ("wlie", 1))
                if t > 0:
                    for e in range(NE):
                        k.wait(tensor, xn_sem(t - 1, e), ("xn", t - 1, e))
                tensor.matmul(p_bc[0][:], lhsT=sonesr[:], rhs=mrbuf[0][:],
                              start=True, stop=True)
                mm = tensor.matmul(p_bc[1][:], lhsT=sonesr[:], rhs=rsbuf[0][:],
                                   start=True, stop=True)
                k.inc(mm, "t", ("bc", t))

            def vq_path():
                k.wait(tensor, "v", ("hmvv", 1))
                k.wait(tensor, "drow", ("wvi",))
                for e in range(NE):
                    mm = tensor.matmul(p_ms[0:2, 0:64],
                                       lhsT=hmv[:, 2 * e:2 * e + 2],
                                       rhs=s64row[:, e * 64:(e + 1) * 64],
                                       start=(e == 0), stop=(e == 7))
                k.inc(mm, "t", ("spvq",))
                for c in range(2):
                    k.wait(tensor, "v", ("spncb", c))
                    dst = p_rstd if c == 0 else p_ms
                    if c == 1:
                        k.wait(tensor, "v", ("spncbtev", 0))
                    k.inc(tensor.transpose(dst[0:64, 0:128],
                                           sncb[:, c * 64:(c + 1) * 64],
                                           sident[:]), "t", ("spncbt", c))
                k.wait(tensor, "v", ("spnvq",))
                k.wait(tensor, "v", ("hmev", 1))
                k.inc(tensor.transpose(p_lie[0][0:64, 0:2], nvq[:],
                                       sident[0:2, 0:2]), "t", ("spnvqt",))
                k.wait(tensor, "v", ("spncbtev", 1))
                k.wait(tensor, "v", ("spnvqte",))
                k.inc(tensor.matmul(p_rstd[0:2, 0:256], lhsT=nvqT[:], rhs=ncbT[:],
                                    start=True, stop=True), "t", ("spsc",))
                k.wait(tensor, "v", ("spmx",))
                k.inc(tensor.transpose(p_ms[0:1, 0:2], mxr[:],
                                       sident[0:2, 0:2]), "t", ("spmxt",))
                k.wait(tensor, "v", ("spmxte",))
                k.inc(tensor.matmul(p_rstd[:, 0:2], lhsT=sonesr[:], rhs=mxT[:],
                                    start=True, stop=True), "t", ("spmb",))
                for c in range(2):
                    k.inc(tensor.transpose(p_lie[0][:, c * 2:c * 2 + 2],
                                           scoresR[:, c * 128:(c + 1) * 128],
                                           sident[0:2, 0:2]), "t", ("spsct", c))
                k.wait(tensor, "v", ("spoh", 1))
                for c in range(2):
                    mm = tensor.matmul(p_ms[0:64, 0:2],
                                       lhsT=scb[:, c * 64:(c + 1) * 64],
                                       rhs=soh[:, c * 2:c * 2 + 2],
                                       start=(c == 0), stop=(c == 1))
                k.inc(mm, "t", ("spvc",))

            def omega_h(h):
                for e in range(NE):
                    k.wait(tensor, f"dq{USE_POS[('wtr', (h, e))] % NPOOL}",
                           ("wtr", (h, e)))
                    if e == 0:
                        k.wait(tensor, "v", ("hmvv", 1))
                        k.wait(tensor, "v", ("hmev", 1))
                        if h == 0:
                            k.wait(tensor, "v", ("spoh", 0))
                            k.wait(tensor, "v", ("spnvqte",))
                        else:
                            k.wait(tensor, "v", ("omev", h - 1, 1))
                    buf = pool[POOLBUF[("wtr", (h, e))]]
                    for q in range(2):
                        mm = tensor.matmul(p_lie[q][0:2, :],
                                           lhsT=hmv[:, 2 * e:2 * e + 2],
                                           rhs=buf[:, q * T:(q + 1) * T],
                                           start=(e == 0), stop=(e == 7))
                    k.inc(mm, "t", ("om", h, e))

            def vq_rows():
                k.wait(tensor, "v", ("spvce",))
                k.wait(tensor, "drow", ("wvo",))
                k.wait(tensor, "v", ("omev", 3, 1))
                for s_ in range(2):
                    k.inc(tensor.matmul(p_lie[s_][0:2, :], lhsT=vqcT[:],
                                        rhs=s64row[0:64, s_ * T:(s_ + 1) * T],
                                        start=True, stop=True),
                          "t", ("spvcr", s_))
                for s_ in range(2):
                    k.wait(tensor, "v", ("spc2", s_))
                    for e4 in range(4):
                        e = s_ * 4 + e4
                        dst = p_ms if e % 2 == 0 else p_rstd
                        if e >= 2:
                            k.wait(tensor, "v", ("spcpp", e - 2))
                        k.inc(tensor.transpose(dst[:, 0:2],
                                               booh[:, e4 * 128:(e4 + 1) * 128],
                                               sident[0:2, 0:2]),
                              "t", ("spct", e))

            def g_pe(b):
                k.wait(tensor, "v", ("spomu", b))
                k.wait(tensor, "s", ("bcev", 1))
                k.inc(tensor.transpose(p_bc[0][0:64, 0:64],
                                       omu[:, b * 64:(b + 1) * 64],
                                       sident[0:64, 0:64]), "t", ("sptr", b))
                k.wait(tensor, "v", ("spomb", b))
                k.inc(tensor.matmul(p_bc[1][0:64, 0:64],
                                    lhsT=omg[:, b * 64:(b + 1) * 64],
                                    rhs=omg[:, b * 64:(b + 1) * 64],
                                    start=True, stop=True), "t", ("spm2", b))
                k.wait(tensor, "v", ("spm2e", b))
                k.inc(tensor.matmul(p_bc[0][0:64, 0:64],
                                    lhsT=omg[:, b * 64:(b + 1) * 64],
                                    rhs=m2s[:, b * 64:(b + 1) * 64],
                                    start=True, stop=True), "t", ("spm3", b))
                k.wait(tensor, "v", ("spG", b))
                k.inc(tensor.transpose(p_bc[1][0:64, 0:64], gtmp[:],
                                       sident[0:64, 0:64]), "t", ("gbt", b))
                k.wait(tensor, "v", ("gbte", b))
                k.wait(tensor, "s", ("woos",))
                for q in range(2):
                    mm = tensor.matmul(p_bc[q][0:64, :],
                                       lhsT=GbT[:, b * 64:(b + 1) * 64],
                                       rhs=woos[:, q * T:(q + 1) * T],
                                       start=True, stop=True)
                k.inc(mm, "t", ("wliem", b))

            def lie_t(t):
                b = t // TPB
                for e in range(NE):
                    if e == 0:
                        k.wait(tensor, "v", ("wlie", b))
                        k.wait(tensor, xn_sem(t, 0), ("xn", t, 0))
                        if t == 0:
                            k.wait(tensor, "v", ("spcpp", 7))
                        if t > 0:
                            k.wait(tensor, "v", ("a", t - 1, 6))
                    elif e == 1 and t > 0:
                        k.wait(tensor, "v", ("a", t - 1, 7))
                    elif e >= 2:
                        k.wait(tensor, "v", ("a", t, e - 2))
                    k.inc(tensor.matmul(
                        p_lie[e % 2][:],
                        lhsT=wlie[:, b * EMB + e * 128: b * EMB + (e + 1) * 128],
                        rhs=xnt[t % 2][0:64, 0:T], start=True, stop=True),
                          "t", ("lie", t, e))

            def ms_t(t):
                for e in range(NE):
                    k.wait(tensor, "s", ("sq", t, e))
                    if e == 0:
                        if t > 0:
                            k.wait(tensor, "s", ("sqrt", t - 1))
                        else:
                            k.wait(tensor, "v", ("spcpp", 6))
                    k.inc(tensor.matmul(p_ms[0:1, 0:T], lhsT=sonescb[:],
                                        rhs=sqb[0][:],
                                        start=(e == 0), stop=(e == NE - 1)),
                          "t", ("ms", t, e))

            def rstd_t(t):
                k.wait(tensor, "v", ("recip", t))
                if t > 0:
                    k.wait(tensor, "s", ("rstdev", t - 1))
                else:
                    k.wait(tensor, "v", ("spcpp", 7))
                k.inc(tensor.matmul(p_rstd[:], lhsT=sonesr[:], rhs=rstd_row[:],
                                    start=True, stop=True), "t", ("rstdbc", t))

            AUX = {}

            def add_aux(key, *fns):
                AUX.setdefault(key, []).extend(fns)

            def run_aux(key):
                for fn in AUX.get(key, []):
                    fn()

            for g_ in range(6):
                def mk(g=g_):
                    def f():
                        hm_i(1, 2 * g)
                        hm_i(1, 2 * g + 1)
                    return f
                add_aux(("e2", 0, g_ + 2), mk())
            def hm_tail():
                for i in range(12, 16):
                    hm_i(1, i)
            add_aux(("f", 1, 0), hm_tail, lambda: smr_b(1),
                    lambda: stats_tr(1))
            add_aux(("f", 1, 1), lambda: hmv_b(1))
            add_aux(("f", 1, 2), vq_path)
            add_aux(("f", 1, 4), lambda: omega_h(0))
            add_aux(("f", 1, 8), lambda: omega_h(1))
            add_aux(("f", 1, 12), lambda: omega_h(2))
            add_aux(("f", 1, 16), lambda: omega_h(3))
            add_aux(("f", 1, 20), vq_rows)
            add_aux(("f", 1, 24), lambda: g_pe(0))
            add_aux(("f", 1, 28), lambda: g_pe(1))
            add_aux(("e2", 1, 0), lambda: lie_t(0))
            add_aux(("e2", 1, 2), lambda: ms_t(0), lambda: rstd_t(0))
            add_aux(("e2", 1, 4), lambda: lie_t(1))
            add_aux(("f", 2, 0), lambda: ms_t(1), lambda: rstd_t(1))

            bc_t(0)
            for t in range(NT):
                for f in range(NF):
                    run_aux(("f", t, f))
                    if t == 0 and f % 8 == 0:
                        k.wait(tensor, "v", ("w1t8", f // 8, NE - 1))
                    if f == 0:
                        for e in range(NE):
                            k.wait(tensor, xn_sem(t, e), ("xn", t, e))
                        if t == 0:
                            k.wait(tensor, "v", ("hmev", 0))
                        if t > 0:
                            k.wait(tensor, "s", ("silu", t - 1, 6))
                    elif f == 1:
                        if t == 0:
                            k.wait(tensor, "v", ("hmev", 0))
                        if t > 0:
                            k.wait(tensor, "s", ("silu", t - 1, 7))
                    else:
                        k.wait(tensor, "s", ("gelu", t, f - 2))
                    for e in range(NE):
                        mm = tensor.matmul(
                            p_mm[f % 2][:],
                            lhsT=w1t8[:, e * FF + f * 128: e * FF + (f + 1) * 128],
                            rhs=xnt[t % 2][:, e * T:(e + 1) * T],
                            start=(e == 0), stop=(e == NE - 1))
                    k.inc(mm, "t", ("mm1", t, f))
                run_aux(("f", t, NF))
                if t + 1 < NT:
                    bc_t(t + 1)
                if t >= 2:
                    lie_t(t)

                def mm2_group(e2):
                    run_aux(("e2", t, e2))
                    if e2 == 0:
                        k.wait(tensor, "s", ("gelu", t, NF - 1))
                        if t == 0:
                            k.wait(tensor, "v", ("w2t8", NF - 1))
                    elif e2 >= 2:
                        k.wait(tensor, "s", ("silu", t, e2 - 2))
                    for f in range(NF):
                        mm = tensor.matmul(
                            p_mm[e2 % 2][:],
                            lhsT=w2t8[:, f * EMB + e2 * 128: f * EMB + (e2 + 1) * 128],
                            rhs=hT[:, f * T:(f + 1) * T],
                            start=(f == 0), stop=(f == NF - 1))
                    k.inc(mm, "t", ("mm2", t, e2))

                if t >= 2:
                    mm2_group(0)
                    mm2_group(1)
                    ms_t(t)
                    rstd_t(t)
                    for e2 in range(2, NE):
                        mm2_group(e2)
                else:
                    for e2 in range(NE):
                        mm2_group(e2)
        k.body(blk, "tensor", tensor_body)


def build_program():
    nc = bass.Bass("TRN2", target_bir_lowering=False, debug=False)
    io = {}
    io["xTd"] = nc.dram_tensor("xTd", [NB * EMB, SEQ], F32, kind="ExternalInput").ap()
    io["xTokd"] = nc.dram_tensor("xTokd", [NB * SEQ, EMB], F32, kind="ExternalInput").ap()
    io["w1td"] = nc.dram_tensor("w1td", [EMB, FF], F32, kind="ExternalInput").ap()
    io["w2td"] = nc.dram_tensor("w2td", [FF, EMB], F32, kind="ExternalInput").ap()
    io["w1pred"] = nc.dram_tensor("w1pred", [128, FF], F32, kind="ExternalInput").ap()
    io["w2pred"] = nc.dram_tensor("w2pred", [512, EMB], F32, kind="ExternalInput").ap()
    io["wtriud"] = nc.dram_tensor("wtriud", [EMB, FF], F32, kind="ExternalInput").ap()
    io["btriud"] = nc.dram_tensor("btriud", [64, 64], F32, kind="ExternalInput").ap()
    io["wooTd"] = nc.dram_tensor("wooTd", [64, EMB], F32, kind="ExternalInput").ap()
    io["wviTd"] = nc.dram_tensor("wviTd", [128, 512], F32, kind="ExternalInput").ap()
    io["wvoTd"] = nc.dram_tensor("wvoTd", [64, EMB], F32, kind="ExternalInput").ap()
    io["cbd"] = nc.dram_tensor("cbd", [128, 128], F32, kind="ExternalInput").ap()
    io["b1d"] = nc.dram_tensor("b1d", [128, 32], F32, kind="ExternalInput").ap()
    io["b2d"] = nc.dram_tensor("b2d", [128, 8], F32, kind="ExternalInput").ap()
    io["nwd"] = nc.dram_tensor("nwd", [128, 8], F32, kind="ExternalInput").ap()
    io["bvi2d"] = nc.dram_tensor("bvi2d", [2, 64], F32, kind="ExternalInput").ap()
    io["bvo2d"] = nc.dram_tensor("bvo2d", [2, EMB], F32, kind="ExternalInput").ap()
    io["boo2d"] = nc.dram_tensor("boo2d", [2, EMB], F32, kind="ExternalInput").ap()
    io["mixd"] = nc.dram_tensor("mixd", [128, 2], F32, kind="ExternalInput").ap()
    io["identd"] = nc.dram_tensor("identd", [128, 128], F32, kind="ExternalInput").ap()
    io["onesrd"] = nc.dram_tensor("onesrd", [1, 128], F32, kind="ExternalInput").ap()
    io["onescbd"] = nc.dram_tensor("onescbd", [128, 1], BF16, kind="ExternalInput").ap()
    io["onescfd"] = nc.dram_tensor("onescfd", [128, 1], F32, kind="ExternalInput").ap()
    io["cpoold"] = nc.dram_tensor("cpoold", [128, 4], F32, kind="ExternalInput").ap()
    io["omdram"] = nc.dram_tensor("omdram", [NB, 4096], F32, kind="Internal").ap()
    io["stats_d"] = nc.dram_tensor("stats_d", [4, SEQ], F32, kind="Internal").ap()
    io["ccin"] = nc.dram_tensor("ccin", [1, 2], F32, kind="Internal").ap()
    io["ccout"] = nc.dram_tensor("ccout", [1, 2], F32, kind="Internal").ap()
    io["outTd"] = nc.dram_tensor("outTd", [NB * EMB, SEQ], F32, kind="ExternalOutput").ap()

    k1 = K(False)
    construct(k1, {n: Fake() for n in io})
    k2 = K(True, nc, M=k1.M)
    with k2.es:
        construct(k2, io)
    return nc


def kernel(L0_attn_x, L0_attn_omega_proj_w, L0_attn_omega_proj_b, L0_attn_omega_out_w,
           L0_attn_omega_out_b, L0_attn_vq_proj_in_w, L0_attn_vq_proj_in_b,
           L0_attn_vq_proj_out_w, L0_attn_vq_proj_out_b, L0_attn_codebook,
           L0_attn_omega_mix, L0_attn_vq_mix, L0_attn_norm_w, L0_ffn_W1, L0_ffn_b1,
           L0_ffn_W2, L0_ffn_b2):
    f32 = np.float32
    x = np.asarray(L0_attn_x, f32)
    xT = np.ascontiguousarray(x.transpose(0, 2, 1))
    w1t = np.ascontiguousarray(np.asarray(L0_ffn_W1, f32).T)
    w2t = np.ascontiguousarray(np.asarray(L0_ffn_W2, f32).T)

    opw = np.asarray(L0_attn_omega_proj_w, f32)
    opb = np.asarray(L0_attn_omega_proj_b, f32)
    R, C = np.triu_indices(64, k=1)
    wtriu = np.zeros((EMB, 64, 64), f32)
    wtriu[:, R, C] = opw.T
    wtriu = np.ascontiguousarray(wtriu.reshape(EMB, FF))
    btriu = np.zeros((64, 64), f32)
    btriu[R, C] = opb

    wooT = np.ascontiguousarray(np.asarray(L0_attn_omega_out_w, f32).T)
    wviT = np.asarray(L0_attn_vq_proj_in_w, f32).T
    wviT = np.ascontiguousarray(wviT.reshape(8, 128, 64).transpose(1, 0, 2).reshape(128, 512))
    wvoT = np.ascontiguousarray(np.asarray(L0_attn_vq_proj_out_w, f32).T)
    cb = np.asarray(L0_attn_codebook, f32)
    cbPP = np.ascontiguousarray(cb.reshape(2, 128, 64).transpose(1, 0, 2).reshape(128, 128))
    b1PP = np.ascontiguousarray(np.asarray(L0_ffn_b1, f32).reshape(32, 128).T)
    b2PP = np.ascontiguousarray(np.asarray(L0_ffn_b2, f32).reshape(8, 128).T)
    nwPP = np.ascontiguousarray(np.asarray(L0_attn_norm_w, f32).reshape(8, 128).T)
    bvi2 = np.ascontiguousarray(np.broadcast_to(np.asarray(L0_attn_vq_proj_in_b, f32), (2, 64)))
    bvo2 = np.ascontiguousarray(np.broadcast_to(np.asarray(L0_attn_vq_proj_out_b, f32), (2, EMB)))
    boo2 = np.ascontiguousarray(np.broadcast_to(np.asarray(L0_attn_omega_out_b, f32), (2, EMB)))
    mixd = np.empty((128, 2), f32)
    mixd[:, 0] = np.float32(L0_attn_omega_mix)
    mixd[:, 1] = np.float32(L0_attn_vq_mix)
    identd = np.eye(128, dtype=f32)
    onesrd = np.ones((1, 128), f32)
    onescbd = np.ones((128, 1), ml_dtypes.bfloat16)
    onescfd = np.ones((128, 1), np.float32)
    cpoold = np.broadcast_to(np.array([1e-6, 1e-5, 0.0, 0.0], f32), (128, 4)).copy()

    shared = {
        "w1td": w1t, "w2td": w2t, "wtriud": wtriu, "btriud": btriu, "wooTd": wooT,
        "wviTd": wviT, "wvoTd": wvoT, "cbd": cbPP, "b1d": b1PP, "b2d": b2PP,
        "nwd": nwPP, "bvi2d": bvi2, "bvo2d": bvo2, "boo2d": boo2, "mixd": mixd,
        "identd": identd, "onesrd": onesrd, "onescbd": onescbd,
        "onescfd": onescfd, "cpoold": cpoold,
    }
    in_maps = []
    for c in range(NCORES):
        m = dict(shared)
        m["xTd"] = np.ascontiguousarray(xT[2 * c:2 * c + 2].reshape(NB * EMB, SEQ))
        m["xTokd"] = np.ascontiguousarray(x[2 * c:2 * c + 2].reshape(NB * SEQ, EMB))
        m["w1pred"] = np.ascontiguousarray(w1t[c * 128:(c + 1) * 128, :])
        m["w2pred"] = np.ascontiguousarray(w2t[c * 512:(c + 1) * 512, :])
        in_maps.append(m)

    nc = build_program()
    res = run_bass_kernel_spmd(nc, in_maps, core_ids=list(range(NCORES)))
    outs = []
    for c in range(NCORES):
        o = np.asarray(res.results[c]["outTd"], f32).reshape(NB, EMB, SEQ)
        outs.append(o.transpose(0, 2, 1))
    return np.ascontiguousarray(np.concatenate(outs, axis=0))


# version 14
# speedup vs baseline: 1.5561x; 1.0426x over previous
import numpy as np
import ml_dtypes
from contextlib import ExitStack, contextmanager

import concourse.bass as bass
import concourse.mybir as mybir
from concourse.bass_utils import run_bass_kernel_spmd
from concourse.alu_op_type import AluOpType as ALU

F32 = mybir.dt.float32
BF16 = mybir.dt.bfloat16
FP8 = mybir.dt.float8e4
AF = mybir.ActivationFunctionType
AX = mybir.AxisListType

EMB = 1024
FF = 4096
SEQ = 2048
NB = 2
NE = 8
NF = 32
T = 512
NT = 8
TPB = 4
NCORES = 8
CH = 1024
MEAN_W = float(1.0 / (FF * EMB))
LN_INV = float(1.0 / EMB)
SQ_SCALE = float(1.0 / 32.0)
SEQ_INV = float(1.0 / SEQ)
NPOOL = 3


class Fake:
    def __getattr__(self, n):
        return self

    def __call__(self, *a, **k):
        return self

    def __getitem__(self, i):
        return self

    def __enter__(self):
        return self

    def __exit__(self, *a):
        return False


class K:

    def __init__(self, emit, nc=None, M=None):
        self.emit = emit
        self.nc = nc
        self.cnt = {}
        self.M = M if M is not None else {}
        self.sems = {}
        self.es = ExitStack()

    def sem(self, name):
        self.cnt[name] = 0
        if self.emit:
            self.sems[name] = self.es.enter_context(self.nc.semaphore(name))

    def inc(self, instr, s, key=None, n=1):
        self.cnt[s] += n
        if key is not None:
            if not self.emit:
                self.M[(s,) + tuple(key)] = self.cnt[s]
        if self.emit and instr is not None:
            instr.then_inc(self.sems[s], n)

    def wait(self, eng, s, key):
        if self.emit:
            eng.wait_ge(self.sems[s], self.M[(s,) + tuple(key)])

    def dma(self, eng, semname, key, out, in_, chain=True):
        cur = self.cnt[semname]
        if self.emit and chain and cur:
            eng.wait_ge(self.sems[semname], cur)
        instr = eng.dma_start(out=out, in_=in_) if self.emit else None
        self.inc(instr, semname, key, 16)

    def sb(self, name, shape, dtype=F32):
        if self.emit:
            return self.es.enter_context(self.nc.sbuf_tensor(name, shape, dtype))
        return Fake()

    def ps(self, name, shape, dtype=F32):
        if self.emit:
            return self.es.enter_context(self.nc.psum_tensor(name, shape, dtype))
        return Fake()

    @contextmanager
    def block(self, name):
        if self.emit:
            with self.nc.Block(name) as b:
                yield b
        else:
            yield Fake()

    def body(self, blk, engname, fn):
        if self.emit:
            getattr(blk, engname)(fn)
        else:
            fn(Fake())


def xn_sem(t, e):
    return "v" if (t < 2 or e % 2 == 0) else "p"


def sg_sem(e2):
    return "v" if e2 % 2 == 0 else "p"


def construct(k: "K", io):
    CONSTS = ["identd", "onesrd", "onescbd", "onescfd", "cpoold", "b1d",
              "b2d", "nwd", "mixd", "cbd", "btriud", "bvi2d"]
    for s in ("t", "v", "s", "p", "cc"):
        k.sem(s)
    dma_sems = ["dcs", "dq0", "dq1", "dq2", "dxl0", "dxl1", "dxr0",
                "dxr1", "do0", "do1", "dst0", "dst1", "dsw", "dow0", "dor",
                "dcc", "drow"]
    for s in dma_sems:
        k.sem(s)

    w1t8 = k.sb("w1t8", [128, NE * FF], FP8)
    w2t8 = k.sb("w2t8", [128, NF * EMB], FP8)
    hT = k.sb("hT", [128, NF * T], BF16)
    xnt = [k.sb(f"xnt{i}", [128, NE * T], BF16) for i in range(2)]
    a_all = k.sb("a_all", [128, NE * T], BF16)
    gtb = k.sb("gtb", [128, NE * T], BF16)
    sqb = [k.sb("sqb0", [128, T], BF16)]
    outb = [k.sb(f"outb{i}", [128, T]) for i in range(2)]
    xlnb = [k.sb(f"xlnb{i}", [128, T]) for i in range(2)]
    xresb = [k.sb(f"xresb{i}", [128, T]) for i in range(2)]
    mrB = [k.sb(f"mrB{i}", [128, T]) for i in range(2)]
    rsB = [k.sb(f"rsB{i}", [128, T]) for i in range(2)]
    rstdB = k.sb("rstdB", [128, T])
    mrbuf = [k.sb("mrbuf0", [1, T])]
    rsbuf = [k.sb("rsbuf0", [1, T])]
    rms_row = k.sb("rms_row", [1, T])
    rstd_row = k.sb("rstd_row", [1, T])
    pool = [k.sb(f"pool{i}", [128, CH]) for i in range(NPOOL)]
    mposb = [k.sb(f"mposb{i}", [128, CH], BF16) for i in range(2)]
    mnegb = [k.sb(f"mnegb{i}", [128, CH], BF16) for i in range(2)]
    mcols = k.sb("mcols", [128, 16])
    rscols = k.sb("rscols", [128, 16])
    mrcols = k.sb("mrcols", [128, 16])
    sqcols = k.sb("sqcols", [128, 16])
    varcols = k.sb("varcols", [128, 16])
    mrT = k.sb("mrT", [16, 128])
    rsT = k.sb("rsT", [16, 128])
    hmv = k.sb("hmv", [128, 16])
    thrpart = k.sb("thrpart", [128, 8])
    thrcomb = k.sb("thrcomb", [128, 2])
    thrsums = k.sb("thrsums", [1, 2])
    thrsumg = k.sb("thrsumg", [1, 2])
    thrrow = k.sb("thrrow", [1, 2])
    thrcol = k.sb("thrcol", [128, 2])
    nthrcol = k.sb("nthrcol", [128, 2])
    sident = k.sb("sident", [128, 128])
    sonesr = k.sb("sonesr", [1, 128])
    sonescb = k.sb("sonescb", [128, 1], BF16)
    sonescf = k.sb("sonescf", [128, 1])
    scpool = k.sb("scpool", [128, 4])
    sb1 = k.sb("sb1", [128, 32])
    sb2 = k.sb("sb2", [128, 8])
    snw = k.sb("snw", [128, 8])
    smix = k.sb("smix", [128, 2])
    scb = k.sb("scb", [128, 128])
    sbtriu = k.sb("sbtriu", [64, 64])
    sbvi2 = k.sb("sbvi2", [2, 64])
    s64row = k.sb("s64row", [128, EMB])
    woos = k.sb("woos", [64, EMB], BF16)
    omu = k.sb("omu", [64, NB * 64])
    omg = k.sb("omg", [64, NB * 64])
    m2s = k.sb("m2s", [64, NB * 64])
    m3s = k.sb("m3s", [64, NB * 64])
    gtmp = k.sb("gtmp", [64, 64])
    GbT = k.sb("GbT", [64, NB * 64], BF16)
    wlie = k.sb("wlie", [64, NB * EMB], BF16)
    vqi = k.sb("vqi", [2, 64])
    vqq = k.sb("vqq", [2, 64])
    vnorm = k.sb("vnorm", [2, 1])
    nvq = k.sb("nvq", [2, 64])
    nvqT = k.sb("nvqT", [64, 2])
    csq = k.sb("csq", [128, 128])
    cnorm = k.sb("cnorm", [128, 2])
    sncb = k.sb("sncb", [128, 128])
    ncbT = k.sb("ncbT", [64, 256])
    scoresR = k.sb("scoresR", [2, 256])
    mxr = k.sb("mxr", [2, 1])
    mxT = k.sb("mxT", [1, 2])
    scT = k.sb("scT", [128, 4])
    soh = k.sb("soh", [128, 4])
    vqcT = k.sb("vqcT", [64, 2])
    bvoh = k.sb("bvoh", [2, T])
    booh = k.sb("booh", [2, T])
    constPP = k.sb("constPP", [128, 16])

    p_mm = [k.ps(f"p_mm{i}", [128, T]) for i in range(2)]
    p_lie = [k.ps(f"p_lie{i}", [128, T]) for i in range(2)]
    p_ms = k.ps("p_ms", [128, T])
    p_rstd = k.ps("p_rstd", [128, T])
    p_bc = [k.ps(f"p_bc{i}", [128, T]) for i in range(2)]

    def src_of(stream, idx):
        if stream == "w1s":
            return io["w1pred"][:, idx * CH:(idx + 1) * CH]
        if stream == "w2s":
            return io["w2pred"][idx * 128:(idx + 1) * 128, :]
        if stream == "xt":
            b, i = idx
            return io["xTokd"][b * SEQ + i * 128: b * SEQ + (i + 1) * 128, :]
        if stream == "w1b":
            h, e = idx
            return io["w1td"][e * 128:(e + 1) * 128, h * CH:(h + 1) * CH]
        if stream == "w2b":
            return io["w2td"][idx * 128:(idx + 1) * 128, :]
        if stream == "wtr":
            h, e = idx
            return io["wtriud"][e * 128:(e + 1) * 128, h * CH:(h + 1) * CH]
        raise KeyError(stream)

    def release_of(stream, idx):
        if stream in ("w1s", "w2s"):
            w = 0 if stream == "w1s" else 1
            return ("v", ("wred", w, idx))
        if stream == "xt":
            return ("t", ("hm",) + idx)
        if stream == "w1b":
            h, e = idx
            return ("s", ("sgn2", 0, h * NE + e))
        if stream == "w2b":
            return ("s", ("sgn2", 1, idx))
        if stream == "wtr":
            return ("t", ("om",) + idx)
        raise KeyError(stream)

    uses = []
    for j in range(4):
        uses.append(("w1s", j))
    for j in range(4):
        uses.append(("w2s", j))
    for i in range(16):
        uses.append(("xt", (0, i)))
    for h in range(4):
        for e in range(NE):
            uses.append(("w1b", (h, e)))
    for F in range(NF):
        uses.append(("w2b", F))
    for i in range(16):
        uses.append(("xt", (1, i)))
    for h in range(4):
        for e in range(NE):
            uses.append(("wtr", (h, e)))
    POOLBUF = {u: n % NPOOL for n, u in enumerate(uses)}
    USE_POS = {u: n for n, u in enumerate(uses)}

    def x_src(t, e):
        b, i = t // TPB, t % TPB
        return io["xTd"][b * EMB + e * 128: b * EMB + (e + 1) * 128,
                         i * T:(i + 1) * T]

    def out_dst(t, e):
        b, i = t // TPB, t % TPB
        return io["outTd"][b * EMB + e * 128: b * EMB + (e + 1) * 128,
                           i * T:(i + 1) * T]

    def stats_src(t, row):
        b, i = t // TPB, t % TPB
        return io["stats_d"][b * 2 + row: b * 2 + row + 1, i * T:(i + 1) * T]

    def xln_prev(t, e):
        if e >= 2:
            return (t, e - 2)
        if t > 0:
            return (t - 1, e + 6)
        return None

    with k.block("main") as blk:
        def sync_body(sync):
            cdst = {"identd": sident, "onesrd": sonesr, "onescbd": sonescb,
                    "cpoold": scpool, "b1d": sb1, "b2d": sb2, "nwd": snw,
                    "mixd": smix, "cbd": scb, "btriud": sbtriu,
                    "bvi2d": sbvi2, "onescfd": sonescf}
            for n in CONSTS:
                k.dma(sync, "dcs", ("c", n), out=cdst[n][:], in_=io[n],
                      chain=False)
            k.inc(None, "dcs", ("call",), 0)
            k.dma(sync, "drow", ("woo",), out=s64row[0:64, :], in_=io["wooTd"])

            def fifo(u):
                stream, idx = u
                pos = USE_POS[u]
                if pos >= NPOOL:
                    ps_, pk = release_of(*uses[pos - NPOOL])
                    k.wait(sync, ps_, pk)
                k.dma(sync, f"dq{pos % NPOOL}", u, out=pool[POOLBUF[u]][:],
                      in_=src_of(stream, idx))

            for j in range(4):
                fifo(("w1s", j))
            for j in range(4):
                fifo(("w2s", j))
            k.wait(sync, "v", ("thrsums",))
            k.dma(sync, "dcc", ("ccw",), out=io["ccin"], in_=thrsums[:])
            for i in range(16):
                fifo(("xt", (0, i)))
            k.wait(sync, "cc", ("ccdone",))
            k.dma(sync, "dcc", ("ccr",), out=thrsumg[:], in_=io["ccout"])
            k.wait(sync, "s", ("stev", 0))
            k.dma(sync, "dsw", ("swb", 0, 0),
                  out=io["stats_d"][0:1, :].rearrange(
                      "one (c p) -> (one c) p", c=16),
                  in_=mrT[:])
            k.dma(sync, "dsw", ("swb", 0, 1),
                  out=io["stats_d"][1:2, :].rearrange(
                      "one (c p) -> (one c) p", c=16),
                  in_=rsT[:])
            k.wait(sync, "dsw", ("swb", 0, 1))
            k.dma(sync, "dst0", ("stm", 0), out=mrbuf[0][:], in_=stats_src(0, 0))
            k.dma(sync, "dst1", ("str", 0), out=rsbuf[0][:], in_=stats_src(0, 1))
            t0slots = [xlnb[0], xlnb[1], xresb[0], xresb[1], mrB[1], rsB[1],
                       rstdB, outb[1]]
            for e in range(NE):
                k.dma(sync, f"dxl{e % 2}", ("xln", 0, e),
                      out=t0slots[e][:], in_=x_src(0, e))
            for h in range(4):
                for e in range(NE):
                    fifo(("w1b", (h, e)))
            k.wait(sync, "t", ("bc", 0))
            for t_ in (1,):
                k.dma(sync, "dst0", ("stm", t_), out=mrbuf[0][:],
                      in_=stats_src(t_, 0))
                k.dma(sync, "dst1", ("str", t_), out=rsbuf[0][:],
                      in_=stats_src(t_, 1))
            k.wait(sync, "s", ("woos",))
            k.dma(sync, "drow", ("wvi",), out=s64row[:, 0:512], in_=io["wviTd"])
            for F in range(NF):
                fifo(("w2b", F))
                if F == 4:
                    for e in range(2):
                        pt, pe = xln_prev(1, e)
                        k.wait(sync, xn_sem(pt, pe), ("xn", pt, pe))
                        k.dma(sync, f"dxl{e % 2}", ("xln", 1, e),
                              out=xlnb[e % 2][:], in_=x_src(1, e))
                if F == 8:
                    k.wait(sync, "t", ("bc", 1))
                    for t_ in (2,):
                        k.dma(sync, "dst0", ("stm", t_), out=mrbuf[0][:],
                              in_=stats_src(t_, 0))
                        k.dma(sync, "dst1", ("str", t_), out=rsbuf[0][:],
                              in_=stats_src(t_, 1))
                if F == 12:
                    for e2 in (0, 1):
                        k.wait(sync, "v", ("xn", 0, e2 + 2))
                        k.dma(sync, f"dxr{e2 % 2}", ("xres", 0, e2),
                              out=xresb[e2 % 2][:], in_=x_src(0, e2))
            for i in range(16):
                fifo(("xt", (1, i)))
            for e in range(2, NE):
                pt, pe = xln_prev(1, e)
                k.wait(sync, xn_sem(pt, pe), ("xn", pt, pe))
                k.dma(sync, f"dxl{e % 2}", ("xln", 1, e),
                      out=xlnb[e % 2][:], in_=x_src(1, e))
            for e in range(2):
                pt, pe = xln_prev(2, e)
                k.wait(sync, xn_sem(pt, pe), ("xn", pt, pe))
                k.dma(sync, f"dxl{e % 2}", ("xln", 2, e),
                      out=xlnb[e % 2][:], in_=x_src(2, e))
            k.wait(sync, "s", ("stev", 1))
            k.dma(sync, "dsw", ("swb", 1, 0),
                  out=io["stats_d"][2:3, :].rearrange("one (c p) -> (one c) p", c=16),
                  in_=mrT[:])
            k.dma(sync, "dsw", ("swb", 1, 1),
                  out=io["stats_d"][3:4, :].rearrange("one (c p) -> (one c) p", c=16),
                  in_=rsT[:])
            k.dma(sync, "drow", ("bvo0a",), out=bvoh[:], in_=io["bvo2d"][:, 0:T])
            k.dma(sync, "drow", ("bvo", 0), out=booh[:], in_=io["boo2d"][:, 0:T])
            for h in range(4):
                for e in range(NE):
                    fifo(("wtr", (h, e)))
                for q in range(2):
                    k.wait(sync, "v", ("omev", h, q))
                    k.dma(sync, "dow0", ("omw", h, q),
                          out=io["omdram"][0:2, h * CH + q * T: h * CH + (q + 1) * T],
                          in_=outb[0][0:2, :])
            k.wait(sync, "t", ("spvq",))
            k.dma(sync, "drow", ("wvo",), out=s64row[0:64, :], in_=io["wvoTd"])
            k.wait(sync, "dow0", ("omw", 3, 1))
            for b in range(NB):
                rdsrc = io["omdram"][b:b + 1, :].rearrange(
                    "one (r c) -> (one r) c", r=64)
                k.dma(sync, "dor", ("omr", b), out=omu[:, b * 64:(b + 1) * 64],
                      in_=rdsrc)
            k.wait(sync, "v", ("spc2", 0))
            k.dma(sync, "drow", ("bvo1a",), out=bvoh[:], in_=io["bvo2d"][:, T:EMB])
            k.dma(sync, "drow", ("bvo", 1), out=booh[:], in_=io["boo2d"][:, T:EMB])
            for t in range(NT):
                if 2 <= t + 2 < NT:
                    erange = range(2, NE) if t + 2 == 2 else range(NE)
                    for e in erange:
                        pt, pe = xln_prev(t + 2, e)
                        k.wait(sync, xn_sem(pt, pe), ("xn", pt, pe))
                        k.dma(sync, f"dxl{e % 2}", ("xln", t + 2, e),
                              out=xlnb[e % 2][:], in_=x_src(t + 2, e))
                if t + 3 < NT:
                    k.wait(sync, "t", ("bc", t + 2))
                    if t + 3 == 4:
                        k.wait(sync, "dsw", ("swb", 1, 1))
                    k.dma(sync, "dst0", ("stm", t + 3), out=mrbuf[0][:],
                          in_=stats_src(t + 3, 0))
                    k.dma(sync, "dst1", ("str", t + 3), out=rsbuf[0][:],
                          in_=stats_src(t + 3, 1))
                for e2 in range(NE):
                    if not (t == 0 and e2 < 2):
                        pe = e2 - 2
                        pt = t
                        if pe < 0:
                            pe += 8
                            pt -= 1
                        k.wait(sync, sg_sem(pe), ("sg", pt, pe))
                        k.dma(sync, f"dxr{e2 % 2}", ("xres", t, e2),
                              out=xresb[e2 % 2][:], in_=x_src(t, e2))
                    if e2 >= 2:
                        k.wait(sync, sg_sem(e2 - 2), ("sg", t, e2 - 2))
                        k.dma(sync, f"do{e2 % 2}", ("od", t, e2 - 2),
                              out=out_dst(t, e2 - 2), in_=outb[e2 % 2][:])
                for e2 in (6, 7):
                    k.wait(sync, sg_sem(e2), ("sg", t, e2))
                    k.dma(sync, f"do{e2 % 2}", ("od", t, e2),
                          out=out_dst(t, e2), in_=outb[e2 % 2][:])
            k.wait(sync, "do0", ("od", NT - 1, NE - 2))
            k.wait(sync, "do1", ("od", NT - 1, NE - 1))
        k.body(blk, "sync", sync_body)

        def gp_body(gp):
            k.wait(gp, "dcs", ("call",))
            k.wait(gp, "dcc", ("ccw",))
            cci = gp.collective_compute(
                "AllReduce", ALU.add, replica_groups=[list(range(NCORES))],
                ins=[io["ccin"]], outs=[io["ccout"]]) if k.emit else None
            k.inc(cci, "cc", ("ccdone",))

            def tadd_gp(w, clist):
                for c in clist:
                    k.wait(gp, "s", ("sgn2", w, c))
                    k.inc(gp.tensor_tensor(mposb[c % 2][:], mposb[c % 2][:],
                                           mnegb[c % 2][:], op=ALU.add),
                          "p", ("tadd", w, c))

            tadd_gp(0, list(range(0, 32, 2)))
            tadd_gp(1, list(range(0, 32, 2)))
            for t in range(NT):
                for e2 in range(1, NE, 2):
                    k.wait(gp, "s", ("silu", t, e2))
                    k.wait(gp, "v", ("a", t, e2))
                    k.inc(gp.tensor_tensor(gtb[:, e2 * T:(e2 + 1) * T],
                                           gtb[:, e2 * T:(e2 + 1) * T],
                                           a_all[:, e2 * T:(e2 + 1) * T],
                                           op=ALU.mult), "p", ("prod", t, e2))
                t2 = t + 2
                if t2 < NT:
                    for e in range(1, NE, 2):
                        k.wait(gp, f"dxl{e % 2}", ("xln", t2, e))
                        k.wait(gp, "s", ("bcev", t2))
                        if e == 1 and t >= 1:
                            k.wait(gp, "do1", ("od", t - 1, 7))
                        gp.tensor_tensor(outb[1][:], xlnb[e % 2][:],
                                         mrB[t2 % 2][:], op=ALU.subtract)
                        k.inc(gp.tensor_tensor(xnt[t2 % 2][:, e * T:(e + 1) * T],
                                               outb[1][:], rsB[t2 % 2][:],
                                               op=ALU.mult), "p", ("xn", t2, e))
                for e2 in range(1, NE, 2):
                    k.wait(gp, "s", ("rstdev", t))
                    k.wait(gp, f"dxr{e2 % 2}", ("xres", t, e2))
                    if (t, e2) > (0, 1):
                        pe = e2 - 2
                        pt = t
                        if pe < 0:
                            pe += 8
                            pt -= 1
                        k.wait(gp, f"do{pe % 2}", ("od", pt, pe))
                    gp.tensor_tensor(outb[e2 % 2][:],
                                     gtb[:, e2 * T:(e2 + 1) * T],
                                     rstdB[:], op=ALU.mult)
                    k.inc(gp.tensor_tensor(outb[e2 % 2][:], outb[e2 % 2][:],
                                           xresb[e2 % 2][:], op=ALU.add),
                          "p", ("sg", t, e2))
        k.body(blk, "gpsimd", gp_body)

        def vector_body(vector):
            k.wait(vector, "dcs", ("call",))
            for w in range(2):
                for j in range(4):
                    stream = "w1s" if w == 0 else "w2s"
                    k.wait(vector, f"dq{USE_POS[(stream, j)] % NPOOL}", (stream, j))
                    k.inc(vector.reduce_sum(thrpart[:, w * 4 + j:w * 4 + j + 1],
                                            pool[POOLBUF[(stream, j)]][:],
                                            axis=AX.X, apply_absolute_value=True),
                          "v", ("wred", w, j))
            for w in range(2):
                k.inc(vector.reduce_sum(thrcomb[:, w:w + 1],
                                        thrpart[:, w * 4:w * 4 + 4], axis=AX.X),
                      "v", ("thrc0", w))
            k.wait(vector, "t", ("thrtr",))
            vector.tensor_copy(rms_row[0:1, 0:128], p_bc[0][0:1, 0:128])
            k.inc(vector.reduce_sum(thrsums[0:1, 0:1], rms_row[0:1, 0:128],
                                    axis=AX.X), "v", ("ts0",))
            k.wait(vector, "t", ("thrtr2",))
            vector.tensor_copy(rms_row[0:1, 0:128], p_bc[1][0:1, 0:128])
            k.inc(vector.reduce_sum(thrsums[0:1, 1:2], rms_row[0:1, 0:128],
                                    axis=AX.X), "v", ("thrsums",))

            def stat_chain(b, i):
                k.wait(vector, "s", ("acc", b, i))
                k.inc(vector.tensor_tensor(varcols[:, i:i + 1], sqcols[:, i:i + 1],
                                           varcols[:, i:i + 1], op=ALU.subtract),
                      "v", ("var", b, i))
                k.wait(vector, "s", ("sqv", b, i))
                k.inc(vector.reciprocal(rscols[:, i:i + 1], varcols[:, i:i + 1]),
                      "v", ("rscr", b, i))
                k.wait(vector, "v", ("rscr", b, i))
                k.inc(vector.tensor_tensor(mrcols[:, i:i + 1], mcols[:, i:i + 1],
                                           rscols[:, i:i + 1], op=ALU.mult),
                      "v", ("rsc", b, i))

            for i in range(16):
                stat_chain(0, i)
            vector.reduce_sum(thrpart[:, 0:1], mrcols[:], axis=AX.X)
            k.inc(vector.tensor_copy(thrcomb[:, 0:1], thrpart[:, 0:1]),
                  "v", ("mrsum", 0))
            k.wait(vector, "t", ("smr", 0))
            k.inc(vector.tensor_copy(mcols[0:1, 0:1], p_ms[0:1, 20:21]),
                  "v", ("smre", 0))
            k.wait(vector, "t", ("hm", 0, 15))
            vector.tensor_scalar(rms_row[:], p_mm[0][0:1, :],
                                 scalar1=mcols[0:1, 0:1], scalar2=None,
                                 op0=ALU.subtract)
            k.inc(vector.tensor_scalar(rstd_row[:], p_mm[1][0:1, :],
                                       scalar1=mcols[0:1, 0:1], scalar2=None,
                                       op0=ALU.subtract), "v", ("hmev", 0))
            k.wait(vector, "t", ("hmv", 0))
            k.inc(vector.tensor_scalar(
                hmv[:, 0:16].rearrange("p (e b) -> p b e", b=2)[:, 0, :],
                p_ms[:, 0:16].rearrange("p (e b) -> p b e", b=2)[:, 0, :],
                scalar1=SEQ_INV, scalar2=None, op0=ALU.mult), "v", ("hmvv", 0))
            k.wait(vector, "dcc", ("ccr",))
            k.inc(vector.tensor_scalar(thrrow[:], thrsumg[:], scalar1=MEAN_W,
                                       scalar2=1e-8, op0=ALU.mult, op1=ALU.max),
                  "v", ("thrrow",))
            k.wait(vector, "t", ("thrbc",))
            vector.tensor_scalar(thrcol[:], p_bc[0][:, 0:2], scalar1=0.5,
                                 scalar2=None, op0=ALU.mult)
            k.inc(vector.tensor_scalar(nthrcol[:], p_bc[0][:, 0:2], scalar1=-0.5,
                                       scalar2=None, op0=ALU.mult),
                  "v", ("thrc",))

            t0slots_v = [xlnb[0], xlnb[1], xresb[0], xresb[1], mrB[1], rsB[1],
                         rstdB, outb[1]]

            def xn_tile(t):
                for e in range(NE):
                    if t == 0:
                        k.wait(vector, f"dxl{e % 2}", ("xln", t, e))
                        src = t0slots_v[e]
                    else:
                        k.wait(vector, f"dxl{e % 2}", ("xln", t, e))
                        src = xlnb[e % 2]
                    k.wait(vector, "s", ("bcev", t))
                    vector.tensor_tensor(outb[0][:], src[:], mrB[t % 2][:],
                                         op=ALU.subtract)
                    k.inc(vector.tensor_tensor(xnt[t % 2][:, e * T:(e + 1) * T],
                                               outb[0][:], rsB[t % 2][:],
                                               op=ALU.mult), "v", ("xn", t, e))

            def tfin_v(w, clist):
                for c in clist:
                    if w == 0:
                        h, e = c // NE, c % NE
                        dst = w1t8[:, e * FF + h * CH: e * FF + (h + 1) * CH]
                        key = ("w1t8", h, e)
                    else:
                        dst = w2t8[:, c * EMB:(c + 1) * EMB]
                        key = ("w2t8", c)
                    if c % 2 == 0:
                        k.wait(vector, "p", ("tadd", w, c))
                    else:
                        k.wait(vector, "s", ("sgn2", w, c))
                        vector.tensor_tensor(mposb[c % 2][:], mposb[c % 2][:],
                                             mnegb[c % 2][:], op=ALU.add)
                    k.inc(vector.tensor_scalar(dst, mposb[c % 2][:], scalar1=0.5,
                                               scalar2=None, op0=ALU.mult),
                          "v", key)

            tfin_v(0, list(range(8)))
            xn_tile(0)
            tfin_v(0, list(range(8, 32)))
            tfin_v(1, list(range(32)))
            k.wait(vector, "t", ("sttr", 0))
            k.wait(vector, "v", ("hmev", 0))
            for i in range(16):
                stat_chain(1, i)
            xn_tile(1)
            vector.reduce_sum(thrpart[:, 0:1], mrcols[:], axis=AX.X)
            k.inc(vector.tensor_copy(thrcomb[:, 0:1], thrpart[:, 0:1]),
                  "v", ("mrsum", 1))
            k.wait(vector, "t", ("smr", 1))
            k.inc(vector.tensor_copy(mcols[0:1, 0:1], p_ms[0:1, 20:21]),
                  "v", ("smre", 1))
            k.wait(vector, "t", ("hm", 1, 15))
            k.wait(vector, "t", ("hmv", 0))
            vector.tensor_scalar(rms_row[:], p_lie[0][0:1, :],
                                 scalar1=mcols[0:1, 0:1], scalar2=None,
                                 op0=ALU.subtract)
            k.inc(vector.tensor_scalar(rstd_row[:], p_lie[1][0:1, :],
                                       scalar1=mcols[0:1, 0:1], scalar2=None,
                                       op0=ALU.subtract), "v", ("hmev", 1))
            k.wait(vector, "t", ("hmv", 1))
            k.inc(vector.tensor_scalar(
                hmv[:, 0:16].rearrange("p (e b) -> p b e", b=2)[:, 1, :],
                p_ms[:, 0:16].rearrange("p (e b) -> p b e", b=2)[:, 1, :],
                scalar1=SEQ_INV, scalar2=None, op0=ALU.mult), "v", ("hmvv", 1))

            k.wait(vector, "t", ("spvq",))
            k.inc(vector.tensor_tensor(vqi[:], p_ms[0:2, 0:64], sbvi2[:],
                                       op=ALU.add), "v", ("spvqi",))
            k.wait(vector, "s", ("spvqsq",))
            k.inc(vector.reduce_sum(vnorm[:], vqq[:], axis=AX.X), "v", ("spvnr",))
            k.wait(vector, "s", ("spvqn",))
            vector.tensor_scalar(vnorm[:], vnorm[:], scalar1=1e-12, scalar2=None,
                                 op0=ALU.max)
            k.inc(vector.reciprocal(vnorm[:], vnorm[:]), "v", ("spvrec",))
            k.wait(vector, "v", ("spvrec",))
            k.inc(vector.tensor_scalar(nvq[:], vqi[:], scalar1=vnorm[:],
                                       scalar2=None, op0=ALU.mult),
                  "v", ("spnvq",))
            k.wait(vector, "s", ("spcbsq",))
            for c in range(2):
                rs_cn = vector.reduce_sum(cnorm[:, c:c + 1],
                                          csq[:, c * 64:(c + 1) * 64], axis=AX.X)
            k.inc(rs_cn, "v", ("spcn",))
            k.wait(vector, "s", ("spcbn",))
            for c in range(2):
                vector.tensor_scalar(cnorm[:, c:c + 1], cnorm[:, c:c + 1],
                                     scalar1=1e-12, scalar2=None, op0=ALU.max)
            k.inc(vector.reciprocal(cnorm[:], cnorm[:]), "v", ("spcrec",))
            k.wait(vector, "v", ("spcrec",))
            for c in range(2):
                k.inc(vector.tensor_scalar(sncb[:, c * 64:(c + 1) * 64],
                                           scb[:, c * 64:(c + 1) * 64],
                                           scalar1=cnorm[:, c:c + 1],
                                           scalar2=None, op0=ALU.mult),
                      "v", ("spncb", c))
            for c in range(2):
                k.wait(vector, "t", ("spncbt", c))
                src = p_rstd if c == 0 else p_ms
                k.inc(vector.tensor_copy(ncbT[:, c * 128:(c + 1) * 128],
                                         src[0:64, 0:128]), "v", ("spncbtev", c))
            k.wait(vector, "t", ("spnvqt",))
            k.inc(vector.tensor_copy(nvqT[:], p_lie[0][0:64, 0:2]),
                  "v", ("spnvqte",))
            k.wait(vector, "t", ("spsc",))
            vector.tensor_copy(scoresR[:], p_rstd[0:2, 0:256])
            k.inc(vector.reduce_max(mxr[:], scoresR[:], axis=AX.X), "v", ("spmx",))
            k.wait(vector, "t", ("spmxt",))
            k.inc(vector.tensor_copy(mxT[:], p_ms[0:1, 0:2]), "v", ("spmxte",))
            k.wait(vector, "t", ("spsct", 1))
            for c in range(2):
                vector.tensor_copy(scT[:, c * 2:c * 2 + 2],
                                   p_lie[0][:, c * 2:c * 2 + 2])
            k.wait(vector, "t", ("spmb",))
            for c in range(2):
                k.inc(vector.tensor_tensor(soh[:, c * 2:c * 2 + 2],
                                           scT[:, c * 2:c * 2 + 2],
                                           p_rstd[:, 0:2], op=ALU.is_ge),
                      "v", ("spoh", c))
            k.wait(vector, "t", ("spvc",))
            k.inc(vector.tensor_copy(vqcT[:], p_ms[0:64, 0:2]), "v", ("spvce",))
            for h in range(4):
                k.wait(vector, "t", ("om", h, NE - 1))
                for q in range(2):
                    if (h, q) != (0, 0):
                        pq = (h, q - 1) if q == 1 else (h - 1, 1)
                        k.wait(vector, "dow0", ("omw",) + pq)
                    k.inc(vector.tensor_copy(outb[0][0:2, :], p_lie[q][0:2, :]),
                          "v", ("omev", h, q))
            for s_ in range(2):
                k.wait(vector, "drow", ("bvo", s_))
                k.wait(vector, "t", ("spvcr", s_))
                vector.tensor_tensor(bvoh[:], bvoh[:], p_lie[s_][0:2, :],
                                     op=ALU.add)
                vector.tensor_scalar(bvoh[:], bvoh[:], scalar1=smix[0:2, 1:2],
                                     scalar2=None, op0=ALU.mult)
                vector.tensor_scalar(booh[:], booh[:], scalar1=smix[0:2, 0:1],
                                     scalar2=None, op0=ALU.mult)
                k.inc(vector.tensor_tensor(booh[:], booh[:], bvoh[:],
                                           op=ALU.add), "v", ("spc2", s_))
                for e4 in range(4):
                    e = s_ * 4 + e4
                    k.wait(vector, "t", ("spct", e))
                    src = p_ms if e % 2 == 0 else p_rstd
                    k.inc(vector.tensor_copy(constPP[:, e * 2:e * 2 + 2],
                                             src[:, 0:2]), "v", ("spcpp", e))
            for b in range(NB):
                k.wait(vector, "dor", ("omr", b))
                k.inc(vector.tensor_tensor(omu[:, b * 64:(b + 1) * 64],
                                           omu[:, b * 64:(b + 1) * 64],
                                           sbtriu[:], op=ALU.add),
                      "v", ("spomu", b))
                k.wait(vector, "t", ("sptr", b))
                k.inc(vector.tensor_tensor(omg[:, b * 64:(b + 1) * 64],
                                           omu[:, b * 64:(b + 1) * 64],
                                           p_bc[0][0:64, 0:64], op=ALU.subtract),
                      "v", ("spomb", b))
                k.wait(vector, "t", ("spm2", b))
                k.inc(vector.tensor_copy(m2s[:, b * 64:(b + 1) * 64],
                                         p_bc[1][0:64, 0:64]), "v", ("spm2e", b))
                k.wait(vector, "v", ("spm2e", b))
                k.wait(vector, "t", ("spm3", b))
                k.inc(vector.tensor_copy(m3s[:, b * 64:(b + 1) * 64],
                                         p_bc[0][0:64, 0:64]), "v", ("spm3e", b))
                k.wait(vector, "v", ("spm3e", b))
                if b == 1:
                    k.wait(vector, "t", ("gbt", 0))
                bs = slice(b * 64, (b + 1) * 64)
                vector.tensor_scalar(gtmp[:], m2s[:, bs], scalar1=-0.5,
                                     scalar2=None, op0=ALU.mult)
                vector.tensor_tensor(gtmp[:], gtmp[:], omg[:, bs], op=ALU.add)
                vector.tensor_scalar(m3s[:, bs], m3s[:, bs], scalar1=1.0 / 6.0,
                                     scalar2=None, op0=ALU.mult)
                vector.tensor_tensor(gtmp[:], gtmp[:], m3s[:, bs], op=ALU.add)
                k.inc(vector.tensor_tensor(gtmp[:], gtmp[:],
                                           sident[0:64, 0:64], op=ALU.add),
                      "v", ("spG", b))
                k.wait(vector, "t", ("gbt", b))
                k.inc(vector.tensor_copy(GbT[:, bs], p_bc[1][0:64, 0:64]),
                      "v", ("gbte", b))
                k.wait(vector, "t", ("wliem", b))
                vector.tensor_copy(wlie[:, b * EMB: b * EMB + T],
                                   p_bc[0][0:64, :])
                k.inc(vector.tensor_copy(wlie[:, b * EMB + T: (b + 1) * EMB],
                                         p_bc[1][0:64, :]), "v", ("wlie", b))

            for t in range(NT):
                b = t // TPB
                for e in range(NE):
                    k.wait(vector, "t", ("lie", t, e))
                    if t == 0 and e == 0:
                        k.wait(vector, "v", ("spcpp", 7))
                    if t >= 1:
                        k.wait(vector, sg_sem(e), ("prod", t - 1, e))
                        k.wait(vector, "s", ("sq", t - 1, e))
                    k.inc(vector.scalar_tensor_tensor(
                        out=a_all[:, e * T:(e + 1) * T], in0=p_lie[e % 2][:],
                        scalar=constPP[:, e * 2 + b:e * 2 + b + 1],
                        in1=xnt[t % 2][:, e * T:(e + 1) * T],
                        op0=ALU.add, op1=ALU.add), "v", ("a", t, e))
                k.wait(vector, "s", ("sqrt", t))
                if t >= 1:
                    k.wait(vector, "t", ("rstdbc", t - 1))
                k.inc(vector.reciprocal(rstd_row[:], rms_row[:]),
                      "v", ("recip", t))
                for e2 in range(0, NE, 2):
                    k.wait(vector, "s", ("silu", t, e2))
                    k.inc(vector.tensor_tensor(gtb[:, e2 * T:(e2 + 1) * T],
                                               gtb[:, e2 * T:(e2 + 1) * T],
                                               a_all[:, e2 * T:(e2 + 1) * T],
                                               op=ALU.mult), "v", ("prod", t, e2))
                if t + 2 < NT:
                    t2 = t + 2
                    for e in range(0, NE, 2):
                        k.wait(vector, f"dxl{e % 2}", ("xln", t2, e))
                        k.wait(vector, "s", ("bcev", t2))
                        if e == 0 and t >= 1:
                            k.wait(vector, "do0", ("od", t - 1, 6))
                        vector.tensor_tensor(outb[0][:], xlnb[e % 2][:],
                                             mrB[t2 % 2][:], op=ALU.subtract)
                        k.inc(vector.tensor_tensor(
                            xnt[t2 % 2][:, e * T:(e + 1) * T], outb[0][:],
                            rsB[t2 % 2][:], op=ALU.mult), "v", ("xn", t2, e))
                for e2 in range(0, NE, 2):
                    k.wait(vector, "s", ("rstdev", t))
                    k.wait(vector, f"dxr{e2 % 2}", ("xres", t, e2))
                    if (t, e2) > (0, 1):
                        pe = e2 - 2
                        pt = t
                        if pe < 0:
                            pe += 8
                            pt -= 1
                        k.wait(vector, f"do{pe % 2}", ("od", pt, pe))
                    vector.tensor_tensor(outb[e2 % 2][:],
                                         gtb[:, e2 * T:(e2 + 1) * T],
                                         rstdB[:], op=ALU.mult)
                    k.inc(vector.tensor_tensor(outb[e2 % 2][:], outb[e2 % 2][:],
                                               xresb[e2 % 2][:], op=ALU.add),
                          "v", ("sg", t, e2))
        k.body(blk, "vector", vector_body)

        def scalar_body(scalar):
            k.wait(scalar, "dcs", ("call",))

            def stats_acc(b, i):
                u = ("xt", (b, i))
                k.wait(scalar, f"dq{USE_POS[u] % NPOOL}", u)
                if b == 1 and i == 0:
                    k.wait(scalar, "v", ("hmev", 0))
                scr = mposb[0][:] if b == 0 else xnt[1][:, 0:CH]
                scalar.activation(scr, pool[POOLBUF[u]][:], AF.Copy,
                                  scale=LN_INV, accum_out=mcols[:, i:i + 1])
                scalar.activation(scr, pool[POOLBUF[u]][:],
                                  AF.Square, scale=SQ_SCALE,
                                  accum_out=sqcols[:, i:i + 1])
                k.inc(scalar.square(varcols[:, i:i + 1], mcols[:, i:i + 1]),
                      "s", ("acc", b, i))

            def stats_sqv(b, i):
                k.wait(scalar, "v", ("var", b, i))
                k.inc(scalar.activation(varcols[:, i:i + 1], varcols[:, i:i + 1],
                                        AF.Sqrt, bias=scpool[:, 1:2]),
                      "s", ("sqv", b, i))

            for i in range(16):
                stats_acc(0, i)
                if i >= 1:
                    stats_sqv(0, i - 1)
            stats_sqv(0, 15)
            k.wait(scalar, "t", ("sttr", 0))
            scalar.copy(mrT[:], p_bc[0][0:16, 0:128])
            k.inc(scalar.copy(rsT[:], p_bc[1][0:16, 0:128]), "s", ("stev", 0))
            k.wait(scalar, "drow", ("woo",))
            k.inc(scalar.activation(woos[:], s64row[0:64, :], AF.Copy,
                                    scale=smix[0:64, 0:1]), "s", ("woos",))
            k.wait(scalar, "t", ("bc", 0))
            scalar.copy(mrB[0][:], p_bc[0][:])
            k.inc(scalar.copy(rsB[0][:], p_bc[1][:]), "s", ("bcev", 0))
            def wsigns(w, clist):
                for c in clist:
                    if w == 0:
                        stream, idx = "w1b", (c // NE, c % NE)
                    else:
                        stream, idx = "w2b", c
                    u = (stream, idx)
                    k.wait(scalar, f"dq{USE_POS[u] % NPOOL}", u)
                    if c == 0:
                        k.wait(scalar, "v", ("thrc",))
                    if c >= 2:
                        pk = ("w1t8", (c - 2) // NE, (c - 2) % NE) if w == 0 \
                            else ("w2t8", c - 2)
                        k.wait(scalar, "v", pk)
                    scalar.activation(mposb[c % 2][:], pool[POOLBUF[u]][:],
                                      AF.Sign, bias=nthrcol[:, w:w + 1])
                    k.inc(scalar.activation(mnegb[c % 2][:], pool[POOLBUF[u]][:],
                                            AF.Sign, bias=thrcol[:, w:w + 1]),
                          "s", ("sgn2", w, c))

            wsigns(0, list(range(16)))
            for f in range(NF):
                k.wait(scalar, "t", ("mm1", 0, f))
                k.inc(scalar.activation(hT[:, f * T:(f + 1) * T], p_mm[f % 2][:],
                                        AF.Gelu, bias=sb1[:, f:f + 1]),
                      "s", ("gelu", 0, f))
                if f < 16:
                    wsigns(0, [16 + f])
            k.wait(scalar, "t", ("bc", 1))
            k.wait(scalar, "v", ("xn", 0, 4))
            k.wait(scalar, "v", ("xn", 0, 5))
            scalar.copy(mrB[1][:], p_bc[0][:])
            k.inc(scalar.copy(rsB[1][:], p_bc[1][:]), "s", ("bcev", 1))
            wsigns(1, list(range(32)))
            for e2 in range(NE):
                for i in (2 * e2, 2 * e2 + 1):
                    stats_acc(1, i)
                    if i >= 1:
                        stats_sqv(1, i - 1)
                k.wait(scalar, "t", ("mm2", 0, e2))
                scalar.activation(gtb[:, e2 * T:(e2 + 1) * T],
                                  p_mm[e2 % 2][:], AF.Silu,
                                  bias=sb2[:, e2:e2 + 1])
                k.inc(scalar.activation(gtb[:, e2 * T:(e2 + 1) * T],
                                        gtb[:, e2 * T:(e2 + 1) * T],
                                        AF.Copy, scale=snw[:, e2:e2 + 1]),
                      "s", ("silu", 0, e2))
            stats_sqv(1, 15)
            k.wait(scalar, "t", ("sttr", 1))
            scalar.copy(mrT[:], p_ms[0:16, 0:128])
            k.inc(scalar.copy(rsT[:], p_rstd[0:16, 0:128]), "s", ("stev", 1))
            k.wait(scalar, "v", ("spvqi",))
            k.inc(scalar.square(vqq[:], vqi[:]), "s", ("spvqsq",))
            k.inc(scalar.square(csq[:], scb[:]), "s", ("spcbsq",))
            k.wait(scalar, "v", ("spvnr",))
            k.inc(scalar.activation(vnorm[:], vnorm[:], AF.Sqrt,
                                    bias=scpool[0:2, 2:3]), "s", ("spvqn",))
            k.wait(scalar, "v", ("spcn",))
            k.inc(scalar.activation(cnorm[:], cnorm[:], AF.Sqrt,
                                    bias=scpool[:, 2:3]), "s", ("spcbn",))

            def sq_t(t):
                for e in range(NE):
                    k.wait(scalar, "v", ("a", t, e))
                    if e >= 1:
                        k.wait(scalar, "t", ("ms", t, e - 1))
                    elif t > 0:
                        k.wait(scalar, "t", ("ms", t - 1, 7))
                    k.inc(scalar.square(sqb[0][:],
                                        a_all[:, e * T:(e + 1) * T]),
                          "s", ("sq", t, e))

            def sqrt_t(t):
                k.wait(scalar, "t", ("ms", t, NE - 1))
                if t > 0:
                    k.wait(scalar, "v", ("recip", t - 1))
                k.inc(scalar.activation(rms_row[:], p_ms[0:1, :], AF.Sqrt,
                                        bias=scpool[0:1, 0:1], scale=LN_INV),
                      "s", ("sqrt", t))

            def rstdev_t(t):
                k.wait(scalar, "t", ("rstdbc", t))
                if t >= 1:
                    k.wait(scalar, "v", ("sg", t - 1, 6))
                    k.wait(scalar, "p", ("sg", t - 1, 7))
                k.inc(scalar.copy(rstdB[:], p_rstd[:]), "s", ("rstdev", t))

            def silu_t(t):
                for e2 in range(NE):
                    k.wait(scalar, "t", ("mm2", t, e2))
                    k.wait(scalar, sg_sem(e2), ("sg", t - 1, e2))
                    scalar.activation(gtb[:, e2 * T:(e2 + 1) * T],
                                      p_mm[e2 % 2][:], AF.Silu,
                                      bias=sb2[:, e2:e2 + 1])
                    k.inc(scalar.activation(gtb[:, e2 * T:(e2 + 1) * T],
                                            gtb[:, e2 * T:(e2 + 1) * T],
                                            AF.Copy, scale=snw[:, e2:e2 + 1]),
                          "s", ("silu", t, e2))

            def gelu_t(t):
                for f in range(NF):
                    k.wait(scalar, "t", ("mm1", t, f))
                    if f == 0:
                        k.wait(scalar, "t", ("mm2", t - 1, 7))
                    k.inc(scalar.activation(hT[:, f * T:(f + 1) * T],
                                            p_mm[f % 2][:], AF.Gelu,
                                            bias=sb1[:, f:f + 1]),
                          "s", ("gelu", t, f))

            def bcev_t(t):
                k.wait(scalar, "t", ("bc", t))
                if t >= 2:
                    k.wait(scalar, xn_sem(t - 2, 6), ("xn", t - 2, 6))
                    k.wait(scalar, xn_sem(t - 2, 7), ("xn", t - 2, 7))
                scalar.copy(mrB[t % 2][:], p_bc[0][:])
                k.inc(scalar.copy(rsB[t % 2][:], p_bc[1][:]), "s", ("bcev", t))

            gelu_t(1)
            bcev_t(2)
            sq_t(0)
            sqrt_t(0)
            rstdev_t(0)
            silu_t(1)
            sq_t(1)
            sqrt_t(1)
            rstdev_t(1)
            for t in range(2, NT):
                if t + 1 < NT:
                    bcev_t(t + 1)
                gelu_t(t)
                sq_t(t)
                sqrt_t(t)
                rstdev_t(t)
                silu_t(t)
        k.body(blk, "scalar", scalar_body)

        def tensor_body(tensor):
            k.wait(tensor, "dcs", ("call",))
            k.wait(tensor, "v", ("thrc0", 0))
            k.inc(tensor.transpose(p_bc[0][0:1, 0:128], thrcomb[:, 0:1],
                                   sident[:]), "t", ("thrtr",))
            k.wait(tensor, "v", ("thrc0", 1))
            k.inc(tensor.transpose(p_bc[1][0:1, 0:128], thrcomb[:, 1:2],
                                   sident[:]), "t", ("thrtr2",))

            def hm_i(b, i):
                hb = [p_mm[0], p_mm[1]] if b == 0 else [p_lie[0], p_lie[1]]
                u = ("xt", (b, i))
                k.wait(tensor, f"dq{USE_POS[u] % NPOOL}", u)
                k.wait(tensor, "v", ("rsc", b, i))
                for q in range(2):
                    mm = tensor.matmul(hb[q][0:1, :],
                                       lhsT=rscols[:, i:i + 1],
                                       rhs=pool[POOLBUF[u]][:, q * T:(q + 1) * T],
                                       start=(i == 0), stop=(i == 15))
                k.inc(mm, "t", ("hm", b, i))

            def smr_b(b):
                k.wait(tensor, "v", ("mrsum", b))
                tensor.matmul(p_ms[0:1, 20:21], lhsT=sonescf[:],
                              rhs=thrpart[:, 0:1], start=True, stop=True)
                mm = tensor.matmul(p_ms[0:1, 22:23], lhsT=sonescf[:],
                                   rhs=thrpart[:, 0:1], start=True, stop=True)
                k.inc(mm, "t", ("smr", b))

            def stats_tr(b):
                k.wait(tensor, "v", ("rsc", b, 15))
                tr = [p_bc[0], p_bc[1]] if b == 0 else [p_ms, p_rstd]
                if b == 0:
                    k.wait(tensor, "v", ("ts0",))
                    k.wait(tensor, "v", ("thrsums",))
                else:
                    k.wait(tensor, "v", ("hmvv", 0))
                    k.wait(tensor, "v", ("smre", 1))
                k.inc(tensor.transpose(tr[0][0:16, 0:128], mrcols[:], sident[:]),
                      "t", ("sttr0", b))
                k.inc(tensor.transpose(tr[1][0:16, 0:128], rscols[:], sident[:]),
                      "t", ("sttr", b))

            def stats_b(b):
                for i in range(16):
                    hm_i(b, i)
                smr_b(b)
                stats_tr(b)

            def hmv_b(b):
                k.wait(tensor, "v", ("hmev", b))
                if b == 1:
                    k.wait(tensor, "s", ("stev", 1))
                for e in range(NE):
                    hsrc = rms_row if e < 4 else rstd_row
                    mm = tensor.matmul(p_ms[:, (2 * e + b):(2 * e + b + 1)],
                                       lhsT=hsrc[0:1, (e % 4) * 128:(e % 4 + 1) * 128],
                                       rhs=sonesr[0:1, 0:1], start=True,
                                       stop=True)
                k.inc(mm, "t", ("hmv", b))

            stats_b(0)
            hmv_b(0)
            k.wait(tensor, "v", ("thrrow",))
            k.wait(tensor, "s", ("stev", 0))
            k.inc(tensor.matmul(p_bc[0][:, 0:2], lhsT=sonesr[:], rhs=thrrow[:],
                                start=True, stop=True), "t", ("thrbc",))

            def bc_t(t):
                k.wait(tensor, "dst0", ("stm", t))
                k.wait(tensor, "dst1", ("str", t))
                if t == 0:
                    k.wait(tensor, "v", ("thrc",))
                if t == 2:
                    k.wait(tensor, "v", ("wlie", 1))
                if t > 0:
                    k.wait(tensor, "s", ("bcev", t - 1))
                tensor.matmul(p_bc[0][:], lhsT=sonesr[:], rhs=mrbuf[0][:],
                              start=True, stop=True)
                tensor.matmul(p_bc[1][:], lhsT=sonesr[:], rhs=rsbuf[0][:],
                              start=True, stop=True)
                tensor.matmul(p_bc[0][:], lhsT=sonesr[:], rhs=mrbuf[0][:],
                              start=True, stop=True)
                mm = tensor.matmul(p_bc[1][:], lhsT=sonesr[:], rhs=rsbuf[0][:],
                                   start=True, stop=True)
                k.inc(mm, "t", ("bc", t))

            def vq_path():
                k.wait(tensor, "v", ("hmvv", 1))
                k.wait(tensor, "drow", ("wvi",))
                for e in range(NE):
                    mm = tensor.matmul(p_ms[0:2, 0:64],
                                       lhsT=hmv[:, 2 * e:2 * e + 2],
                                       rhs=s64row[:, e * 64:(e + 1) * 64],
                                       start=(e == 0), stop=(e == 7))
                k.inc(mm, "t", ("spvq",))
                for c in range(2):
                    k.wait(tensor, "v", ("spncb", c))
                    dst = p_rstd if c == 0 else p_ms
                    if c == 1:
                        k.wait(tensor, "v", ("spncbtev", 0))
                    k.inc(tensor.transpose(dst[0:64, 0:128],
                                           sncb[:, c * 64:(c + 1) * 64],
                                           sident[:]), "t", ("spncbt", c))
                k.wait(tensor, "v", ("spnvq",))
                k.wait(tensor, "v", ("hmev", 1))
                k.inc(tensor.transpose(p_lie[0][0:64, 0:2], nvq[:],
                                       sident[0:2, 0:2]), "t", ("spnvqt",))
                k.wait(tensor, "v", ("spncbtev", 1))
                k.wait(tensor, "v", ("spnvqte",))
                k.inc(tensor.matmul(p_rstd[0:2, 0:256], lhsT=nvqT[:], rhs=ncbT[:],
                                    start=True, stop=True), "t", ("spsc",))
                k.wait(tensor, "v", ("spmx",))
                k.inc(tensor.transpose(p_ms[0:1, 0:2], mxr[:],
                                       sident[0:2, 0:2]), "t", ("spmxt",))
                k.wait(tensor, "v", ("spmxte",))
                k.inc(tensor.matmul(p_rstd[:, 0:2], lhsT=sonesr[:], rhs=mxT[:],
                                    start=True, stop=True), "t", ("spmb",))
                for c in range(2):
                    k.inc(tensor.transpose(p_lie[0][:, c * 2:c * 2 + 2],
                                           scoresR[:, c * 128:(c + 1) * 128],
                                           sident[0:2, 0:2]), "t", ("spsct", c))
                k.wait(tensor, "v", ("spoh", 1))
                for c in range(2):
                    mm = tensor.matmul(p_ms[0:64, 0:2],
                                       lhsT=scb[:, c * 64:(c + 1) * 64],
                                       rhs=soh[:, c * 2:c * 2 + 2],
                                       start=(c == 0), stop=(c == 1))
                k.inc(mm, "t", ("spvc",))

            def omega_h(h):
                for e in range(NE):
                    k.wait(tensor, f"dq{USE_POS[('wtr', (h, e))] % NPOOL}",
                           ("wtr", (h, e)))
                    if e == 0:
                        k.wait(tensor, "v", ("hmvv", 1))
                        k.wait(tensor, "v", ("hmev", 1))
                        if h == 0:
                            k.wait(tensor, "v", ("spoh", 0))
                            k.wait(tensor, "v", ("spnvqte",))
                        else:
                            k.wait(tensor, "v", ("omev", h - 1, 1))
                    buf = pool[POOLBUF[("wtr", (h, e))]]
                    for q in range(2):
                        mm = tensor.matmul(p_lie[q][0:2, :],
                                           lhsT=hmv[:, 2 * e:2 * e + 2],
                                           rhs=buf[:, q * T:(q + 1) * T],
                                           start=(e == 0), stop=(e == 7))
                    k.inc(mm, "t", ("om", h, e))

            def vq_rows():
                k.wait(tensor, "v", ("spvce",))
                k.wait(tensor, "drow", ("wvo",))
                k.wait(tensor, "v", ("omev", 3, 1))
                for s_ in range(2):
                    k.inc(tensor.matmul(p_lie[s_][0:2, :], lhsT=vqcT[:],
                                        rhs=s64row[0:64, s_ * T:(s_ + 1) * T],
                                        start=True, stop=True),
                          "t", ("spvcr", s_))
                for s_ in range(2):
                    k.wait(tensor, "v", ("spc2", s_))
                    for e4 in range(4):
                        e = s_ * 4 + e4
                        dst = p_ms if e % 2 == 0 else p_rstd
                        if e >= 2:
                            k.wait(tensor, "v", ("spcpp", e - 2))
                        k.inc(tensor.transpose(dst[:, 0:2],
                                               booh[:, e4 * 128:(e4 + 1) * 128],
                                               sident[0:2, 0:2]),
                              "t", ("spct", e))

            def g_pe(b):
                k.wait(tensor, "v", ("spomu", b))
                k.wait(tensor, "s", ("bcev", 1))
                k.inc(tensor.transpose(p_bc[0][0:64, 0:64],
                                       omu[:, b * 64:(b + 1) * 64],
                                       sident[0:64, 0:64]), "t", ("sptr", b))
                k.wait(tensor, "v", ("spomb", b))
                k.inc(tensor.matmul(p_bc[1][0:64, 0:64],
                                    lhsT=omg[:, b * 64:(b + 1) * 64],
                                    rhs=omg[:, b * 64:(b + 1) * 64],
                                    start=True, stop=True), "t", ("spm2", b))
                k.wait(tensor, "v", ("spm2e", b))
                k.inc(tensor.matmul(p_bc[0][0:64, 0:64],
                                    lhsT=omg[:, b * 64:(b + 1) * 64],
                                    rhs=m2s[:, b * 64:(b + 1) * 64],
                                    start=True, stop=True), "t", ("spm3", b))
                k.wait(tensor, "v", ("spG", b))
                k.inc(tensor.transpose(p_bc[1][0:64, 0:64], gtmp[:],
                                       sident[0:64, 0:64]), "t", ("gbt", b))
                k.wait(tensor, "v", ("gbte", b))
                k.wait(tensor, "s", ("woos",))
                for q in range(2):
                    mm = tensor.matmul(p_bc[q][0:64, :],
                                       lhsT=GbT[:, b * 64:(b + 1) * 64],
                                       rhs=woos[:, q * T:(q + 1) * T],
                                       start=True, stop=True)
                k.inc(mm, "t", ("wliem", b))

            def lie_t(t):
                b = t // TPB
                for e in range(NE):
                    if e == 0:
                        k.wait(tensor, "v", ("wlie", b))
                        k.wait(tensor, xn_sem(t, 0), ("xn", t, 0))
                        if t == 0:
                            k.wait(tensor, "v", ("spcpp", 7))
                        if t > 0:
                            k.wait(tensor, "v", ("a", t - 1, 6))
                    elif e == 1 and t > 0:
                        k.wait(tensor, "v", ("a", t - 1, 7))
                    elif e >= 2:
                        k.wait(tensor, "v", ("a", t, e - 2))
                    k.inc(tensor.matmul(
                        p_lie[e % 2][:],
                        lhsT=wlie[:, b * EMB + e * 128: b * EMB + (e + 1) * 128],
                        rhs=xnt[t % 2][0:64, 0:T], start=True, stop=True),
                          "t", ("lie", t, e))

            def ms_t(t):
                for e in range(NE):
                    k.wait(tensor, "s", ("sq", t, e))
                    if e == 0:
                        if t > 0:
                            k.wait(tensor, "s", ("sqrt", t - 1))
                        else:
                            k.wait(tensor, "v", ("spcpp", 6))
                    k.inc(tensor.matmul(p_ms[0:1, 0:T], lhsT=sonescb[:],
                                        rhs=sqb[0][:],
                                        start=(e == 0), stop=(e == NE - 1)),
                          "t", ("ms", t, e))

            def rstd_t(t):
                k.wait(tensor, "v", ("recip", t))
                if t > 0:
                    k.wait(tensor, "s", ("rstdev", t - 1))
                else:
                    k.wait(tensor, "v", ("spcpp", 7))
                k.inc(tensor.matmul(p_rstd[:], lhsT=sonesr[:], rhs=rstd_row[:],
                                    start=True, stop=True), "t", ("rstdbc", t))

            AUX = {}

            def add_aux(key, *fns):
                AUX.setdefault(key, []).extend(fns)

            def run_aux(key):
                for fn in AUX.get(key, []):
                    fn()

            for g_ in range(6):
                def mk(g=g_):
                    def f():
                        hm_i(1, 2 * g)
                        hm_i(1, 2 * g + 1)
                    return f
                add_aux(("e2", 0, g_ + 2), mk())
            def hm_tail():
                for i in range(12, 16):
                    hm_i(1, i)
            add_aux(("f", 1, 0), hm_tail, lambda: smr_b(1),
                    lambda: stats_tr(1))
            add_aux(("f", 1, 1), lambda: hmv_b(1))
            add_aux(("f", 1, 2), vq_path)
            add_aux(("f", 1, 4), lambda: omega_h(0))
            add_aux(("f", 1, 8), lambda: omega_h(1))
            add_aux(("f", 1, 12), lambda: omega_h(2))
            add_aux(("f", 1, 16), lambda: omega_h(3))
            add_aux(("f", 1, 20), vq_rows)
            add_aux(("f", 1, 24), lambda: g_pe(0))
            add_aux(("f", 1, 28), lambda: g_pe(1))
            add_aux(("e2", 1, 0), lambda: lie_t(0))
            add_aux(("e2", 1, 2), lambda: ms_t(0), lambda: rstd_t(0))
            add_aux(("e2", 1, 4), lambda: lie_t(1))
            add_aux(("f", 2, 0), lambda: ms_t(1), lambda: rstd_t(1))

            bc_t(0)
            for t in range(NT):
                if t >= 2 and t + 1 < NT:
                    bc_t(t + 1)
                for f in range(NF):
                    run_aux(("f", t, f))
                    if t == 0 and f % 8 == 0:
                        k.wait(tensor, "v", ("w1t8", f // 8, NE - 1))
                    if f == 0:
                        for e in range(NE):
                            k.wait(tensor, xn_sem(t, e), ("xn", t, e))
                        if t == 0:
                            k.wait(tensor, "v", ("hmev", 0))
                        if t > 0:
                            k.wait(tensor, "s", ("silu", t - 1, 6))
                    elif f == 1:
                        if t == 0:
                            k.wait(tensor, "v", ("hmev", 0))
                        if t > 0:
                            k.wait(tensor, "s", ("silu", t - 1, 7))
                    else:
                        k.wait(tensor, "s", ("gelu", t, f - 2))
                    for e in range(NE):
                        mm = tensor.matmul(
                            p_mm[f % 2][:],
                            lhsT=w1t8[:, e * FF + f * 128: e * FF + (f + 1) * 128],
                            rhs=xnt[t % 2][:, e * T:(e + 1) * T],
                            start=(e == 0), stop=(e == NE - 1))
                    k.inc(mm, "t", ("mm1", t, f))
                run_aux(("f", t, NF))
                if t + 1 < NT and t < 2:
                    bc_t(t + 1)
                if t >= 2:
                    lie_t(t)

                def mm2_group(e2):
                    run_aux(("e2", t, e2))
                    if e2 == 0:
                        k.wait(tensor, "s", ("gelu", t, NF - 1))
                        if t == 0:
                            k.wait(tensor, "v", ("w2t8", NF - 1))
                    elif e2 >= 2:
                        k.wait(tensor, "s", ("silu", t, e2 - 2))
                    for f in range(NF):
                        mm = tensor.matmul(
                            p_mm[e2 % 2][:],
                            lhsT=w2t8[:, f * EMB + e2 * 128: f * EMB + (e2 + 1) * 128],
                            rhs=hT[:, f * T:(f + 1) * T],
                            start=(f == 0), stop=(f == NF - 1))
                    k.inc(mm, "t", ("mm2", t, e2))

                if t >= 2:
                    mm2_group(0)
                    mm2_group(1)
                    ms_t(t)
                    rstd_t(t)
                    for e2 in range(2, NE):
                        mm2_group(e2)
                else:
                    for e2 in range(NE):
                        mm2_group(e2)
        k.body(blk, "tensor", tensor_body)


def build_program():
    nc = bass.Bass("TRN2", target_bir_lowering=False, debug=False)
    io = {}
    io["xTd"] = nc.dram_tensor("xTd", [NB * EMB, SEQ], F32, kind="ExternalInput").ap()
    io["xTokd"] = nc.dram_tensor("xTokd", [NB * SEQ, EMB], F32, kind="ExternalInput").ap()
    io["w1td"] = nc.dram_tensor("w1td", [EMB, FF], F32, kind="ExternalInput").ap()
    io["w2td"] = nc.dram_tensor("w2td", [FF, EMB], F32, kind="ExternalInput").ap()
    io["w1pred"] = nc.dram_tensor("w1pred", [128, FF], F32, kind="ExternalInput").ap()
    io["w2pred"] = nc.dram_tensor("w2pred", [512, EMB], F32, kind="ExternalInput").ap()
    io["wtriud"] = nc.dram_tensor("wtriud", [EMB, FF], F32, kind="ExternalInput").ap()
    io["btriud"] = nc.dram_tensor("btriud", [64, 64], F32, kind="ExternalInput").ap()
    io["wooTd"] = nc.dram_tensor("wooTd", [64, EMB], F32, kind="ExternalInput").ap()
    io["wviTd"] = nc.dram_tensor("wviTd", [128, 512], F32, kind="ExternalInput").ap()
    io["wvoTd"] = nc.dram_tensor("wvoTd", [64, EMB], F32, kind="ExternalInput").ap()
    io["cbd"] = nc.dram_tensor("cbd", [128, 128], F32, kind="ExternalInput").ap()
    io["b1d"] = nc.dram_tensor("b1d", [128, 32], F32, kind="ExternalInput").ap()
    io["b2d"] = nc.dram_tensor("b2d", [128, 8], F32, kind="ExternalInput").ap()
    io["nwd"] = nc.dram_tensor("nwd", [128, 8], F32, kind="ExternalInput").ap()
    io["bvi2d"] = nc.dram_tensor("bvi2d", [2, 64], F32, kind="ExternalInput").ap()
    io["bvo2d"] = nc.dram_tensor("bvo2d", [2, EMB], F32, kind="ExternalInput").ap()
    io["boo2d"] = nc.dram_tensor("boo2d", [2, EMB], F32, kind="ExternalInput").ap()
    io["mixd"] = nc.dram_tensor("mixd", [128, 2], F32, kind="ExternalInput").ap()
    io["identd"] = nc.dram_tensor("identd", [128, 128], F32, kind="ExternalInput").ap()
    io["onesrd"] = nc.dram_tensor("onesrd", [1, 128], F32, kind="ExternalInput").ap()
    io["onescbd"] = nc.dram_tensor("onescbd", [128, 1], BF16, kind="ExternalInput").ap()
    io["onescfd"] = nc.dram_tensor("onescfd", [128, 1], F32, kind="ExternalInput").ap()
    io["cpoold"] = nc.dram_tensor("cpoold", [128, 4], F32, kind="ExternalInput").ap()
    io["omdram"] = nc.dram_tensor("omdram", [NB, 4096], F32, kind="Internal").ap()
    io["stats_d"] = nc.dram_tensor("stats_d", [4, SEQ], F32, kind="Internal").ap()
    io["ccin"] = nc.dram_tensor("ccin", [1, 2], F32, kind="Internal").ap()
    io["ccout"] = nc.dram_tensor("ccout", [1, 2], F32, kind="Internal").ap()
    io["outTd"] = nc.dram_tensor("outTd", [NB * EMB, SEQ], F32, kind="ExternalOutput").ap()

    k1 = K(False)
    construct(k1, {n: Fake() for n in io})
    k2 = K(True, nc, M=k1.M)
    with k2.es:
        construct(k2, io)
    return nc


def kernel(L0_attn_x, L0_attn_omega_proj_w, L0_attn_omega_proj_b, L0_attn_omega_out_w,
           L0_attn_omega_out_b, L0_attn_vq_proj_in_w, L0_attn_vq_proj_in_b,
           L0_attn_vq_proj_out_w, L0_attn_vq_proj_out_b, L0_attn_codebook,
           L0_attn_omega_mix, L0_attn_vq_mix, L0_attn_norm_w, L0_ffn_W1, L0_ffn_b1,
           L0_ffn_W2, L0_ffn_b2):
    f32 = np.float32
    x = np.asarray(L0_attn_x, f32)
    xT = np.ascontiguousarray(x.transpose(0, 2, 1))
    w1t = np.ascontiguousarray(np.asarray(L0_ffn_W1, f32).T)
    w2t = np.ascontiguousarray(np.asarray(L0_ffn_W2, f32).T)

    opw = np.asarray(L0_attn_omega_proj_w, f32)
    opb = np.asarray(L0_attn_omega_proj_b, f32)
    R, C = np.triu_indices(64, k=1)
    wtriu = np.zeros((EMB, 64, 64), f32)
    wtriu[:, R, C] = opw.T
    wtriu = np.ascontiguousarray(wtriu.reshape(EMB, FF))
    btriu = np.zeros((64, 64), f32)
    btriu[R, C] = opb

    wooT = np.ascontiguousarray(np.asarray(L0_attn_omega_out_w, f32).T)
    wviT = np.asarray(L0_attn_vq_proj_in_w, f32).T
    wviT = np.ascontiguousarray(wviT.reshape(8, 128, 64).transpose(1, 0, 2).reshape(128, 512))
    wvoT = np.ascontiguousarray(np.asarray(L0_attn_vq_proj_out_w, f32).T)
    cb = np.asarray(L0_attn_codebook, f32)
    cbPP = np.ascontiguousarray(cb.reshape(2, 128, 64).transpose(1, 0, 2).reshape(128, 128))
    b1PP = np.ascontiguousarray(np.asarray(L0_ffn_b1, f32).reshape(32, 128).T)
    b2PP = np.ascontiguousarray(np.asarray(L0_ffn_b2, f32).reshape(8, 128).T)
    nwPP = np.ascontiguousarray(np.asarray(L0_attn_norm_w, f32).reshape(8, 128).T)
    bvi2 = np.ascontiguousarray(np.broadcast_to(np.asarray(L0_attn_vq_proj_in_b, f32), (2, 64)))
    bvo2 = np.ascontiguousarray(np.broadcast_to(np.asarray(L0_attn_vq_proj_out_b, f32), (2, EMB)))
    boo2 = np.ascontiguousarray(np.broadcast_to(np.asarray(L0_attn_omega_out_b, f32), (2, EMB)))
    mixd = np.empty((128, 2), f32)
    mixd[:, 0] = np.float32(L0_attn_omega_mix)
    mixd[:, 1] = np.float32(L0_attn_vq_mix)
    identd = np.eye(128, dtype=f32)
    onesrd = np.ones((1, 128), f32)
    onescbd = np.ones((128, 1), ml_dtypes.bfloat16)
    onescfd = np.ones((128, 1), np.float32)
    cpoold = np.broadcast_to(np.array([1e-6, 1e-5, 0.0, 0.0], f32), (128, 4)).copy()

    shared = {
        "w1td": w1t, "w2td": w2t, "wtriud": wtriu, "btriud": btriu, "wooTd": wooT,
        "wviTd": wviT, "wvoTd": wvoT, "cbd": cbPP, "b1d": b1PP, "b2d": b2PP,
        "nwd": nwPP, "bvi2d": bvi2, "bvo2d": bvo2, "boo2d": boo2, "mixd": mixd,
        "identd": identd, "onesrd": onesrd, "onescbd": onescbd,
        "onescfd": onescfd, "cpoold": cpoold,
    }
    in_maps = []
    for c in range(NCORES):
        m = dict(shared)
        m["xTd"] = np.ascontiguousarray(xT[2 * c:2 * c + 2].reshape(NB * EMB, SEQ))
        m["xTokd"] = np.ascontiguousarray(x[2 * c:2 * c + 2].reshape(NB * SEQ, EMB))
        m["w1pred"] = np.ascontiguousarray(w1t[c * 128:(c + 1) * 128, :])
        m["w2pred"] = np.ascontiguousarray(w2t[c * 512:(c + 1) * 512, :])
        in_maps.append(m)

    nc = build_program()
    res = run_bass_kernel_spmd(nc, in_maps, core_ids=list(range(NCORES)))
    outs = []
    for c in range(NCORES):
        o = np.asarray(res.results[c]["outTd"], f32).reshape(NB, EMB, SEQ)
        outs.append(o.transpose(0, 2, 1))
    return np.ascontiguousarray(np.concatenate(outs, axis=0))
